# version 16
# baseline (speedup 1.0000x reference)
import numpy as np
import ml_dtypes
import concourse.bass as bass
import concourse.mybir as mybir
from concourse.bass_utils import run_bass_kernel_spmd
from contextlib import ExitStack

F32 = mybir.dt.float32
BF16 = mybir.dt.bfloat16
AF = mybir.ActivationFunctionType
ALU = mybir.AluOpType

D = 1024
H = 16
DH = 64
WIN = 7
NPASS = 4
PT = 1024
PTH = PT + 6
CORE_T = NPASS * PT
ACH = 128
NACH = PT // ACH
E = 8
FF = 4096
FH = 2048
TH = 512
NFB = 16
XROWS = CORE_T + 6 + 122
CAP = 352

ENGS = ("sync", "gpsimd", "tensor", "scalar", "vector")


def build_program():
    nc = bass.Bass("TRN2", target_bir_lowering=False, debug=False)

    def din(name, shape, dt=F32):
        return nc.dram_tensor(name, shape, dt, kind="ExternalInput").ap()

    x = din("x", [XROWS, D])
    w1T = din("w1T", [E, D, FF])
    w2T = din("w2T", [E, FF, D])
    wqT = din("wqT", [D, D])
    wkT = din("wkT", [D, D])
    wvT = din("wvT", [D, D])
    woT = din("woT", [D, D])
    rwT = din("rwT", [D, E])
    b1T = din("b1T", [E, 128, 32])
    b2i = din("b2i", [E, D])
    boi = din("boi", [1, D])
    cw1t = din("cw1t", [112, 32])
    cb1t = din("cb1t", [112, 32])
    cw2f = din("cw2f", [112, 1024])
    cb2t = din("cb2t", [112, 32])
    cw3s = din("cw3s", [112, 32])
    cb3s = din("cb3s", [112, 1])
    post = din("post", [112, 1])
    D1 = din("D1", [WIN, 128, 128])
    D2 = din("D2", [WIN, 6, 128])
    identb = din("identb", [128, 128], BF16)
    identf = din("identf", [128, 128])
    onescol_b = din("onescol_b", [128, 1], BF16)
    onesrow_f = din("onesrow_f", [1, 128])
    iotaM_i = din("iotaM_i", [128, CAP])
    ltri_i = din("ltri_i", [128, 128], BF16)
    sel9_i = din("sel9_i", [9, 9 * 128], BF16)
    ones128_i = din("ones128_i", [128, 128], BF16)
    ones_b_row_src = onesrow_f

    out_t = nc.dram_tensor("out", [CORE_T, D], F32, kind="ExternalOutput").ap()
    cwt_d = nc.dram_tensor("cwt_d", [112, 1], F32, kind="Internal").ap()
    w1d = nc.dram_tensor("w1d", [16, 128, 16384], BF16, kind="Internal").ap()
    w2d = nc.dram_tensor("w2d", [16, 128, 16384], BF16, kind="Internal").ap()
    attd = nc.dram_tensor("attd", [4, 128, 8192], BF16, kind="Internal").ap()

    ops = {e: [] for e in ENGS}
    cnt = {s: 0 for s in ("s_xc", "s_xs0", "s_xs1", "s_xs2", "s_xs3", "s_xa", "s_xb", "s_out",
                          "s_wc", "s_wa0", "s_wa1", "s_wa2", "s_wt", "s_wt2", "s_wv",
                          "s_wd0", "s_wd1", "s_wd2",
                          "s_te", "s_se", "s_ve")}
    hiw = {e: {s: 0 for s in cnt} for e in ENGS}

    def wt(eng, sem, val):
        if val is None or val <= 0 or val <= hiw[eng][sem]:
            return
        hiw[eng][sem] = val
        def _w(e, S, _s=sem, _v=val):
            e.wait_ge(S[_s], _v)
        ops[eng].append(_w)

    def op(eng, fn, inc_sem=None, inc_by=1):
        if inc_sem is not None:
            cnt[inc_sem] += inc_by
        def _f(e, S, _fn=fn, _i=inc_sem, _b=inc_by):
            ins = _fn(e, S)
            if _i is not None:
                ins.then_inc(S[_i], _b)
        ops[eng].append(_f)
        return cnt[inc_sem] if inc_sem else None

    def ve(fn):
        wt("vector", "s_ve", cnt["s_ve"])
        return op("vector", fn, "s_ve")

    def se(fn):
        wt("scalar", "s_se", cnt["s_se"])
        return op("scalar", fn, "s_se")

    def te(fn, inc=True):
        return op("tensor", fn, "s_te" if inc else None)

    bank_free = {b: {"s_se": 0, "s_ve": 0} for b in range(8)}

    def bank_guard(b):
        wt("tensor", "s_se", bank_free[b]["s_se"])
        wt("tensor", "s_ve", bank_free[b]["s_ve"])

    def bank_read(b, sem):
        bank_free[b][sem] = cnt[sem]

    with ExitStack() as ctx:
        En = ctx.enter_context
        acc = En(nc.sbuf_tensor("acc", [128, 8 * D], F32))
        slot0 = En(nc.sbuf_tensor("slot0", [128, 32768], BF16))
        slot1 = En(nc.sbuf_tensor("slot1", [128, 16384], BF16))
        scr = En(nc.sbuf_tensor("scr", [128, 16480], BF16))
        b1s = En(nc.sbuf_tensor("b1s", [128, E * 32], F32))
        b2bo = En(nc.sbuf_tensor("b2bo", [9, D], BF16))
        sel9 = En(nc.sbuf_tensor("sel9", [9, 9 * 128], BF16))
        rwb = En(nc.sbuf_tensor("rwb", [128, 8 * E], BF16))
        idb = En(nc.sbuf_tensor("idb", [128, 128], BF16))
        ones_c = En(nc.sbuf_tensor("ones_c", [128, 1], BF16))
        cMall = En(nc.sbuf_tensor("cMall", [128, H * 256], BF16))
        c_row_t = En(nc.sbuf_tensor("c_row_t", [1, H * WIN], F32))
        Gs = En(nc.sbuf_tensor("Gs", [128, 8 * E], F32))
        m8 = En(nc.sbuf_tensor("m8", [128, 64], F32))
        poss = m8
        g2c = En(nc.sbuf_tensor("g2c", [128, 8], F32))
        g1c = En(nc.sbuf_tensor("g1c", [128, 8], F32))
        gtmp = En(nc.sbuf_tensor("gtmp", [128, E], F32))
        bnst = En(nc.sbuf_tensor("bnst", [128, 36], F32))
        bnagg = En(nc.sbuf_tensor("bnagg", [128, 6], F32))
        rstd = En(nc.sbuf_tensor("rstd", [128, 3], F32))
        nmr = En(nc.sbuf_tensor("nmr", [128, 3], F32))
        lnt = En(nc.sbuf_tensor("lnt", [128, 3], F32))
        rden = En(nc.sbuf_tensor("rden", [128, 1], F32))
        xs_t = [En(nc.sbuf_tensor(f"xs{i}", [128, D], F32)) for i in range(3)]
        xnb_t = [En(nc.sbuf_tensor(f"xnb{i}", [128, D], BF16)) for i in range(8)]
        xnbH = En(nc.sbuf_tensor("xnbH", [6, D], BF16))
        iotaM = En(nc.sbuf_tensor("iotaM", [128, CAP], F32))
        ltri = En(nc.sbuf_tensor("ltri", [128, 128], BF16))
        ones128 = En(nc.sbuf_tensor("ones128", [128, 128], BF16))
        maskf = En(nc.sbuf_tensor("maskf", [128, 64], F32))
        maskb = En(nc.sbuf_tensor("maskb", [128, 64], BF16))

        pk = [En(nc.psum_tensor(f"pk{i}", [128, 512], F32)) for i in range(6)]
        pkb = [En(nc.psum_tensor(f"pkb{i}", [128, 1024], BF16)) for i in range(2)]

        S = {s: En(nc.semaphore(s)) for s in cnt}
        block = En(nc.Block())

        def scr_f32(col, n):
            return scr[:, col:col + 2 * n].bitcast(F32)
        XS = [lambda i=i: xs_t[i][:] for i in range(3)]
        xnT = None
        def _cw(col, ncols_f32, np_=H):
            return lambda: scr[0:np_, col:col + 2 * ncols_f32].bitcast(F32)
        c_cw1 = _cw(0, 32, 112)
        c_cb1 = _cw(64, 32, 112)
        c_cw2 = _cw(128, 1024, 112)
        c_cb2 = _cw(2176, 32, 112)
        c_cw3 = _cw(2240, 32, 112)
        c_cb3 = _cw(2304, 8, 112)
        c_pos = _cw(2320, 8, 112)
        c_h1 = _cw(2336, 32, 112)
        c_h2 = _cw(2400, 32, 112)
        c_cwt = _cw(2464, 8, 112)
        c_bn = _cw(2480, 8, 112)
        XNB = [lambda i=i: xnb_t[i][:] for i in range(8)]
        D1V = lambda: scr_f32(5480, 896)
        D2V = lambda: scr[0:6, 0:1792].bitcast(F32)
        xnT = scr[:, 0:8240]
        A0 = 8240
        QTV = lambda: scr[:, A0:A0 + 1024]
        KTV = lambda: scr[:, A0 + 1024:A0 + 1024 + 8 * 134]
        V3 = [lambda k=k: scr[:, A0 + 2096 + 1024 * k:A0 + 2096 + 1024 * (k + 1)]
              for k in range(3)]
        E128 = scr[:, A0:A0 + 224].bitcast(F32)
        ones_rf = scr[0:1, A0 + 224:A0 + 480].bitcast(F32)
        AHB = lambda: scr[:, A0 + 5168:A0 + 6192]
        AHT = lambda: scr[:, A0 + 6192:A0 + 7216]
        EWF = [lambda k=k: scr[:, A0 + 7216 + 256 * k:A0 + 7472 + 256 * k] for k in range(4)]
        EW1 = [lambda k=k: scr[:, A0 + 7216 + 256 * k:A0 + 7344 + 256 * k] for k in range(4)]
        EW2 = [lambda k=k: scr[0:6, A0 + 7344 + 256 * k:A0 + 7472 + 256 * k] for k in range(4)]
        H1V = lambda fb: scr[:, fb * CAP:(fb + 1) * CAP]
        XGV = lambda hb: scr[:, 6144 + hb * CAP:6144 + (hb + 1) * CAP]
        PV = lambda t: scr[:, 9216 + t * CAP:9216 + (t + 1) * CAP]
        PTV = lambda sc: scr[:, 12288 + sc * PT:12288 + (sc + 1) * PT]
        O2V = lambda sb: scr[:, 6144 + sb * D:6144 + (sb + 1) * D]

        AREAS = [(slot0, 0), (slot0, 16384), (slot1, 0)]

        def w1v(ar, hb, fb):
            t, off = AREAS[ar]
            return t[:, off + hb * 2048 + fb * 128: off + hb * 2048 + (fb + 1) * 128]

        def w2v(ar, fb, dch):
            t, off = AREAS[ar]
            return t[:, off + fb * 1024 + dch * 512: off + fb * 1024 + (dch + 1) * 512]

        def attwv(wb, which, hb, c0, n):
            ar = (wb + 1) % 3 if which < 2 else (wb + 2) % 3
            t, off = AREAS[ar]
            base = off + (which % 2) * 8192 + hb * 1024
            return t[:, base + c0: base + c0 + n]

        cdmas = [
            (lambda: b1s[:].rearrange("p (e c) -> p e c", e=E), b1T.transpose([1, 0, 2])),
            (lambda: idb[:], identb),
            (lambda: ones_c[:], onescol_b),
            (lambda: ones_rf[:], onesrow_f),
            (lambda: c_cw1(), cw1t), (lambda: c_cb1(), cb1t),
            (lambda: c_cw2(), cw2f), (lambda: c_cb2(), cb2t),
            (lambda: c_cw3(), cw3s), (lambda: c_cb3()[:, 0:1], cb3s),
            (lambda: c_pos()[:, 0:1], post),
            (lambda: D1V().rearrange("p (w j) -> p w j", w=WIN), D1.rearrange("w r j -> r w j")),
            (lambda: iotaM[:], iotaM_i),
            (lambda: ltri[:], ltri_i),
            (lambda: ones128[:], ones128_i),
        ]
        for dstf, src in cdmas:
            op("sync", lambda e, S, _d=dstf, _s=src: e.dma_start(out=_d(), in_=_s),
               "s_xc", 16)
        n_const_sx = cnt["s_xc"]

        for dstf, src in ((lambda: rwb[:].rearrange("p (hb e) -> p hb e", hb=8), rwT.rearrange("(hb p) e -> p hb e", p=128)),
                          (lambda: b2bo[0:8, :], b2i), (lambda: b2bo[8:9, :], boi),
                          (lambda: sel9[:], sel9_i)):
            op("gpsimd", lambda e, S, _d=dstf, _s=src: e.dma_start(out=_d(), in_=_s),
               "s_wc", 16)
        n_const_sw = cnt["s_wc"]

        area_prev = [0, 0, 0]
        area_sem = ["s_wa0", "s_wa1", "s_wa2"]
        wd_sem = ["s_wd0", "s_wd1", "s_wd2"]
        area_wfree = [0, 0, 0]
        pend_wb = []

        def flush_wb(n_keep=0):
            while len(pend_wb) > n_keep:
                ar, lsem, lval, dsts = pend_wb.pop(0)
                wt("gpsimd", lsem, lval)
                for dst, view in dsts:
                    op("gpsimd", lambda e, S, _d=dst, _v=view: e.dma_start(
                        out=_d, in_=_v()), wd_sem[ar], 16)
                area_wfree[ar] = cnt[wd_sem[ar]]

        def emit_piece(ar, dmas, wb_dsts=None):
            flush_wb(0)
            wt("gpsimd", "s_te", area_prev[ar])
            wt("gpsimd", wd_sem[ar], area_wfree[ar])
            sem = area_sem[ar]
            for dstf, srcf in dmas:
                op("gpsimd", lambda e, S, _d=dstf, _s=srcf: e.dma_start(out=_d(), in_=_s),
                   sem, 16)
            if wb_dsts is not None:
                pend_wb.append((ar, sem, cnt[sem], wb_dsts))
            return (sem, cnt[sem])

        def w1_dmas(ar, e_, phi, cached):
            t, off = AREAS[ar]
            if cached:
                return [(lambda _t=t, _o=off: _t[:, _o:_o + 16384],
                         w1d[2 * e_ + phi])]
            return [(lambda _t=t, _o=off: _t[:, _o:_o + 16384]
                     .rearrange("p (hb f) -> p hb f", hb=8),
                     w1T[e_, :, phi * FH:(phi + 1) * FH]
                     .rearrange("(hb p) f -> p hb f", p=128))]

        def w2_dmas(ar, e_, phi, cached):
            t, off = AREAS[ar]
            if cached:
                return [(lambda _t=t, _o=off: _t[:, _o:_o + 16384],
                         w2d[2 * e_ + phi])]
            res = []
            for half in range(2):
                res.append((lambda _t=t, _o=off, _h=half: _t[:, _o + _h * 8192:
                                                             _o + (_h + 1) * 8192]
                            .rearrange("p (fb d) -> p fb d", fb=8),
                            w2T[e_, phi * FH + half * 1024: phi * FH + (half + 1) * 1024, :]
                            .rearrange("(fb p) d -> p fb d", p=128)))
            return res

        def w1_wb(ar, e_, phi):
            t, off = AREAS[ar]
            return [(w1d[2 * e_ + phi], lambda _t=t, _o=off: _t[:, _o:_o + 16384])]

        def w2_wb(ar, e_, phi):
            t, off = AREAS[ar]
            return [(w2d[2 * e_ + phi], lambda _t=t, _o=off: _t[:, _o:_o + 16384])]

        def emit_attw(wb, cached, wbk):
            flush_wb(0)
            ar1, ar2 = (wb + 1) % 3, (wb + 2) % 3
            wt("gpsimd", "s_te", area_prev[ar1])
            wt("gpsimd", wd_sem[ar1], area_wfree[ar1])
            for wi, wsrc in ((0, wqT), (1, wkT)):
                t, off = AREAS[ar1]
                if cached:
                    op("gpsimd", lambda e, S, _t=t, _o=off + wi * 8192, _w=wi:
                       e.dma_start(out=_t[:, _o:_o + 8192], in_=attd[_w]), "s_wt", 16)
                else:
                    op("gpsimd", lambda e, S, _t=t, _o=off + wi * 8192, _s=wsrc:
                       e.dma_start(
                        out=_t[:, _o:_o + 8192].rearrange("p (hb d) -> p hb d", hb=8),
                        in_=_s.rearrange("(hb p) d -> p hb d", p=128)), "s_wt", 16)
            if wbk:
                t, off = AREAS[ar1]
                pend_wb.append((ar1, "s_wt", cnt["s_wt"],
                                [(attd[0], lambda _t=t, _o=off: _t[:, _o:_o + 8192]),
                                 (attd[1], lambda _t=t, _o=off: _t[:, _o + 8192:_o + 16384])]))
            wt("gpsimd", "s_te", area_prev[ar2])
            wt("gpsimd", wd_sem[ar2], area_wfree[ar2])
            for wi, wsrc, sem in ((0, wvT, "s_wv"), (1, woT, "s_wt2")):
                t, off = AREAS[ar2]
                if cached:
                    op("gpsimd", lambda e, S, _t=t, _o=off + wi * 8192, _w=2 + wi:
                       e.dma_start(out=_t[:, _o:_o + 8192], in_=attd[_w]), sem, 16)
                else:
                    op("gpsimd", lambda e, S, _t=t, _o=off + wi * 8192, _s=wsrc:
                       e.dma_start(
                        out=_t[:, _o:_o + 8192].rearrange("p (hb d) -> p hb d", hb=8),
                        in_=_s.rearrange("(hb p) d -> p hb d", p=128)), sem, 16)
                if wbk:
                    pend_wb.append((ar2, sem, cnt[sem],
                                    [(attd[2 + wi], lambda _t=t, _o=off + wi * 8192:
                                      _t[:, _o:_o + 8192])]))
            return (cnt["s_wt"], cnt["s_wv"], cnt["s_wt2"])

        wt("vector", "s_xc", n_const_sx)
        wt("vector", "s_wc", n_const_sw)

        def r3(t, wdim):
            return t().rearrange("p (w j) -> p w j", w=wdim)

        ve(lambda e, S: e.scalar_tensor_tensor(
            c_h1(), c_cw1(), c_pos()[:, 0:1], c_cb1(), ALU.mult, ALU.add))
        wt("scalar", "s_ve", cnt["s_ve"])
        se(lambda e, S: e.activation(c_h1(), c_h1(), AF.Gelu))
        wt("vector", "s_se", cnt["s_se"])
        for i in range(32):
            blk = (lambda _i=i: c_cw2()[:, _i * 32:(_i + 1) * 32])
            col = (lambda _i=i: c_h1()[:, _i:_i + 1])
            if i == 0:
                ve(lambda e, S, _b=blk, _c=col: e.tensor_scalar(
                    c_h2(), _b(), _c(), None, ALU.mult))
            else:
                ve(lambda e, S, _b=blk, _c=col: e.scalar_tensor_tensor(
                    c_h2(), _b(), _c(), c_h2(), ALU.mult, ALU.add))
        ve(lambda e, S: e.tensor_tensor(c_h2(), c_h2(), c_cb2(), ALU.add))
        wt("scalar", "s_ve", cnt["s_ve"])
        se(lambda e, S: e.activation(c_h2(), c_h2(), AF.Gelu))
        wt("vector", "s_se", cnt["s_se"])
        ve(lambda e, S: e.tensor_tensor(c_h2(), c_h2(), c_cw3(), ALU.mult))
        ve(lambda e, S: e.bn_stats(c_bn()[:, 0:6], c_h2()))
        ve(lambda e, S: e.bn_aggr(c_bn()[:, 6:8], c_bn()[:, 0:6]))
        cwt_ve = ve(lambda e, S: e.scalar_tensor_tensor(
            c_cwt()[:, 0:1], c_bn()[:, 6:7], 32.0, c_cb3()[:, 0:1],
            ALU.mult, ALU.add))

        wt("sync", "s_ve", cwt_ve)
        op("sync", lambda e, S: e.dma_start(out=cwt_d, in_=c_cwt()[:, 0:1]), "s_xb", 16)
        op("sync", lambda e, S: e.dma_start(
            out=D2V().rearrange("p (w j) -> p w j", w=WIN),
            in_=D2.rearrange("w r j -> r w j")), "s_xb", 16)
        wt("sync", "s_xb", cnt["s_xb"])
        op("sync", lambda e, S: e.dma_start(
            out=c_row_t[:], in_=cwt_d.rearrange("h w -> (h w)").unsqueeze(0)), "s_xb", 16)
        wt("scalar", "s_xb", cnt["s_xb"])
        exp_se = se(lambda e, S: e.activation(c_row_t[:], c_row_t[:], AF.Exp))
        wt("tensor", "s_se", exp_se)
        wt("tensor", "s_xc", n_const_sx)
        bank_guard(5)
        te(lambda e, S: e.matmul(pk[5][:, 0:H * WIN], ones_rf[:], c_row_t[:],
                                 start=True, stop=True))
        wt("vector", "s_te", cnt["s_te"])
        wt("vector", "s_xb", cnt["s_xb"])
        ve(lambda e, S: e.tensor_copy(E128[:], pk[5][:, 0:H * WIN]))
        bank_read(5, "s_ve")
        ve(lambda e, S: e.memset(cMall[:], 0.0))
        for _b in range(6):
            ve(lambda e, S, _bb=_b: e.memset(pk[_bb][:], 0.0))
        psum_init_ve = cnt["s_ve"]
        wt("tensor", "s_ve", psum_init_ve)
        for h in range(H):
            for w in range(WIN):
                d2w = (lambda _w=w: D2V()[:, _w * 128:(_w + 1) * 128])
                sc6 = (lambda _h=h, _w=w: E128[0:6, _h * WIN + _w:_h * WIN + _w + 1])
                o2 = (lambda _h=h: cMall[0:6, _h * 256 + 128:(_h + 1) * 256])
                if w == 0:
                    ve(lambda e, S, _d=d2w, _s=sc6, _o=o2: e.tensor_scalar(
                        _o(), _d(), _s(), None, ALU.mult))
                else:
                    ve(lambda e, S, _d=d2w, _s=sc6, _o=o2: e.scalar_tensor_tensor(
                        _o(), _d(), _s(), _o(), ALU.mult, ALU.add))
        cM2_ve = cnt["s_ve"]
        for h in range(H):
            for w in range(WIN):
                d1w = (lambda _w=w: D1V()[:, _w * 128:(_w + 1) * 128])
                sc = (lambda _h=h, _w=w: E128[:, _h * WIN + _w:_h * WIN + _w + 1])
                o1 = (lambda _h=h: cMall[:, _h * 256:_h * 256 + 128])
                if w == 0:
                    ve(lambda e, S, _d=d1w, _s=sc, _o=o1: e.tensor_scalar(
                        _o(), _d(), _s(), None, ALU.mult))
                else:
                    ve(lambda e, S, _d=d1w, _s=sc, _o=o1: e.scalar_tensor_tensor(
                        _o(), _d(), _s(), _o(), ALU.mult, ALU.add))
        connM_ve = cnt["s_ve"]

        xs_free = {}
        qkv_last_te = 0
        v3_free_te = {}
        ew_free_te = [0, 0, 0, 0]
        ah_t_te_prev = [0]
        xnb_pass_free = [0]
        aht_free_te = 0
        pass_ve_end = 0
        pre_x = {}

        att_w_next = [None]
        xdma_of = {}
        ln_affine = {}
        ln_evac = {}
        ln_tp_te = {}
        for p in range(NPASS):

            def emit_x(q, _p=p):
                g = _p * 9 + q
                buf = g % 3
                xsem = f"s_xs{buf}"
                if (_p, q) in pre_x:
                    return pre_x[(_p, q)]
                if g >= 3:
                    vefree, sefree = xs_free[g - 3]
                    wt("sync", "s_ve", vefree)
                    wt("sync", "s_se", sefree)
                if q == 0:
                    op("sync", lambda e, S, _pp=_p, _b=buf: e.dma_start(
                        out=XS[_b]()[0:3, :],
                        in_=x[_pp * PT: _pp * PT + 3, :]), xsem, 16)
                    op("sync", lambda e, S, _pp=_p, _b=buf: e.dma_start(
                        out=XS[_b]()[3:6, :],
                        in_=x[_pp * PT + 1027: _pp * PT + 1030, :]), xsem, 16)
                else:
                    c = q - 1
                    op("sync", lambda e, S, _pp=_p, _c=c, _b=buf: e.dma_start(
                        out=XS[_b](),
                        in_=x[_pp * PT + 3 + _c * 128: _pp * PT + 3 + (_c + 1) * 128, :]),
                        xsem, 16)
                return cnt[xsem]

            def emit_acc_init(_p=p):
                wt("sync", "s_out", 16 * _p)
                op("sync", lambda e, S, _pp=_p: e.dma_start(
                    out=acc[:].rearrange("q (tb d) -> q tb d", tb=8),
                    in_=x[3 + _pp * PT: 3 + (_pp + 1) * PT, :]
                        .rearrange("(tb q) d -> q tb d", q=128)), "s_xa", 16)
                return cnt["s_xa"]

            for c in range(3):
                if (p, c) not in xdma_of:
                    xdma_of[(p, c)] = emit_x(c)

            def emit_ln_front(q, _p=p):
                g = _p * 9 + q
                buf = g % 3
                np_ = 6 if q == 0 else 128
                xsem = f"s_xs{buf}"
                wt("vector", xsem, xdma_of[(_p, q)])
                B = buf
                ve(lambda e, S, _b=buf, _B=B, _n=np_: e.bn_stats(
                    bnst[0:_n, _B * 12:_B * 12 + 6], XS[_b]()[0:_n, 0:512]))
                ve(lambda e, S, _b=buf, _B=B, _n=np_: e.bn_stats(
                    bnst[0:_n, _B * 12 + 6:_B * 12 + 12], XS[_b]()[0:_n, 512:1024]))
                stats_ve = ve(lambda e, S, _B=B, _n=np_: e.bn_aggr(
                    bnagg[0:_n, _B * 2:_B * 2 + 2], bnst[0:_n, _B * 12:_B * 12 + 12]))
                veps = ve(lambda e, S, _B=B, _n=np_: e.tensor_scalar_add(
                    lnt[0:_n, _B:_B + 1], bnagg[0:_n, _B * 2 + 1:_B * 2 + 2], 1e-5))
                wt("scalar", "s_ve", veps)
                se(lambda e, S, _B=B, _n=np_: e.activation(
                    lnt[0:_n, _B:_B + 1], lnt[0:_n, _B:_B + 1], AF.Ln))
                rstd_se = se(lambda e, S, _B=B, _n=np_: e.activation(
                    rstd[0:_n, _B:_B + 1], lnt[0:_n, _B:_B + 1], AF.Exp, scale=-0.5))
                wt("vector", "s_se", rstd_se)
                nmr_ve = ve(lambda e, S, _B=B, _n=np_: e.scalar_tensor_tensor(
                    nmr[0:_n, _B:_B + 1], bnagg[0:_n, _B * 2:_B * 2 + 1], -1.0,
                    rstd[0:_n, _B:_B + 1], ALU.mult, ALU.mult))
                wt("scalar", "s_ve", nmr_ve)
                wt("scalar", "s_te", xnb_pass_free[0])
                xnbv = (lambda: xnbH[:]) if q == 0 else XNB[q - 1]
                aff_se = se(lambda e, S, _b=buf, _B=B, _n=np_, _xv=xnbv: e.activation(
                    _xv(), XS[_b]()[0:_n, :], AF.Identity,
                    bias=nmr[0:_n, _B:_B + 1], scale=rstd[0:_n, _B:_B + 1]))
                ln_affine[(_p, q)] = aff_se
                xs_free[g] = (stats_ve, aff_se)

            def emit_ln_back(q, _p=p):
                g = _p * 9 + q
                if q + 3 <= 8:
                    xdma_of[(_p, q + 3)] = emit_x(q + 3)
                pb = q % 2
                np_ = 6 if q == 0 else 128
                xnbv = (lambda: xnbH[:]) if q == 0 else XNB[q - 1]
                wt("tensor", "s_se", ln_affine[(_p, q)])
                wt("tensor", "s_se", bank_free[6 + pb]["s_se"])
                wt("tensor", "s_ve", bank_free[6 + pb]["s_ve"])
                for db in range(8):
                    te(lambda e, S, _db=db, _pb=pb, _n=np_, _xv=xnbv: e.transpose(
                        pkb[_pb][:, _db * 8 if _n == 6 else _db * 128:
                                 (_db * 8 + 6) if _n == 6 else (_db + 1) * 128],
                        _xv()[0:_n, _db * 128:(_db + 1) * 128], idb[0:_n, 0:_n]),
                        inc=(db == 7))
                ln_tp_te[(_p, q)] = cnt["s_te"]
                wt("vector", "s_te", cnt["s_te"])
                if q == 0:
                    ev_ve = ve(lambda e, S: e.tensor_copy(
                        xnT.rearrange("r (db t) -> r db t", db=8)[:, :, 0:3],
                        pkb[0][:, 0:64].rearrange("r (db t) -> r db t", t=8)[:, :, 0:3]))
                    ev_ve = ve(lambda e, S: e.tensor_copy(
                        xnT.rearrange("r (db t) -> r db t", db=8)[:, :, 1027:1030],
                        pkb[0][:, 0:64].rearrange("r (db t) -> r db t", t=8)[:, :, 3:6]))
                else:
                    c = q - 1
                    ev_ve = ve(lambda e, S, _c=c, _pb=pb: e.tensor_copy(
                        xnT.rearrange("r (db t) -> r db t", db=8)
                        [:, :, 3 + _c * 128:3 + _c * 128 + 128],
                        pkb[_pb][:].rearrange("r (db t) -> r db t", db=8)))
                ln_evac[(_p, q)] = ("s_ve", ev_ve)
                bank_free[6 + pb]["s_ve"] = ev_ve

            wb = p % 3
            area_of = lambda i, _wb=wb: (_wb + i) % 3
            if p >= 1:
                for a3 in range(3):
                    wt("gpsimd", wd_sem[a3], area_wfree[a3])
            if att_w_next[0] is not None:
                att_w_sw = att_w_next[0]
                att_w_next[0] = None
            else:
                att_w_sw = emit_attw(wb, cached=(p > 1), wbk=(p == 1))
            sl_att = wb
            w1_sw = {}
            w2_sw = {}
            w1_sw[0] = emit_piece(area_of(0), w1_dmas(area_of(0), 0, 0, p > 0),
                                  w1_wb(area_of(0), 0, 0) if p == 0 else None)

            if p == 0:
                for q0_ in (0, 1, 2):
                    emit_ln_front(q0_)
                emit_ln_back(0)
                emit_ln_back(1)
                emit_ln_front(3)
                emit_ln_back(2)
            else:
                for q0_ in (6, 7, 8):
                    emit_ln_front(q0_)
                for qb_ in (0, 1, 2, 3):
                    emit_ln_back(qb_)
            acc_init_sx = emit_acc_init()

            wt("tensor", "s_wt", att_w_sw[0])
            for ach in range(NACH):
                if ach + 4 <= 8 and (p, ach + 4) not in ln_affine:
                    emit_ln_front(ach + 4)
                nb_ = ach + (3 if p == 0 else 4)
                if nb_ <= 8 and (p, nb_) not in ln_evac:
                    emit_ln_back(nb_)
                qq = min(ach + 2, 8)
                wt("tensor", ln_evac[(p, qq)][0], ln_evac[(p, qq)][1])
                if p == 0 and ach == 0:
                    wt("scalar", "s_ve", connM_ve)
                for dqb in range(8):
                    bk = dqb % 4
                    bank_guard(bk)
                    for hb in range(8):
                        te(lambda e, S, _q=dqb, _h=hb, _a=ach, _sl=sl_att, _bk=bk:
                           e.matmul(pk[_bk][:, 0:ACH],
                                    attwv(_sl, 0, _h, _q * 128, 128),
                                    xnT[:, _h * PTH + 3 + _a * ACH:
                                        _h * PTH + 3 + (_a + 1) * ACH],
                                    start=(_h == 0), stop=(_h == 7)), inc=(hb == 7))
                    wt("scalar", "s_te", cnt["s_te"])
                    ev = se(lambda e, S, _q=dqb, _bk=bk: e.copy(
                        QTV()[:, _q * ACH:(_q + 1) * ACH], pk[_bk][:, 0:ACH]))
                    bank_free[bk]["s_se"] = ev
                for dqb in range(8):
                    bk = dqb % 4
                    bank_guard(bk)
                    for hb in range(8):
                        te(lambda e, S, _q=dqb, _h=hb, _a=ach, _sl=sl_att, _bk=bk:
                           e.matmul(pk[_bk][:, 0:134],
                                    attwv(_sl, 1, _h, _q * 128, 128),
                                    xnT[:, _h * PTH + _a * ACH:
                                        _h * PTH + _a * ACH + 134],
                                    start=(_h == 0), stop=(_h == 7)), inc=(hb == 7))
                    wt("scalar", "s_te", cnt["s_te"])
                    ev = se(lambda e, S, _q=dqb, _bk=bk: e.copy(
                        KTV()[:, _q * 134:(_q + 1) * 134], pk[_bk][:, 0:134]))
                    bank_free[bk]["s_se"] = ev
                newtiles = (0, 1) if ach == 0 else (ach + 1,)
                wt("tensor", "s_wv", att_w_sw[1])
                for gv in newtiles:
                    vb = gv % 3
                    nr = 6 if gv == 8 else 128
                    for dch in range(2):
                        bk = dch
                        bank_guard(bk)
                        for hb in range(8):
                            te(lambda e, S, _g=gv, _h=hb, _d=dch, _sl=sl_att,
                               _bk=bk, _nr=nr:
                               e.matmul(pk[_bk][0:_nr, 0:512],
                                        xnT[:, _h * PTH + _g * 128:
                                            _h * PTH + _g * 128 + _nr],
                                        attwv(_sl, 2, _h, _d * 512, 512),
                                        start=(_h == 0), stop=(_h == 7)),
                               inc=(hb == 7))
                        wt("scalar", "s_te", cnt["s_te"])
                        wt("scalar", "s_te", v3_free_te.get(vb, 0))
                        ev = se(lambda e, S, _vb=vb, _d=dch, _bk=bk, _nr=nr: e.copy(
                            V3[_vb]()[0:_nr, _d * 512:(_d + 1) * 512],
                            pk[_bk][0:_nr, 0:512]))
                        bank_free[bk]["s_se"] = ev

                vb0 = ach % 3
                vb1 = (ach + 1) % 3

                def emit_st(h):
                    sbk = h % 4
                    blk = h // 2
                    po = 64 * (h % 2)
                    bank_guard(sbk)
                    te(lambda e, S, _b=blk, _po=po, _bk=sbk: e.matmul(
                        pk[_bk][:, 0:128],
                        KTV()[_po:_po + 64, _b * 134: _b * 134 + 128],
                        QTV()[_po:_po + 64, _b * 128: _b * 128 + 128],
                        start=True, stop=True), inc=False)
                    return te(lambda e, S, _b=blk, _po=po, _bk=sbk: e.matmul(
                        pk[_bk][0:6, 128:256],
                        KTV()[_po:_po + 64, _b * 134 + 128: _b * 134 + 134],
                        QTV()[_po:_po + 64, _b * 128: _b * 128 + 128],
                        start=True, stop=True))

                st_done = {}
                od_of = {}
                ah_t_te = {}

                def emit_ah_t(db, _ach=ach):
                    _pb = _ach % 2
                    wt("tensor", "s_ve", cnt["s_ve"])
                    if db == 0:
                        wt("tensor", "s_se", bank_free[6 + _pb]["s_se"])
                        wt("tensor", "s_ve", bank_free[6 + _pb]["s_ve"])
                    te(lambda e, S, _db=db, _p2=_pb: e.transpose(
                        pkb[_p2][:, _db * 128:(_db + 1) * 128],
                        AHB()[:, _db * 128:(_db + 1) * 128], idb[:]))
                    ah_t_te[db] = cnt["s_te"]

                def emit_norm(h):
                    obk_ = 4 + h % 2
                    wt("vector", "s_te", od_of[h])
                    if h == 0:
                        wt("vector", "s_te", ah_t_te_prev[0])
                    ve(lambda e, S, _bk=obk_: e.reciprocal(
                        rden[:], pk[_bk][:, 64:65]))
                    ve(lambda e, S, _h=h, _bk=obk_: e.tensor_scalar(
                        AHB()[:, _h * 64:(_h + 1) * 64],
                        pk[_bk][:, 0:64], rden[:, 0:1], None, ALU.mult))
                    bank_read(obk_, "s_ve")

                for h in range(4):
                    st_done[h] = emit_st(h)
                qkv_last_te = st_done[3]
                for h in range(H):
                    sbk = h % 4
                    obk = 4 + h % 2
                    ewb = h % 4
                    wt("scalar", "s_te", st_done[h])
                    wt("scalar", "s_te", ew_free_te[ewb])
                    exp_se = se(lambda e, S, _e=ewb, _bk=sbk: e.activation(
                        EWF[_e](), pk[_bk][:, 0:256], AF.Exp, scale=0.125))
                    bank_free[sbk]["s_se"] = exp_se
                    wt("vector", "s_se", exp_se)
                    ewc_ve = ve(lambda e, S, _e=ewb, _h=h: e.tensor_tensor(
                        EWF[_e](), EWF[_e](), cMall[:, _h * 256:(_h + 1) * 256],
                        ALU.mult))
                    wt("tensor", "s_ve", ewc_ve)
                    bank_guard(obk)
                    te(lambda e, S, _e=ewb, _v=vb0, _h=h, _bk=obk: e.matmul(
                        pk[_bk][:, 0:64], EW1[_e](),
                        V3[_v]()[:, _h * 64:(_h + 1) * 64],
                        start=True, stop=False), inc=False)
                    te(lambda e, S, _e=ewb, _v=vb1, _h=h, _bk=obk: e.matmul(
                        pk[_bk][:, 0:64], EW2[_e](),
                        V3[_v]()[0:6, _h * 64:(_h + 1) * 64],
                        start=False, stop=True), inc=False)
                    te(lambda e, S, _e=ewb, _bk=obk: e.matmul(
                        pk[_bk][:, 64:65], EW1[_e](), ones_c[:],
                        start=True, stop=False), inc=False)
                    od_te = te(lambda e, S, _e=ewb, _bk=obk: e.matmul(
                        pk[_bk][:, 64:65], EW2[_e](), ones_c[0:6, :],
                        start=False, stop=True))
                    ew_free_te[ewb] = od_te
                    v3_free_te[vb0] = od_te
                    v3_free_te[vb1] = od_te
                    od_of[h] = od_te
                    if h + 4 < H:
                        st_done[h + 4] = emit_st(h + 4)
                        qkv_last_te = st_done[h + 4]
                    if h > 0:
                        emit_norm(h - 1)
                    if h >= 3 and h % 2 == 1:
                        emit_ah_t((h - 3) // 2)
                if True:
                    emit_norm(H - 1)
                    emit_ah_t(6)
                    emit_ah_t(7)
                ah_t_te_prev[0] = ah_t_te[7]
                tpb = ach % 2
                wt("scalar", "s_te", ah_t_te[3])
                wt("scalar", "s_te", aht_free_te)
                aht_se1 = se(lambda e, S, _pb=tpb: e.copy(
                    AHT()[:, 0:512], pkb[_pb][:, 0:512]))
                wt("scalar", "s_te", ah_t_te[7])
                aht_se2 = se(lambda e, S, _pb=tpb: e.copy(
                    AHT()[:, 512:1024], pkb[_pb][:, 512:1024]))
                wt("tensor", "s_wt2", att_w_sw[2])
                bank_free[6 + tpb]["s_se"] = aht_se2
                wt("tensor", "s_se", aht_se1)
                bank_guard(4)
                bank_guard(5)
                wt("tensor", "s_wc", n_const_sw)
                for dqb in range(8):
                    if dqb == 4:
                        wt("tensor", "s_se", aht_se2)
                    for dch in range(2):
                        te(lambda e, S, _q=dqb, _d=dch, _sl=sl_att: e.matmul(
                            pk[4 + _d][:, 0:512],
                            AHT()[:, _q * 128:(_q + 1) * 128],
                            attwv(_sl, 3, _q, _d * 512, 512),
                            start=(_q == 0), stop=False), inc=False)
                for dch in range(2):
                    te(lambda e, S, _d=dch: e.matmul(
                        pk[4 + _d][:, 0:512], sel9[:, 8 * 128:9 * 128],
                        b2bo[:, _d * 512:(_d + 1) * 512],
                        start=False, stop=True), inc=(dch == 1))
                aht_free_te = cnt["s_te"]
                wt("vector", "s_te", aht_free_te)
                wt("vector", "s_xa", acc_init_sx)
                for dch in range(2):
                    ve(lambda e, S, _tb=ach, _d=dch: e.tensor_add(
                        acc[:, _tb * D + _d * 512: _tb * D + (_d + 1) * 512],
                        acc[:, _tb * D + _d * 512: _tb * D + (_d + 1) * 512],
                        pk[4 + _d][:, 0:512]))
                bank_read(4, "s_ve")
                bank_read(5, "s_ve")
            att_te_end = cnt["s_te"]
            area_prev[(wb + 1) % 3] = att_te_end
            area_prev[(wb + 2) % 3] = att_te_end
            wt("tensor", ln_evac[(p, 8)][0], ln_evac[(p, 8)][1])
            bank_guard(5)
            for tb in range(8):
                for hb in range(8):
                    te(lambda e, S, _tb=tb, _hb=hb: e.matmul(
                        pk[5][:, _tb * E:(_tb + 1) * E],
                        xnT[:, _hb * PTH + 3 + _tb * 128: _hb * PTH + 3 + (_tb + 1) * 128],
                        rwb[:, _hb * E:(_hb + 1) * E],
                        start=(_hb == 0), stop=(_hb == 7)), inc=(hb == 7))
            lg_te = cnt["s_te"]
            wt("vector", "s_te", lg_te)
            for tb in range(8):
                ve(lambda e, S, _tb=tb: e.max(
                    out=m8[:, _tb * 8:(_tb + 1) * 8],
                    in_=pk[5][:, _tb * E:(_tb + 1) * E]))
            for tb in range(8):
                ve(lambda e, S, _tb=tb: e.tensor_sub(
                    g2c[:, _tb:_tb + 1], m8[:, _tb * 8 + 1:_tb * 8 + 2],
                    m8[:, _tb * 8:_tb * 8 + 1]))
            sub_ve = cnt["s_ve"]
            wt("scalar", "s_ve", sub_ve)
            sig_se = se(lambda e, S: e.activation(g2c[:], g2c[:], AF.Sigmoid))
            wt("vector", "s_se", sig_se)
            ve(lambda e, S: e.tensor_scalar(
                g1c[:], g2c[:], -1.0, 1.0, ALU.mult, ALU.add))
            for tb in range(8):
                ve(lambda e, S, _tb=tb: e.tensor_scalar(
                    Gs[:, _tb * E:(_tb + 1) * E], pk[5][:, _tb * E:(_tb + 1) * E],
                    m8[:, _tb * 8:_tb * 8 + 1], g1c[:, _tb:_tb + 1],
                    ALU.is_equal, ALU.mult))
                ve(lambda e, S, _tb=tb: e.tensor_scalar(
                    gtmp[:], pk[5][:, _tb * E:(_tb + 1) * E],
                    m8[:, _tb * 8 + 1:_tb * 8 + 2], g2c[:, _tb:_tb + 1],
                    ALU.is_equal, ALU.mult))
                ve(lambda e, S, _tb=tb: e.tensor_add(
                    Gs[:, _tb * E:(_tb + 1) * E], Gs[:, _tb * E:(_tb + 1) * E],
                    gtmp[:]))
            bank_read(5, "s_ve")
            ve(lambda e, S: e.tensor_scalar(
                maskf[:], Gs[:], 0.0, None, ALU.is_gt))
            mb_ve = ve(lambda e, S: e.tensor_copy(maskb[:], maskf[:]))
            wt("tensor", "s_ve", mb_ve)
            bank_guard(4)
            for kk in range(8):
                te(lambda e, S, _k=kk: e.matmul(
                    pk[4][:, _k * 8:(_k + 1) * 8], ltri[:],
                    maskb[:, _k * 8:(_k + 1) * 8],
                    start=True, stop=(_k == 0)), inc=False)
                for jj in range(kk):
                    te(lambda e, S, _k=kk, _j=jj: e.matmul(
                        pk[4][:, _k * 8:(_k + 1) * 8], ones128[:],
                        maskb[:, _j * 8:(_j + 1) * 8],
                        start=False, stop=(_j == _k - 1)),
                       inc=(kk == 7 and jj == 6))
            pos_te = cnt["s_te"]
            wt("vector", "s_te", pos_te)
            poss_ve = ve(lambda e, S: e.tensor_copy(poss[:], pk[4][:, 0:64]))
            bank_read(4, "s_ve")

            pbuild_ve = {}

            def emit_pbuild(e_, after_te):
                wt("vector", "s_te", after_te)
                for t in range(8):
                    ve(lambda e, S, _t=t, _e=e_: e.tensor_scalar(
                        PV(_t), iotaM[:],
                        poss[:, _t * 8 + _e:_t * 8 + _e + 1],
                        maskf[:, _t * 8 + _e:_t * 8 + _e + 1],
                        ALU.is_equal, ALU.mult))
                pbuild_ve[e_] = cnt["s_ve"]

            emit_pbuild(0, 0)

            if p + 1 < NPASS:
                for q2 in (0, 1, 2):
                    g2 = (p + 1) * 9 + q2
                    buf2 = g2 % 3
                    vefree, sefree = xs_free[g2 - 3]
                    wt("sync", "s_ve", vefree)
                    wt("sync", "s_se", sefree)
                    if q2 == 0:
                        op("sync", lambda e, S, _p=p + 1, _b=buf2: e.dma_start(
                            out=XS[_b]()[0:3, :],
                            in_=x[_p * PT: _p * PT + 3, :]), f"s_xs{buf2}", 16)
                        op("sync", lambda e, S, _p=p + 1, _b=buf2: e.dma_start(
                            out=XS[_b]()[3:6, :],
                            in_=x[_p * PT + 1027: _p * PT + 1030, :]),
                            f"s_xs{buf2}", 16)
                    else:
                        c2 = q2 - 1
                        op("sync", lambda e, S, _p=p + 1, _c=c2, _b=buf2: e.dma_start(
                            out=XS[_b](),
                            in_=x[_p * PT + 3 + _c * 128:
                                  _p * PT + 3 + (_c + 1) * 128, :]),
                            f"s_xs{buf2}", 16)
                    pre_x[(p + 1, q2)] = cnt[f"s_xs{buf2}"]
                    xdma_of[(p + 1, q2)] = pre_x[(p + 1, q2)]

            w2_sw[0] = emit_piece(area_of(1), w2_dmas(area_of(1), 0, 0, p > 1),
                                  w2_wb(area_of(1), 0, 0) if p == 1 else None)
            w1_sw[1] = emit_piece(area_of(2), w1_dmas(area_of(2), 0, 1, p > 0),
                                  w1_wb(area_of(2), 0, 1) if p == 0 else None)

            comp_te = {}
            for e_ in range(E):
                wt("tensor", "s_ve", pbuild_ve[e_])
                ptv_se = 0
                for sc in range(3):
                    pb = sc % 2
                    ns = min(CAP, (sc + 1) * 128) - sc * 128
                    wt("tensor", "s_se", bank_free[6 + pb]["s_se"])
                    wt("tensor", "s_ve", bank_free[6 + pb]["s_ve"])
                    for t in range(8):
                        te(lambda e, S, _t=t, _sc=sc, _pb=pb, _n=ns: e.transpose(
                            pkb[_pb][0:_n, _t * 128:(_t + 1) * 128],
                            PV(_t)[:, _sc * 128:_sc * 128 + _n], idb[:]),
                            inc=(t == 7))
                    tp_te = cnt["s_te"]
                    wt("scalar", "s_te", tp_te)
                    ptv_se = se(lambda e, S, _sc=sc, _pb=pb, _n=ns: e.activation(
                        PTV(_sc)[0:_n, :], pkb[_pb][0:_n, :], AF.Copy))
                    bank_free[6 + pb]["s_se"] = ptv_se
                for dc in range(8):
                    bk = dc % 2
                    bank_guard(bk)
                    for t in range(8):
                        te(lambda e, S, _t=t, _dc=dc, _bk=bk: e.matmul(
                            pk[_bk][:, 0:CAP],
                            XNB[_t]()[:, _dc * 128:(_dc + 1) * 128],
                            PV(_t), start=(_t == 0), stop=(_t == 7)),
                            inc=(t == 7))
                    cte = cnt["s_te"]
                    wt("scalar", "s_te", cte)
                    se(lambda e, S, _dc=dc, _bk=bk: e.activation(
                        XGV(_dc), pk[_bk][:, 0:CAP], AF.Copy))
                    bank_read(bk, "s_se")
                comp_te[e_] = cnt["s_te"]
                xg_se = cnt["s_se"]
                if e_ + 1 < E:
                    emit_pbuild(e_ + 1, comp_te[e_])
                else:
                    xnb_pass_free[0] = comp_te[e_]
                    if p + 1 < NPASS:
                        def pre_x_late(q3, _p=p + 1):
                            g3 = _p * 9 + q3
                            b3 = g3 % 3
                            vf3, sf3 = xs_free[g3 - 3]
                            wt("sync", "s_ve", vf3)
                            wt("sync", "s_se", sf3)
                            c3 = q3 - 1
                            op("sync", lambda e, S, _pp=_p, _c=c3, _b=b3:
                               e.dma_start(
                                out=XS[_b](),
                                in_=x[_pp * PT + 3 + _c * 128:
                                      _pp * PT + 3 + (_c + 1) * 128, :]),
                                f"s_xs{b3}", 16)
                            pre_x[(_p, q3)] = cnt[f"s_xs{b3}"]
                            xdma_of[(_p, q3)] = pre_x[(_p, q3)]
                        for qf in (0, 1, 2):
                            emit_ln_front(qf, _p=p + 1)
                        for q3 in (3, 4, 5):
                            pre_x_late(q3)
                        for qf in (3, 4, 5):
                            emit_ln_front(qf, _p=p + 1)
                        for q3 in (6, 7, 8):
                            pre_x_late(q3)
                for phi in range(2):
                    k = e_ * 2 + phi
                    ar1 = area_of(2 * k)
                    ar2 = area_of(2 * k + 1)
                    wt("tensor", w1_sw[k][0], w1_sw[k][1])
                    if phi == 0:
                        wt("tensor", "s_se", xg_se)
                    gelu_of = {}
                    for fb in range(NFB):
                        abk = fb % 2
                        bank_guard(abk)
                        for hb in range(8):
                            te(lambda e, S, _sl=ar1, _h=hb, _f=fb, _bk=abk:
                               e.matmul(pk[_bk][:, 0:CAP],
                                        w1v(_sl, _h, _f), XGV(_h),
                                        start=(_h == 0), stop=(_h == 7)),
                               inc=(hb == 7))
                        mm1_te = cnt["s_te"]
                        wt("scalar", "s_te", mm1_te)
                        gse = se(lambda e, S, _f=fb, _bk=abk, _e=e_, _phi=phi:
                                 e.activation(
                            H1V(_f), pk[_bk][:, 0:CAP], AF.Gelu,
                            bias=b1s[:, _e * 32 + _phi * 16 + _f:
                                     _e * 32 + _phi * 16 + _f + 1]))
                        gelu_of[fb] = gse
                        bank_free[abk]["s_se"] = gse
                    mm1_end = cnt["s_te"]
                    wt("tensor", w2_sw[k][0], w2_sw[k][1])
                    for sb in (2, 0, 1):
                        bks = (2 + 2 * sb, 3 + 2 * sb)
                        if phi == 0:
                            bank_guard(bks[0])
                            bank_guard(bks[1])
                        wt("tensor", "s_se", gelu_of[13])
                        for fb in range(NFB):
                            if fb >= 14:
                                wt("tensor", "s_se", gelu_of[fb])
                            for dch in range(2):
                                if sb == 2:
                                    outap = (lambda _d=dch: pkb[_d][:]
                                             .bitcast(F32)[:, 0:512])
                                else:
                                    outap = (lambda _d=dch, _b=bks:
                                             pk[_b[_d]][:, 0:512])
                                nsb = min(CAP, (sb + 1) * 128) - sb * 128
                                te(lambda e, S, _sl=ar2, _f=fb, _sb=sb, _d=dch,
                                   _o=outap, _phi=phi, _n=nsb: e.matmul(
                                    _o()[0:_n, :],
                                    H1V(_f)[:, _sb * 128:_sb * 128 + _n],
                                    w2v(_sl, _f, _d),
                                    start=(_phi == 0 and _f == 0), stop=False),
                                   inc=(phi == 0 and fb == NFB - 1 and dch == 1))
                        if phi == 1:
                            for dch in range(2):
                                if sb == 2:
                                    outap = (lambda _d=dch: pkb[_d][:]
                                             .bitcast(F32)[:, 0:512])
                                else:
                                    outap = (lambda _d=dch, _b=bks:
                                             pk[_b[_d]][:, 0:512])
                                nsb = min(CAP, (sb + 1) * 128) - sb * 128
                                te(lambda e, S, _e=e_, _d=dch, _o=outap, _n=nsb:
                                   e.matmul(
                                    _o()[0:_n, :],
                                    sel9[:, _e * 128:_e * 128 + _n],
                                    b2bo[:, _d * 512:(_d + 1) * 512],
                                    start=False, stop=True), inc=(dch == 1))
                            mm2_sb_te = cnt["s_te"]
                            wt("scalar", "s_te", mm2_sb_te)
                            for dch in range(2):
                                if sb == 2:
                                    inap = (lambda _d=dch: pkb[_d][:]
                                            .bitcast(F32)[:, 0:512])
                                else:
                                    inap = (lambda _d=dch, _b=bks:
                                            pk[_b[_d]][:, 0:512])
                                nsb = min(CAP, (sb + 1) * 128) - sb * 128
                                se(lambda e, S, _sb=sb, _d=dch, _i=inap, _n=nsb:
                                   e.activation(
                                    O2V(_sb)[0:_n, _d * 512:(_d + 1) * 512],
                                    _i()[0:_n, :], AF.Copy))
                            bank_read(bks[0], "s_se")
                            bank_read(bks[1], "s_se")
                    area_prev[ar1] = mm1_end
                    area_prev[ar2] = cnt["s_te"]
                    if k + 1 < 16:
                        kk = k + 1
                        w2_sw[kk] = emit_piece(
                            area_of(2 * kk + 1),
                            w2_dmas(area_of(2 * kk + 1), kk // 2, kk % 2, p > 1),
                            w2_wb(area_of(2 * kk + 1), kk // 2, kk % 2)
                            if p == 1 else None)
                    if k + 2 < 16:
                        kk = k + 2
                        w1_sw[kk] = emit_piece(
                            area_of(2 * kk),
                            w1_dmas(area_of(2 * kk), kk // 2, kk % 2, p > 0),
                            w1_wb(area_of(2 * kk), kk // 2, kk % 2)
                            if p == 0 else None)
                    if k == 15 and p + 1 < NPASS:
                        att_w_next[0] = emit_attw(
                            (p + 1) % 3, cached=(p + 1 > 1), wbk=(p + 1 == 1))
                o2_se = cnt["s_se"]
                wt("tensor", "s_se", o2_se)
                for t in range(8):
                    sl = (3, 0, 1, 2, 0, 1, 2, 0)[t]
                    bks = (2 * sl, 2 * sl + 1)
                    bank_guard(bks[0])
                    bank_guard(bks[1])
                    if sl == 3:
                        slotap = (lambda _d: (lambda: pkb[_d][:]
                                              .bitcast(F32)[:, 0:512]))
                    else:
                        slotap = (lambda _d, _b=bks: (lambda: pk[_b[_d]][:, 0:512]))
                    for sc in range(3):
                        nsc = min(CAP, (sc + 1) * 128) - sc * 128
                        for dch in range(2):
                            te(lambda e, S, _t=t, _sc=sc, _d=dch, _o=slotap(dch),
                               _n=nsc: e.matmul(
                                _o(),
                                PTV(_sc)[0:_n, _t * 128:(_t + 1) * 128],
                                O2V(_sc)[0:_n, _d * 512:(_d + 1) * 512],
                                start=(_sc == 0), stop=(_sc == 2)),
                               inc=(sc == 2 and dch == 1))
                    st_te = cnt["s_te"]
                    wt("vector", "s_te", st_te)
                    wt("vector", "s_xa", acc_init_sx)
                    for dch in range(2):
                        ve(lambda e, S, _t=t, _d=dch, _e=e_, _i=slotap(dch):
                           e.scalar_tensor_tensor(
                            acc[:, _t * D + _d * 512: _t * D + (_d + 1) * 512],
                            _i(),
                            Gs[:, _t * E + _e: _t * E + _e + 1],
                            acc[:, _t * D + _d * 512: _t * D + (_d + 1) * 512],
                            ALU.mult, ALU.add))
                    bank_read(bks[0], "s_ve")
                    bank_read(bks[1], "s_ve")
            if p <= 1:
                flush_wb(0)
            pass_ve_end = cnt["s_ve"]

            wt("sync", "s_ve", pass_ve_end)
            op("sync", lambda e, S, _p=p: e.dma_start(
                out=out_t[_p * PT:(_p + 1) * PT, :]
                    .rearrange("(tb q) d -> q tb d", q=128),
                in_=acc[:].rearrange("q (tb d) -> q tb d", tb=8)), "s_out", 16)

        wt("sync", "s_out", cnt["s_out"])

        def run_body(lst):
            def body(e):
                for f in lst:
                    f(e, S)
            return body
        block.sync(run_body(ops["sync"]))
        block.gpsimd(run_body(ops["gpsimd"]))
        block.tensor(run_body(ops["tensor"]))
        block.scalar(run_body(ops["scalar"]))
        block.vector(run_body(ops["vector"]))

    return nc


def host_inputs(inputs):
    f32 = np.float32
    bf16 = ml_dtypes.bfloat16
    xf = np.asarray(inputs["L0_attn_x"], f32)
    shared = {
        "w1T": np.ascontiguousarray(np.asarray(inputs["L0_ffn_w1"], f32).transpose(0, 2, 1)),
        "w2T": np.ascontiguousarray(np.asarray(inputs["L0_ffn_w2"], f32).transpose(0, 2, 1)),
        "wqT": np.ascontiguousarray(np.asarray(inputs["L0_attn_wq"], f32).T),
        "wkT": np.ascontiguousarray(np.asarray(inputs["L0_attn_wk"], f32).T),
        "wvT": np.ascontiguousarray(np.asarray(inputs["L0_attn_wv"], f32).T),
        "woT": np.ascontiguousarray(np.asarray(inputs["L0_attn_wo"], f32).T),
        "rwT": np.ascontiguousarray(np.asarray(inputs["L0_ffn_router_w"], f32).T),
        "b1T": np.ascontiguousarray(
            np.asarray(inputs["L0_ffn_b1"], f32).reshape(E, 32, 128).transpose(0, 2, 1)),
        "b2i": np.ascontiguousarray(np.asarray(inputs["L0_ffn_b2"], f32)),
        "boi": np.ascontiguousarray(np.asarray(inputs["L0_attn_bo"], f32)[None, :]),
        "cw1t": np.ascontiguousarray(
            np.repeat(np.asarray(inputs["L0_attn_cw1"], f32)[:, :, 0], WIN, axis=0)),
        "cb1t": np.ascontiguousarray(
            np.repeat(np.asarray(inputs["L0_attn_cb1"], f32), WIN, axis=0)),
        "cw2f": np.ascontiguousarray(
            np.repeat(np.asarray(inputs["L0_attn_cw2"], f32)
                      .transpose(0, 2, 1).reshape(H, 1024), WIN, axis=0)),
        "cb2t": np.ascontiguousarray(
            np.repeat(np.asarray(inputs["L0_attn_cb2"], f32), WIN, axis=0)),
        "cw3s": np.ascontiguousarray(
            np.repeat(np.asarray(inputs["L0_attn_cw3"], f32)[:, 0, :], WIN, axis=0)),
        "cb3s": np.ascontiguousarray(
            np.repeat(np.asarray(inputs["L0_attn_cb3"], f32), WIN, axis=0)),
        "post": np.ascontiguousarray(
            np.tile(np.linspace(0.0, 1.0, WIN, dtype=f32), H)[:, None]),
        "identb": np.eye(128, dtype=bf16),
        "identf": np.eye(128, dtype=f32),
        "onescol_b": np.ones((128, 1), dtype=bf16),
        "onesrow_f": np.ones((1, 128), dtype=f32),
        "iotaM_i": np.ascontiguousarray(
            np.tile(np.arange(1, CAP + 1, dtype=f32)[None, :], (128, 1))),
        "ltri_i": np.triu(np.ones((128, 128))).astype(bf16),
        "ones128_i": np.ones((128, 128), dtype=bf16),
        "sel9_i": np.ascontiguousarray(np.kron(np.eye(9), np.ones((1, 128))).astype(bf16)),
    }
    r = np.arange(128)[:, None]
    j = np.arange(128)[None, :]
    D1m = np.stack([(r - j == w).astype(f32) for w in range(WIN)])
    r2 = np.arange(6)[:, None]
    D2m = np.stack([(128 + r2 - j == w).astype(f32) for w in range(WIN)])
    shared["D1"] = np.ascontiguousarray(D1m)
    shared["D2"] = np.ascontiguousarray(D2m)

    in_maps = []
    for c in range(8):
        b, s = c // 2, c % 2
        xc = np.zeros((XROWS, D), f32)
        lo = s * 4096 - 3
        hi = s * 4096 + 4096 + 3
        vlo, vhi = max(lo, 0), min(hi, 8192)
        xc[vlo - lo: vhi - lo] = xf[b, vlo:vhi]
        m = dict(shared)
        m["x"] = xc
        in_maps.append(m)
    return in_maps


def kernel(**inputs):
    in_maps = host_inputs(inputs)
    nc = build_program()
    res = run_bass_kernel_spmd(nc, in_maps, core_ids=list(range(8)))
    out = np.empty((4, 8192, D), np.float32)
    for c in range(8):
        b, s = c // 2, c % 2
        out[b, s * 4096:(s + 1) * 4096] = np.asarray(res.results[c]["out"])
    return out


# version 17
# speedup vs baseline: 1.1823x; 1.1823x over previous
import numpy as np
import ml_dtypes
import concourse.bass as bass
import concourse.mybir as mybir
from concourse.bass_utils import run_bass_kernel_spmd
from contextlib import ExitStack

F32 = mybir.dt.float32
BF16 = mybir.dt.bfloat16
AF = mybir.ActivationFunctionType
ALU = mybir.AluOpType

D = 1024
H = 16
DH = 64
WIN = 7
NPASS = 4
PT = 1024
PTH = PT + 6
CORE_T = NPASS * PT
ACH = 128
NACH = PT // ACH
E = 8
FF = 4096
FH = 2048
TH = 512
NFB = 16
XROWS = CORE_T + 6 + 122
CAP = 352

ENGS = ("sync", "gpsimd", "tensor", "scalar", "vector")


def build_program():
    nc = bass.Bass("TRN2", target_bir_lowering=False, debug=False)

    def din(name, shape, dt=F32):
        return nc.dram_tensor(name, shape, dt, kind="ExternalInput").ap()

    x = din("x", [XROWS, D])
    w1T = din("w1T", [E, D, FF])
    w2T = din("w2T", [E, FF, D])
    wqT = din("wqT", [D, D])
    wkT = din("wkT", [D, D])
    wvT = din("wvT", [D, D])
    woT = din("woT", [D, D])
    rwT = din("rwT", [D, E])
    b1T = din("b1T", [E, 128, 32])
    b2i = din("b2i", [E, D])
    boi = din("boi", [1, D])
    cw1t = din("cw1t", [112, 32])
    cb1t = din("cb1t", [112, 32])
    cw2f = din("cw2f", [112, 1024])
    cb2t = din("cb2t", [112, 32])
    cw3s = din("cw3s", [112, 32])
    cb3s = din("cb3s", [112, 1])
    post = din("post", [112, 1])
    D1 = din("D1", [WIN, 128, 128])
    D2 = din("D2", [WIN, 6, 128])
    identb = din("identb", [128, 128], BF16)
    identf = din("identf", [128, 128])
    onescol_b = din("onescol_b", [128, 1], BF16)
    onesrow_f = din("onesrow_f", [1, 128])
    iotaM_i = din("iotaM_i", [128, CAP])
    ltri_i = din("ltri_i", [128, 128], BF16)
    sel9_i = din("sel9_i", [9, 9 * 128], BF16)
    ones128_i = din("ones128_i", [128, 128], BF16)
    ones_b_row_src = onesrow_f

    out_t = nc.dram_tensor("out", [CORE_T, D], F32, kind="ExternalOutput").ap()
    cwt_d = nc.dram_tensor("cwt_d", [112, 1], F32, kind="Internal").ap()
    w1d = nc.dram_tensor("w1d", [16, 128, 16384], BF16, kind="Internal").ap()
    w2d = nc.dram_tensor("w2d", [16, 128, 16384], BF16, kind="Internal").ap()
    attd = nc.dram_tensor("attd", [4, 128, 8192], BF16, kind="Internal").ap()

    ops = {e: [] for e in ENGS}
    cnt = {s: 0 for s in ("s_xc", "s_xs0", "s_xs1", "s_xs2", "s_xs3", "s_xa", "s_xb", "s_out",
                          "s_wc", "s_wa0", "s_wa1", "s_wa2", "s_wt", "s_wt2", "s_wv",
                          "s_wd0", "s_wd1", "s_wd2",
                          "s_te", "s_se", "s_ve")}
    hiw = {e: {s: 0 for s in cnt} for e in ENGS}

    def wt(eng, sem, val):
        if val is None or val <= 0 or val <= hiw[eng][sem]:
            return
        hiw[eng][sem] = val
        def _w(e, S, _s=sem, _v=val):
            e.wait_ge(S[_s], _v)
        ops[eng].append(_w)

    def op(eng, fn, inc_sem=None, inc_by=1):
        if inc_sem is not None:
            cnt[inc_sem] += inc_by
        def _f(e, S, _fn=fn, _i=inc_sem, _b=inc_by):
            ins = _fn(e, S)
            if _i is not None:
                ins.then_inc(S[_i], _b)
        ops[eng].append(_f)
        return cnt[inc_sem] if inc_sem else None

    def ve(fn):
        wt("vector", "s_ve", cnt["s_ve"])
        return op("vector", fn, "s_ve")

    def se(fn):
        wt("scalar", "s_se", cnt["s_se"])
        return op("scalar", fn, "s_se")

    def te(fn, inc=True):
        return op("tensor", fn, "s_te" if inc else None)

    bank_free = {b: {"s_se": 0, "s_ve": 0} for b in range(8)}

    def bank_guard(b):
        wt("tensor", "s_se", bank_free[b]["s_se"])
        wt("tensor", "s_ve", bank_free[b]["s_ve"])

    def bank_read(b, sem):
        bank_free[b][sem] = cnt[sem]

    with ExitStack() as ctx:
        En = ctx.enter_context
        acc = En(nc.sbuf_tensor("acc", [128, 8 * D], F32))
        slot0 = En(nc.sbuf_tensor("slot0", [128, 32768], BF16))
        slot1 = En(nc.sbuf_tensor("slot1", [128, 16384], BF16))
        scr = En(nc.sbuf_tensor("scr", [128, 16480], BF16))
        b1s = En(nc.sbuf_tensor("b1s", [128, E * 32], F32))
        b2bo = En(nc.sbuf_tensor("b2bo", [9, D], BF16))
        sel9 = En(nc.sbuf_tensor("sel9", [9, 9 * 128], BF16))
        rwb = En(nc.sbuf_tensor("rwb", [128, 8 * E], BF16))
        idb = En(nc.sbuf_tensor("idb", [128, 128], BF16))
        ones_c = En(nc.sbuf_tensor("ones_c", [128, 1], BF16))
        cMall = En(nc.sbuf_tensor("cMall", [128, H * 256], BF16))
        c_row_t = En(nc.sbuf_tensor("c_row_t", [1, H * WIN], F32))
        Gs = En(nc.sbuf_tensor("Gs", [128, 8 * E], F32))
        m8 = En(nc.sbuf_tensor("m8", [128, 64], F32))
        poss = m8
        g2c = En(nc.sbuf_tensor("g2c", [128, 8], F32))
        g1c = En(nc.sbuf_tensor("g1c", [128, 8], F32))
        gtmp = En(nc.sbuf_tensor("gtmp", [128, 64], F32))
        bnst = En(nc.sbuf_tensor("bnst", [128, 36], F32))
        bnagg = En(nc.sbuf_tensor("bnagg", [128, 6], F32))
        rstd = En(nc.sbuf_tensor("rstd", [128, 3], F32))
        nmr = En(nc.sbuf_tensor("nmr", [128, 3], F32))
        lnt = En(nc.sbuf_tensor("lnt", [128, 3], F32))
        rden = En(nc.sbuf_tensor("rden", [128, 1], F32))
        xs_t = [En(nc.sbuf_tensor(f"xs{i}", [128, D], F32)) for i in range(3)]
        xnb_t = [En(nc.sbuf_tensor(f"xnb{i}", [128, D], BF16)) for i in range(8)]
        xnbH = En(nc.sbuf_tensor("xnbH", [6, D], BF16))
        iotaM = En(nc.sbuf_tensor("iotaM", [128, CAP], F32))
        ltri = En(nc.sbuf_tensor("ltri", [128, 128], BF16))
        ones128 = En(nc.sbuf_tensor("ones128", [128, 128], BF16))
        maskf = En(nc.sbuf_tensor("maskf", [128, 64], F32))
        gtm2 = maskf
        maskb = En(nc.sbuf_tensor("maskb", [128, 64], BF16))

        pk = [En(nc.psum_tensor(f"pk{i}", [128, 512], F32)) for i in range(6)]
        pkb = [En(nc.psum_tensor(f"pkb{i}", [128, 1024], BF16)) for i in range(2)]

        S = {s: En(nc.semaphore(s)) for s in cnt}
        block = En(nc.Block())

        def scr_f32(col, n):
            return scr[:, col:col + 2 * n].bitcast(F32)
        XS = [lambda i=i: xs_t[i][:] for i in range(3)]
        xnT = None
        def _cw(col, ncols_f32, np_=H):
            return lambda: scr[0:np_, col:col + 2 * ncols_f32].bitcast(F32)
        c_cw1 = _cw(0, 32, 112)
        c_cb1 = _cw(64, 32, 112)
        c_cw2 = _cw(128, 1024, 112)
        c_cb2 = _cw(2176, 32, 112)
        c_cw3 = _cw(2240, 32, 112)
        c_cb3 = _cw(2304, 8, 112)
        c_pos = _cw(2320, 8, 112)
        c_h1 = _cw(2336, 32, 112)
        c_h2 = _cw(2400, 32, 112)
        c_cwt = _cw(2464, 8, 112)
        c_bn = _cw(2480, 8, 112)
        XNB = [lambda i=i: xnb_t[i][:] for i in range(8)]
        D1V = lambda: scr_f32(5480, 896)
        D2V = lambda: scr[0:6, 0:1792].bitcast(F32)
        xnT = scr[:, 0:8240]
        A0 = 8240
        QTV = lambda: scr[:, A0:A0 + 1024]
        KTV = lambda: scr[:, A0 + 1024:A0 + 1024 + 8 * 134]
        V3 = [lambda k=k: scr[:, A0 + 2096 + 1024 * k:A0 + 2096 + 1024 * (k + 1)]
              for k in range(3)]
        E128 = scr[:, A0:A0 + 224].bitcast(F32)
        ones_rf = scr[0:1, A0 + 224:A0 + 480].bitcast(F32)
        AHB = lambda: scr[:, A0 + 5168:A0 + 6192]
        AHT = lambda: scr[:, A0 + 6192:A0 + 7216]
        EWF = [lambda k=k: scr[:, A0 + 7216 + 256 * k:A0 + 7472 + 256 * k] for k in range(4)]
        EW1 = [lambda k=k: scr[:, A0 + 7216 + 256 * k:A0 + 7344 + 256 * k] for k in range(4)]
        EW2 = [lambda k=k: scr[0:6, A0 + 7344 + 256 * k:A0 + 7472 + 256 * k] for k in range(4)]
        H1V = lambda fb: scr[:, fb * CAP:(fb + 1) * CAP]
        XGV = lambda hb: scr[:, 6144 + hb * CAP:6144 + (hb + 1) * CAP]
        PV = lambda t: scr[:, 9216 + t * CAP:9216 + (t + 1) * CAP]
        PTV = lambda sc: scr[:, 12288 + sc * PT:12288 + (sc + 1) * PT]
        O2V = lambda sb: scr[:, 6144 + sb * D:6144 + (sb + 1) * D]

        AREAS = [(slot0, 0), (slot0, 16384), (slot1, 0)]

        def w1v(ar, hb, fb):
            t, off = AREAS[ar]
            return t[:, off + hb * 2048 + fb * 128: off + hb * 2048 + (fb + 1) * 128]

        def w2v(ar, fb, dch):
            t, off = AREAS[ar]
            return t[:, off + fb * 1024 + dch * 512: off + fb * 1024 + (dch + 1) * 512]

        def attwv(wb, which, hb, c0, n):
            ar = (wb + 1) % 3 if which < 2 else (wb + 2) % 3
            t, off = AREAS[ar]
            base = off + (which % 2) * 8192 + hb * 1024
            return t[:, base + c0: base + c0 + n]

        cdmas = [
            (lambda: b1s[:].rearrange("p (e c) -> p e c", e=E), b1T.transpose([1, 0, 2])),
            (lambda: idb[:], identb),
            (lambda: ones_c[:], onescol_b),
            (lambda: ones_rf[:], onesrow_f),
            (lambda: c_cw1(), cw1t), (lambda: c_cb1(), cb1t),
            (lambda: c_cw2(), cw2f), (lambda: c_cb2(), cb2t),
            (lambda: c_cw3(), cw3s), (lambda: c_cb3()[:, 0:1], cb3s),
            (lambda: c_pos()[:, 0:1], post),
            (lambda: D1V().rearrange("p (w j) -> p w j", w=WIN), D1.rearrange("w r j -> r w j")),
            (lambda: iotaM[:], iotaM_i),
            (lambda: ltri[:], ltri_i),
            (lambda: ones128[:], ones128_i),
        ]
        for dstf, src in cdmas:
            op("sync", lambda e, S, _d=dstf, _s=src: e.dma_start(out=_d(), in_=_s),
               "s_xc", 16)
        n_const_sx = cnt["s_xc"]

        for dstf, src in ((lambda: rwb[:].rearrange("p (hb e) -> p hb e", hb=8), rwT.rearrange("(hb p) e -> p hb e", p=128)),
                          (lambda: b2bo[0:8, :], b2i), (lambda: b2bo[8:9, :], boi),
                          (lambda: sel9[:], sel9_i)):
            op("gpsimd", lambda e, S, _d=dstf, _s=src: e.dma_start(out=_d(), in_=_s),
               "s_wc", 16)
        n_const_sw = cnt["s_wc"]

        area_prev = [0, 0, 0]
        area_sem = ["s_wa0", "s_wa1", "s_wa2"]
        wd_sem = ["s_wd0", "s_wd1", "s_wd2"]
        area_wfree = [0, 0, 0]
        pend_wb = []

        def flush_wb(n_keep=0):
            while len(pend_wb) > n_keep:
                ar, lsem, lval, dsts = pend_wb.pop(0)
                wt("gpsimd", lsem, lval)
                for dst, view in dsts:
                    op("gpsimd", lambda e, S, _d=dst, _v=view: e.dma_start(
                        out=_d, in_=_v()), wd_sem[ar], 16)
                area_wfree[ar] = cnt[wd_sem[ar]]

        def emit_piece(ar, dmas, wb_dsts=None):
            flush_wb(0)
            wt("gpsimd", "s_te", area_prev[ar])
            wt("gpsimd", wd_sem[ar], area_wfree[ar])
            sem = area_sem[ar]
            for dstf, srcf in dmas:
                op("gpsimd", lambda e, S, _d=dstf, _s=srcf: e.dma_start(out=_d(), in_=_s),
                   sem, 16)
            if wb_dsts is not None:
                pend_wb.append((ar, sem, cnt[sem], wb_dsts))
            return (sem, cnt[sem])

        def w1_dmas(ar, e_, phi, cached):
            t, off = AREAS[ar]
            if cached:
                return [(lambda _t=t, _o=off: _t[:, _o:_o + 16384],
                         w1d[2 * e_ + phi])]
            return [(lambda _t=t, _o=off: _t[:, _o:_o + 16384]
                     .rearrange("p (hb f) -> p hb f", hb=8),
                     w1T[e_, :, phi * FH:(phi + 1) * FH]
                     .rearrange("(hb p) f -> p hb f", p=128))]

        def w2_dmas(ar, e_, phi, cached):
            t, off = AREAS[ar]
            if cached:
                return [(lambda _t=t, _o=off: _t[:, _o:_o + 16384],
                         w2d[2 * e_ + phi])]
            res = []
            for half in range(2):
                res.append((lambda _t=t, _o=off, _h=half: _t[:, _o + _h * 8192:
                                                             _o + (_h + 1) * 8192]
                            .rearrange("p (fb d) -> p fb d", fb=8),
                            w2T[e_, phi * FH + half * 1024: phi * FH + (half + 1) * 1024, :]
                            .rearrange("(fb p) d -> p fb d", p=128)))
            return res

        def w1_wb(ar, e_, phi):
            t, off = AREAS[ar]
            return [(w1d[2 * e_ + phi], lambda _t=t, _o=off: _t[:, _o:_o + 16384])]

        def w2_wb(ar, e_, phi):
            t, off = AREAS[ar]
            return [(w2d[2 * e_ + phi], lambda _t=t, _o=off: _t[:, _o:_o + 16384])]

        def emit_attw(wb, cached, wbk):
            flush_wb(0)
            ar1, ar2 = (wb + 1) % 3, (wb + 2) % 3
            wt("gpsimd", "s_te", area_prev[ar1])
            wt("gpsimd", wd_sem[ar1], area_wfree[ar1])
            for wi, wsrc in ((0, wqT), (1, wkT)):
                t, off = AREAS[ar1]
                if cached:
                    op("gpsimd", lambda e, S, _t=t, _o=off + wi * 8192, _w=wi:
                       e.dma_start(out=_t[:, _o:_o + 8192], in_=attd[_w]), "s_wt", 16)
                else:
                    op("gpsimd", lambda e, S, _t=t, _o=off + wi * 8192, _s=wsrc:
                       e.dma_start(
                        out=_t[:, _o:_o + 8192].rearrange("p (hb d) -> p hb d", hb=8),
                        in_=_s.rearrange("(hb p) d -> p hb d", p=128)), "s_wt", 16)
            if wbk:
                t, off = AREAS[ar1]
                pend_wb.append((ar1, "s_wt", cnt["s_wt"],
                                [(attd[0], lambda _t=t, _o=off: _t[:, _o:_o + 8192]),
                                 (attd[1], lambda _t=t, _o=off: _t[:, _o + 8192:_o + 16384])]))
            wt("gpsimd", "s_te", area_prev[ar2])
            wt("gpsimd", wd_sem[ar2], area_wfree[ar2])
            for wi, wsrc, sem in ((0, wvT, "s_wv"), (1, woT, "s_wt2")):
                t, off = AREAS[ar2]
                if cached:
                    op("gpsimd", lambda e, S, _t=t, _o=off + wi * 8192, _w=2 + wi:
                       e.dma_start(out=_t[:, _o:_o + 8192], in_=attd[_w]), sem, 16)
                else:
                    op("gpsimd", lambda e, S, _t=t, _o=off + wi * 8192, _s=wsrc:
                       e.dma_start(
                        out=_t[:, _o:_o + 8192].rearrange("p (hb d) -> p hb d", hb=8),
                        in_=_s.rearrange("(hb p) d -> p hb d", p=128)), sem, 16)
                if wbk:
                    pend_wb.append((ar2, sem, cnt[sem],
                                    [(attd[2 + wi], lambda _t=t, _o=off + wi * 8192:
                                      _t[:, _o:_o + 8192])]))
            return (cnt["s_wt"], cnt["s_wv"], cnt["s_wt2"])

        wt("vector", "s_xc", n_const_sx)
        wt("vector", "s_wc", n_const_sw)

        def r3(t, wdim):
            return t().rearrange("p (w j) -> p w j", w=wdim)

        ve(lambda e, S: e.scalar_tensor_tensor(
            c_h1(), c_cw1(), c_pos()[:, 0:1], c_cb1(), ALU.mult, ALU.add))
        wt("scalar", "s_ve", cnt["s_ve"])
        se(lambda e, S: e.activation(c_h1(), c_h1(), AF.Gelu))
        wt("vector", "s_se", cnt["s_se"])
        for i in range(32):
            blk = (lambda _i=i: c_cw2()[:, _i * 32:(_i + 1) * 32])
            col = (lambda _i=i: c_h1()[:, _i:_i + 1])
            if i == 0:
                ve(lambda e, S, _b=blk, _c=col: e.tensor_scalar(
                    c_h2(), _b(), _c(), None, ALU.mult))
            else:
                ve(lambda e, S, _b=blk, _c=col: e.scalar_tensor_tensor(
                    c_h2(), _b(), _c(), c_h2(), ALU.mult, ALU.add))
        ve(lambda e, S: e.tensor_tensor(c_h2(), c_h2(), c_cb2(), ALU.add))
        wt("scalar", "s_ve", cnt["s_ve"])
        se(lambda e, S: e.activation(c_h2(), c_h2(), AF.Gelu))
        wt("vector", "s_se", cnt["s_se"])
        ve(lambda e, S: e.tensor_tensor(c_h2(), c_h2(), c_cw3(), ALU.mult))
        ve(lambda e, S: e.bn_stats(c_bn()[:, 0:6], c_h2()))
        ve(lambda e, S: e.bn_aggr(c_bn()[:, 6:8], c_bn()[:, 0:6]))
        cwt_ve = ve(lambda e, S: e.scalar_tensor_tensor(
            c_cwt()[:, 0:1], c_bn()[:, 6:7], 32.0, c_cb3()[:, 0:1],
            ALU.mult, ALU.add))

        wt("sync", "s_ve", cwt_ve)
        op("sync", lambda e, S: e.dma_start(out=cwt_d, in_=c_cwt()[:, 0:1]), "s_xb", 16)
        op("sync", lambda e, S: e.dma_start(
            out=D2V().rearrange("p (w j) -> p w j", w=WIN),
            in_=D2.rearrange("w r j -> r w j")), "s_xb", 16)
        wt("sync", "s_xb", cnt["s_xb"])
        op("sync", lambda e, S: e.dma_start(
            out=c_row_t[:], in_=cwt_d.rearrange("h w -> (h w)").unsqueeze(0)), "s_xb", 16)
        wt("scalar", "s_xb", cnt["s_xb"])
        exp_se = se(lambda e, S: e.activation(c_row_t[:], c_row_t[:], AF.Exp))
        wt("tensor", "s_se", exp_se)
        wt("tensor", "s_xc", n_const_sx)
        bank_guard(5)
        te(lambda e, S: e.matmul(pk[5][:, 0:H * WIN], ones_rf[:], c_row_t[:],
                                 start=True, stop=True))
        wt("vector", "s_te", cnt["s_te"])
        wt("vector", "s_xb", cnt["s_xb"])
        ve(lambda e, S: e.tensor_copy(E128[:], pk[5][:, 0:H * WIN]))
        bank_read(5, "s_ve")
        ve(lambda e, S: e.memset(cMall[:], 0.0))
        for _b in range(6):
            ve(lambda e, S, _bb=_b: e.memset(pk[_bb][:], 0.0))
        psum_init_ve = cnt["s_ve"]
        wt("tensor", "s_ve", psum_init_ve)
        for h in range(H):
            for w in range(WIN):
                d2w = (lambda _w=w: D2V()[:, _w * 128:(_w + 1) * 128])
                sc6 = (lambda _h=h, _w=w: E128[0:6, _h * WIN + _w:_h * WIN + _w + 1])
                o2 = (lambda _h=h: cMall[0:6, _h * 256 + 128:(_h + 1) * 256])
                if w == 0:
                    ve(lambda e, S, _d=d2w, _s=sc6, _o=o2: e.tensor_scalar(
                        _o(), _d(), _s(), None, ALU.mult))
                else:
                    ve(lambda e, S, _d=d2w, _s=sc6, _o=o2: e.scalar_tensor_tensor(
                        _o(), _d(), _s(), _o(), ALU.mult, ALU.add))
        cM2_ve = cnt["s_ve"]
        for h in range(H):
            for w in range(WIN):
                d1w = (lambda _w=w: D1V()[:, _w * 128:(_w + 1) * 128])
                sc = (lambda _h=h, _w=w: E128[:, _h * WIN + _w:_h * WIN + _w + 1])
                o1 = (lambda _h=h: cMall[:, _h * 256:_h * 256 + 128])
                if w == 0:
                    ve(lambda e, S, _d=d1w, _s=sc, _o=o1: e.tensor_scalar(
                        _o(), _d(), _s(), None, ALU.mult))
                else:
                    ve(lambda e, S, _d=d1w, _s=sc, _o=o1: e.scalar_tensor_tensor(
                        _o(), _d(), _s(), _o(), ALU.mult, ALU.add))
        connM_ve = cnt["s_ve"]

        xs_free = {}
        qkv_last_te = 0
        v3_free_te = {}
        ew_free_te = [0, 0, 0, 0]
        ah_t_te_prev = [0]
        xnb_pass_free = [0]
        aht_free_te = 0
        pass_ve_end = 0
        pre_x = {}

        att_w_next = [None]
        xdma_of = {}
        ln_affine = {}
        ln_evac = {}
        ln_tp_te = {}
        for p in range(NPASS):

            def emit_x(q, _p=p):
                g = _p * 9 + q
                buf = g % 3
                xsem = f"s_xs{buf}"
                if (_p, q) in pre_x:
                    return pre_x[(_p, q)]
                if g >= 3:
                    vefree, sefree = xs_free[g - 3]
                    wt("sync", "s_ve", vefree)
                    wt("sync", "s_se", sefree)
                if q == 0:
                    op("sync", lambda e, S, _pp=_p, _b=buf: e.dma_start(
                        out=XS[_b]()[0:3, :],
                        in_=x[_pp * PT: _pp * PT + 3, :]), xsem, 16)
                    op("sync", lambda e, S, _pp=_p, _b=buf: e.dma_start(
                        out=XS[_b]()[3:6, :],
                        in_=x[_pp * PT + 1027: _pp * PT + 1030, :]), xsem, 16)
                else:
                    c = q - 1
                    op("sync", lambda e, S, _pp=_p, _c=c, _b=buf: e.dma_start(
                        out=XS[_b](),
                        in_=x[_pp * PT + 3 + _c * 128: _pp * PT + 3 + (_c + 1) * 128, :]),
                        xsem, 16)
                return cnt[xsem]

            def emit_acc_init(_p=p):
                wt("sync", "s_out", 16 * _p)
                op("sync", lambda e, S, _pp=_p: e.dma_start(
                    out=acc[:].rearrange("q (tb d) -> q tb d", tb=8),
                    in_=x[3 + _pp * PT: 3 + (_pp + 1) * PT, :]
                        .rearrange("(tb q) d -> q tb d", q=128)), "s_xa", 16)
                return cnt["s_xa"]

            for c in range(3):
                if (p, c) not in xdma_of:
                    xdma_of[(p, c)] = emit_x(c)

            def emit_ln_front(q, _p=p):
                g = _p * 9 + q
                buf = g % 3
                np_ = 6 if q == 0 else 128
                xsem = f"s_xs{buf}"
                wt("vector", xsem, xdma_of[(_p, q)])
                B = buf
                ve(lambda e, S, _b=buf, _B=B, _n=np_: e.bn_stats(
                    bnst[0:_n, _B * 12:_B * 12 + 6], XS[_b]()[0:_n, 0:512]))
                ve(lambda e, S, _b=buf, _B=B, _n=np_: e.bn_stats(
                    bnst[0:_n, _B * 12 + 6:_B * 12 + 12], XS[_b]()[0:_n, 512:1024]))
                stats_ve = ve(lambda e, S, _B=B, _n=np_: e.bn_aggr(
                    bnagg[0:_n, _B * 2:_B * 2 + 2], bnst[0:_n, _B * 12:_B * 12 + 12]))
                veps = ve(lambda e, S, _B=B, _n=np_: e.tensor_scalar_add(
                    lnt[0:_n, _B:_B + 1], bnagg[0:_n, _B * 2 + 1:_B * 2 + 2], 1e-5))
                wt("scalar", "s_ve", veps)
                se(lambda e, S, _B=B, _n=np_: e.activation(
                    lnt[0:_n, _B:_B + 1], lnt[0:_n, _B:_B + 1], AF.Ln))
                rstd_se = se(lambda e, S, _B=B, _n=np_: e.activation(
                    rstd[0:_n, _B:_B + 1], lnt[0:_n, _B:_B + 1], AF.Exp, scale=-0.5))
                wt("vector", "s_se", rstd_se)
                nmr_ve = ve(lambda e, S, _B=B, _n=np_: e.scalar_tensor_tensor(
                    nmr[0:_n, _B:_B + 1], bnagg[0:_n, _B * 2:_B * 2 + 1], -1.0,
                    rstd[0:_n, _B:_B + 1], ALU.mult, ALU.mult))
                wt("scalar", "s_ve", nmr_ve)
                wt("scalar", "s_te", xnb_pass_free[0])
                xnbv = (lambda: xnbH[:]) if q == 0 else XNB[q - 1]
                aff_se = se(lambda e, S, _b=buf, _B=B, _n=np_, _xv=xnbv: e.activation(
                    _xv(), XS[_b]()[0:_n, :], AF.Identity,
                    bias=nmr[0:_n, _B:_B + 1], scale=rstd[0:_n, _B:_B + 1]))
                ln_affine[(_p, q)] = aff_se
                xs_free[g] = (stats_ve, aff_se)

            def emit_ln_back(q, _p=p):
                g = _p * 9 + q
                if q + 3 <= 8:
                    xdma_of[(_p, q + 3)] = emit_x(q + 3)
                pb = q % 2
                np_ = 6 if q == 0 else 128
                xnbv = (lambda: xnbH[:]) if q == 0 else XNB[q - 1]
                wt("tensor", "s_se", ln_affine[(_p, q)])
                wt("tensor", "s_se", bank_free[6 + pb]["s_se"])
                wt("tensor", "s_ve", bank_free[6 + pb]["s_ve"])
                for db in range(8):
                    te(lambda e, S, _db=db, _pb=pb, _n=np_, _xv=xnbv: e.transpose(
                        pkb[_pb][:, _db * 8 if _n == 6 else _db * 128:
                                 (_db * 8 + 6) if _n == 6 else (_db + 1) * 128],
                        _xv()[0:_n, _db * 128:(_db + 1) * 128], idb[0:_n, 0:_n]),
                        inc=(db == 7))
                ln_tp_te[(_p, q)] = cnt["s_te"]
                wt("vector", "s_te", cnt["s_te"])
                if q == 0:
                    ev_ve = ve(lambda e, S: e.tensor_copy(
                        xnT.rearrange("r (db t) -> r db t", db=8)[:, :, 0:3],
                        pkb[0][:, 0:64].rearrange("r (db t) -> r db t", t=8)[:, :, 0:3]))
                    ev_ve = ve(lambda e, S: e.tensor_copy(
                        xnT.rearrange("r (db t) -> r db t", db=8)[:, :, 1027:1030],
                        pkb[0][:, 0:64].rearrange("r (db t) -> r db t", t=8)[:, :, 3:6]))
                else:
                    c = q - 1
                    ev_ve = ve(lambda e, S, _c=c, _pb=pb: e.tensor_copy(
                        xnT.rearrange("r (db t) -> r db t", db=8)
                        [:, :, 3 + _c * 128:3 + _c * 128 + 128],
                        pkb[_pb][:].rearrange("r (db t) -> r db t", db=8)))
                ln_evac[(_p, q)] = ("s_ve", ev_ve)
                bank_free[6 + pb]["s_ve"] = ev_ve

            wb = p % 3
            area_of = lambda i, _wb=wb: (_wb + i) % 3
            if p >= 1:
                for a3 in range(3):
                    wt("gpsimd", wd_sem[a3], area_wfree[a3])
            if att_w_next[0] is not None:
                att_w_sw = att_w_next[0]
                att_w_next[0] = None
            else:
                att_w_sw = emit_attw(wb, cached=(p > 1), wbk=(p == 1))
            sl_att = wb
            w1_sw = {}
            w2_sw = {}
            w1_sw[0] = emit_piece(area_of(0), w1_dmas(area_of(0), 0, 0, p > 0),
                                  w1_wb(area_of(0), 0, 0) if p == 0 else None)

            if p == 0:
                for q0_ in (0, 1, 2):
                    emit_ln_front(q0_)
                emit_ln_back(0)
                emit_ln_back(1)
                emit_ln_front(3)
                emit_ln_back(2)
            else:
                for q0_ in (6, 7, 8):
                    emit_ln_front(q0_)
                for qb_ in (0, 1, 2, 3):
                    emit_ln_back(qb_)
            acc_init_sx = emit_acc_init()

            wt("tensor", "s_wt", att_w_sw[0])
            for ach in range(NACH):
                if ach + 4 <= 8 and (p, ach + 4) not in ln_affine:
                    emit_ln_front(ach + 4)
                nb_ = ach + (3 if p == 0 else 4)
                if nb_ <= 8 and (p, nb_) not in ln_evac:
                    emit_ln_back(nb_)
                qq = min(ach + 2, 8)
                wt("tensor", ln_evac[(p, qq)][0], ln_evac[(p, qq)][1])
                if p == 0 and ach == 0:
                    wt("scalar", "s_ve", connM_ve)
                for dqb in range(8):
                    bk = dqb % 4
                    bank_guard(bk)
                    for hb in range(8):
                        te(lambda e, S, _q=dqb, _h=hb, _a=ach, _sl=sl_att, _bk=bk:
                           e.matmul(pk[_bk][:, 0:ACH],
                                    attwv(_sl, 0, _h, _q * 128, 128),
                                    xnT[:, _h * PTH + 3 + _a * ACH:
                                        _h * PTH + 3 + (_a + 1) * ACH],
                                    start=(_h == 0), stop=(_h == 7)), inc=(hb == 7))
                    wt("scalar", "s_te", cnt["s_te"])
                    ev = se(lambda e, S, _q=dqb, _bk=bk: e.copy(
                        QTV()[:, _q * ACH:(_q + 1) * ACH], pk[_bk][:, 0:ACH]))
                    bank_free[bk]["s_se"] = ev
                for dqb in range(8):
                    bk = dqb % 4
                    bank_guard(bk)
                    for hb in range(8):
                        te(lambda e, S, _q=dqb, _h=hb, _a=ach, _sl=sl_att, _bk=bk:
                           e.matmul(pk[_bk][:, 0:134],
                                    attwv(_sl, 1, _h, _q * 128, 128),
                                    xnT[:, _h * PTH + _a * ACH:
                                        _h * PTH + _a * ACH + 134],
                                    start=(_h == 0), stop=(_h == 7)), inc=(hb == 7))
                    wt("scalar", "s_te", cnt["s_te"])
                    ev = se(lambda e, S, _q=dqb, _bk=bk: e.copy(
                        KTV()[:, _q * 134:(_q + 1) * 134], pk[_bk][:, 0:134]))
                    bank_free[bk]["s_se"] = ev
                newtiles = (0, 1) if ach == 0 else (ach + 1,)
                wt("tensor", "s_wv", att_w_sw[1])
                for gv in newtiles:
                    vb = gv % 3
                    nr = 6 if gv == 8 else 128
                    for dch in range(2):
                        bk = dch
                        bank_guard(bk)
                        for hb in range(8):
                            te(lambda e, S, _g=gv, _h=hb, _d=dch, _sl=sl_att,
                               _bk=bk, _nr=nr:
                               e.matmul(pk[_bk][0:_nr, 0:512],
                                        xnT[:, _h * PTH + _g * 128:
                                            _h * PTH + _g * 128 + _nr],
                                        attwv(_sl, 2, _h, _d * 512, 512),
                                        start=(_h == 0), stop=(_h == 7)),
                               inc=(hb == 7))
                        wt("scalar", "s_te", cnt["s_te"])
                        wt("scalar", "s_te", v3_free_te.get(vb, 0))
                        ev = se(lambda e, S, _vb=vb, _d=dch, _bk=bk, _nr=nr: e.copy(
                            V3[_vb]()[0:_nr, _d * 512:(_d + 1) * 512],
                            pk[_bk][0:_nr, 0:512]))
                        bank_free[bk]["s_se"] = ev

                vb0 = ach % 3
                vb1 = (ach + 1) % 3

                def emit_st(h):
                    sbk = h % 4
                    blk = h // 2
                    po = 64 * (h % 2)
                    bank_guard(sbk)
                    te(lambda e, S, _b=blk, _po=po, _bk=sbk: e.matmul(
                        pk[_bk][:, 0:128],
                        KTV()[_po:_po + 64, _b * 134: _b * 134 + 128],
                        QTV()[_po:_po + 64, _b * 128: _b * 128 + 128],
                        start=True, stop=True), inc=False)
                    return te(lambda e, S, _b=blk, _po=po, _bk=sbk: e.matmul(
                        pk[_bk][0:6, 128:256],
                        KTV()[_po:_po + 64, _b * 134 + 128: _b * 134 + 134],
                        QTV()[_po:_po + 64, _b * 128: _b * 128 + 128],
                        start=True, stop=True))

                st_done = {}
                od_of = {}
                ah_t_te = {}

                def emit_ah_t(db, _ach=ach):
                    _pb = _ach % 2
                    wt("tensor", "s_ve", cnt["s_ve"])
                    if db == 0:
                        wt("tensor", "s_se", bank_free[6 + _pb]["s_se"])
                        wt("tensor", "s_ve", bank_free[6 + _pb]["s_ve"])
                    te(lambda e, S, _db=db, _p2=_pb: e.transpose(
                        pkb[_p2][:, _db * 128:(_db + 1) * 128],
                        AHB()[:, _db * 128:(_db + 1) * 128], idb[:]))
                    ah_t_te[db] = cnt["s_te"]

                def emit_norm(h):
                    obk_ = 4 + h % 2
                    wt("vector", "s_te", od_of[h])
                    if h == 0:
                        wt("vector", "s_te", ah_t_te_prev[0])
                    ve(lambda e, S, _bk=obk_: e.reciprocal(
                        rden[:], pk[_bk][:, 64:65]))
                    ve(lambda e, S, _h=h, _bk=obk_: e.tensor_scalar(
                        AHB()[:, _h * 64:(_h + 1) * 64],
                        pk[_bk][:, 0:64], rden[:, 0:1], None, ALU.mult))
                    bank_read(obk_, "s_ve")

                for h in range(4):
                    st_done[h] = emit_st(h)
                qkv_last_te = st_done[3]
                for h in range(H):
                    sbk = h % 4
                    obk = 4 + h % 2
                    ewb = h % 4
                    wt("scalar", "s_te", st_done[h])
                    wt("scalar", "s_te", ew_free_te[ewb])
                    exp_se = se(lambda e, S, _e=ewb, _bk=sbk: e.activation(
                        EWF[_e](), pk[_bk][:, 0:256], AF.Exp, scale=0.125))
                    bank_free[sbk]["s_se"] = exp_se
                    wt("vector", "s_se", exp_se)
                    ewc_ve = ve(lambda e, S, _e=ewb, _h=h: e.tensor_tensor(
                        EWF[_e](), EWF[_e](), cMall[:, _h * 256:(_h + 1) * 256],
                        ALU.mult))
                    wt("tensor", "s_ve", ewc_ve)
                    bank_guard(obk)
                    te(lambda e, S, _e=ewb, _v=vb0, _h=h, _bk=obk: e.matmul(
                        pk[_bk][:, 0:64], EW1[_e](),
                        V3[_v]()[:, _h * 64:(_h + 1) * 64],
                        start=True, stop=False), inc=False)
                    te(lambda e, S, _e=ewb, _v=vb1, _h=h, _bk=obk: e.matmul(
                        pk[_bk][:, 0:64], EW2[_e](),
                        V3[_v]()[0:6, _h * 64:(_h + 1) * 64],
                        start=False, stop=True), inc=False)
                    te(lambda e, S, _e=ewb, _bk=obk: e.matmul(
                        pk[_bk][:, 64:65], EW1[_e](), ones_c[:],
                        start=True, stop=False), inc=False)
                    od_te = te(lambda e, S, _e=ewb, _bk=obk: e.matmul(
                        pk[_bk][:, 64:65], EW2[_e](), ones_c[0:6, :],
                        start=False, stop=True))
                    ew_free_te[ewb] = od_te
                    v3_free_te[vb0] = od_te
                    v3_free_te[vb1] = od_te
                    od_of[h] = od_te
                    if h + 4 < H:
                        st_done[h + 4] = emit_st(h + 4)
                        qkv_last_te = st_done[h + 4]
                    if h > 0:
                        emit_norm(h - 1)
                    if h >= 3 and h % 2 == 1:
                        emit_ah_t((h - 3) // 2)
                if True:
                    emit_norm(H - 1)
                    emit_ah_t(6)
                    emit_ah_t(7)
                ah_t_te_prev[0] = ah_t_te[7]
                tpb = ach % 2
                wt("scalar", "s_te", ah_t_te[3])
                wt("scalar", "s_te", aht_free_te)
                aht_se1 = se(lambda e, S, _pb=tpb: e.copy(
                    AHT()[:, 0:512], pkb[_pb][:, 0:512]))
                wt("scalar", "s_te", ah_t_te[7])
                aht_se2 = se(lambda e, S, _pb=tpb: e.copy(
                    AHT()[:, 512:1024], pkb[_pb][:, 512:1024]))
                wt("tensor", "s_wt2", att_w_sw[2])
                bank_free[6 + tpb]["s_se"] = aht_se2
                wt("tensor", "s_se", aht_se1)
                bank_guard(4)
                bank_guard(5)
                wt("tensor", "s_wc", n_const_sw)
                for dqb in range(8):
                    if dqb == 4:
                        wt("tensor", "s_se", aht_se2)
                    for dch in range(2):
                        te(lambda e, S, _q=dqb, _d=dch, _sl=sl_att: e.matmul(
                            pk[4 + _d][:, 0:512],
                            AHT()[:, _q * 128:(_q + 1) * 128],
                            attwv(_sl, 3, _q, _d * 512, 512),
                            start=(_q == 0), stop=False), inc=False)
                for dch in range(2):
                    te(lambda e, S, _d=dch: e.matmul(
                        pk[4 + _d][:, 0:512], sel9[:, 8 * 128:9 * 128],
                        b2bo[:, _d * 512:(_d + 1) * 512],
                        start=False, stop=True), inc=(dch == 1))
                aht_free_te = cnt["s_te"]
                wt("vector", "s_te", aht_free_te)
                wt("vector", "s_xa", acc_init_sx)
                for dch in range(2):
                    ve(lambda e, S, _tb=ach, _d=dch: e.tensor_add(
                        acc[:, _tb * D + _d * 512: _tb * D + (_d + 1) * 512],
                        acc[:, _tb * D + _d * 512: _tb * D + (_d + 1) * 512],
                        pk[4 + _d][:, 0:512]))
                bank_read(4, "s_ve")
                bank_read(5, "s_ve")
            att_te_end = cnt["s_te"]
            area_prev[(wb + 1) % 3] = att_te_end
            area_prev[(wb + 2) % 3] = att_te_end
            wt("tensor", ln_evac[(p, 8)][0], ln_evac[(p, 8)][1])
            bank_guard(5)
            for tb in range(8):
                for hb in range(8):
                    te(lambda e, S, _tb=tb, _hb=hb: e.matmul(
                        pk[5][:, _tb * E:(_tb + 1) * E],
                        xnT[:, _hb * PTH + 3 + _tb * 128: _hb * PTH + 3 + (_tb + 1) * 128],
                        rwb[:, _hb * E:(_hb + 1) * E],
                        start=(_hb == 0), stop=(_hb == 7)), inc=(hb == 7))
            lg_te = cnt["s_te"]
            wt("vector", "s_te", lg_te)
            for tb in range(8):
                ve(lambda e, S, _tb=tb: e.max(
                    out=m8[:, _tb * 8:(_tb + 1) * 8],
                    in_=pk[5][:, _tb * E:(_tb + 1) * E]))
            m3 = lambda c: (m8[:].rearrange("p (t c) -> p t c", t=8)
                            [:, :, c:c + 1].broadcast_to([128, 8, 8]))
            ve(lambda e, S: e.tensor_sub(
                g2c[:].rearrange("p t -> p t ()"),
                m8[:].rearrange("p (t c) -> p t c", t=8)[:, :, 1:2],
                m8[:].rearrange("p (t c) -> p t c", t=8)[:, :, 0:1]))
            sub_ve = cnt["s_ve"]
            wt("scalar", "s_ve", sub_ve)
            sig_se = se(lambda e, S: e.activation(g2c[:], g2c[:], AF.Sigmoid))
            wt("vector", "s_se", sig_se)
            ve(lambda e, S: e.tensor_scalar(
                g1c[:], g2c[:], -1.0, 1.0, ALU.mult, ALU.add))
            p5t = lambda: pk[5][:, 0:64].rearrange("p (t c) -> p t c", t=8)
            gv = lambda g: g[:].unsqueeze(2).broadcast_to([128, 8, 8])
            gt3 = lambda t: t[:].rearrange("p (t c) -> p t c", t=8)
            ve(lambda e, S: e.tensor_tensor(gt3(gtmp), p5t(), m3(0), ALU.is_equal))
            ve(lambda e, S: e.tensor_tensor(gt3(gtmp), gt3(gtmp), gv(g1c), ALU.mult))
            ve(lambda e, S: e.tensor_tensor(gt3(gtm2), p5t(), m3(1), ALU.is_equal))
            ve(lambda e, S: e.tensor_tensor(gt3(gtm2), gt3(gtm2), gv(g2c), ALU.mult))
            ve(lambda e, S: e.tensor_add(Gs[:], gtmp[:], gtm2[:]))
            ve(lambda e, S: e.tensor_scalar(
                maskf[:], Gs[:], 0.0, None, ALU.is_gt))
            mb_ve = ve(lambda e, S: e.tensor_copy(maskb[:], maskf[:]))
            wt("tensor", "s_ve", mb_ve)
            bank_guard(4)
            for kk in range(8):
                te(lambda e, S, _k=kk: e.matmul(
                    pk[4][:, _k * 8:(_k + 1) * 8], ltri[:],
                    maskb[:, _k * 8:(_k + 1) * 8],
                    start=True, stop=(_k == 0)), inc=False)
                for jj in range(kk):
                    te(lambda e, S, _k=kk, _j=jj: e.matmul(
                        pk[4][:, _k * 8:(_k + 1) * 8], ones128[:],
                        maskb[:, _j * 8:(_j + 1) * 8],
                        start=False, stop=(_j == _k - 1)),
                       inc=(kk == 7 and jj == 6))
            pos_te = cnt["s_te"]
            wt("vector", "s_te", pos_te)
            poss_ve = ve(lambda e, S: e.tensor_copy(poss[:], pk[4][:, 0:64]))
            bank_read(4, "s_ve")

            pbuild_ve = {}

            def emit_pbuild(e_, after_te):
                wt("vector", "s_te", after_te)
                for t in range(8):
                    ve(lambda e, S, _t=t, _e=e_: e.tensor_scalar(
                        PV(_t), iotaM[:],
                        poss[:, _t * 8 + _e:_t * 8 + _e + 1],
                        maskf[:, _t * 8 + _e:_t * 8 + _e + 1],
                        ALU.is_equal, ALU.mult))
                pbuild_ve[e_] = cnt["s_ve"]

            emit_pbuild(0, 0)

            if p + 1 < NPASS:
                for q2 in (0, 1, 2):
                    g2 = (p + 1) * 9 + q2
                    buf2 = g2 % 3
                    vefree, sefree = xs_free[g2 - 3]
                    wt("sync", "s_ve", vefree)
                    wt("sync", "s_se", sefree)
                    if q2 == 0:
                        op("sync", lambda e, S, _p=p + 1, _b=buf2: e.dma_start(
                            out=XS[_b]()[0:3, :],
                            in_=x[_p * PT: _p * PT + 3, :]), f"s_xs{buf2}", 16)
                        op("sync", lambda e, S, _p=p + 1, _b=buf2: e.dma_start(
                            out=XS[_b]()[3:6, :],
                            in_=x[_p * PT + 1027: _p * PT + 1030, :]),
                            f"s_xs{buf2}", 16)
                    else:
                        c2 = q2 - 1
                        op("sync", lambda e, S, _p=p + 1, _c=c2, _b=buf2: e.dma_start(
                            out=XS[_b](),
                            in_=x[_p * PT + 3 + _c * 128:
                                  _p * PT + 3 + (_c + 1) * 128, :]),
                            f"s_xs{buf2}", 16)
                    pre_x[(p + 1, q2)] = cnt[f"s_xs{buf2}"]
                    xdma_of[(p + 1, q2)] = pre_x[(p + 1, q2)]

            w2_sw[0] = emit_piece(area_of(1), w2_dmas(area_of(1), 0, 0, p > 1),
                                  w2_wb(area_of(1), 0, 0) if p == 1 else None)
            w1_sw[1] = emit_piece(area_of(2), w1_dmas(area_of(2), 0, 1, p > 0),
                                  w1_wb(area_of(2), 0, 1) if p == 0 else None)

            comp_te = {}
            for e_ in range(E):
                wt("tensor", "s_ve", pbuild_ve[e_])
                ptv_se = 0
                for sc in range(3):
                    pb = sc % 2
                    ns = min(CAP, (sc + 1) * 128) - sc * 128
                    wt("tensor", "s_se", bank_free[6 + pb]["s_se"])
                    wt("tensor", "s_ve", bank_free[6 + pb]["s_ve"])
                    for t in range(8):
                        te(lambda e, S, _t=t, _sc=sc, _pb=pb, _n=ns: e.transpose(
                            pkb[_pb][0:_n, _t * 128:(_t + 1) * 128],
                            PV(_t)[:, _sc * 128:_sc * 128 + _n], idb[:]),
                            inc=(t == 7))
                    tp_te = cnt["s_te"]
                    wt("scalar", "s_te", tp_te)
                    ptv_se = se(lambda e, S, _sc=sc, _pb=pb, _n=ns: e.activation(
                        PTV(_sc)[0:_n, :], pkb[_pb][0:_n, :], AF.Copy))
                    bank_free[6 + pb]["s_se"] = ptv_se
                for dc in range(8):
                    bk = dc % 2
                    bank_guard(bk)
                    for t in range(8):
                        te(lambda e, S, _t=t, _dc=dc, _bk=bk: e.matmul(
                            pk[_bk][:, 0:CAP],
                            XNB[_t]()[:, _dc * 128:(_dc + 1) * 128],
                            PV(_t), start=(_t == 0), stop=(_t == 7)),
                            inc=(t == 7))
                    cte = cnt["s_te"]
                    wt("scalar", "s_te", cte)
                    se(lambda e, S, _dc=dc, _bk=bk: e.activation(
                        XGV(_dc), pk[_bk][:, 0:CAP], AF.Copy))
                    bank_read(bk, "s_se")
                comp_te[e_] = cnt["s_te"]
                xg_se = cnt["s_se"]
                if e_ + 1 < E:
                    emit_pbuild(e_ + 1, comp_te[e_])
                else:
                    xnb_pass_free[0] = comp_te[e_]
                    if p + 1 < NPASS:
                        def pre_x_late(q3, _p=p + 1):
                            g3 = _p * 9 + q3
                            b3 = g3 % 3
                            vf3, sf3 = xs_free[g3 - 3]
                            wt("sync", "s_ve", vf3)
                            wt("sync", "s_se", sf3)
                            c3 = q3 - 1
                            op("sync", lambda e, S, _pp=_p, _c=c3, _b=b3:
                               e.dma_start(
                                out=XS[_b](),
                                in_=x[_pp * PT + 3 + _c * 128:
                                      _pp * PT + 3 + (_c + 1) * 128, :]),
                                f"s_xs{b3}", 16)
                            pre_x[(_p, q3)] = cnt[f"s_xs{b3}"]
                            xdma_of[(_p, q3)] = pre_x[(_p, q3)]
                        for qf in (0, 1, 2):
                            emit_ln_front(qf, _p=p + 1)
                        for q3 in (3, 4, 5):
                            pre_x_late(q3)
                        for qf in (3, 4, 5):
                            emit_ln_front(qf, _p=p + 1)
                        for q3 in (6, 7, 8):
                            pre_x_late(q3)
                for phi in range(2):
                    k = e_ * 2 + phi
                    ar1 = area_of(2 * k)
                    ar2 = area_of(2 * k + 1)
                    wt("tensor", w1_sw[k][0], w1_sw[k][1])
                    if phi == 0:
                        wt("tensor", "s_se", xg_se)
                    gelu_of = {}
                    for fb in range(NFB):
                        abk = fb % 2
                        bank_guard(abk)
                        for hb in range(8):
                            te(lambda e, S, _sl=ar1, _h=hb, _f=fb, _bk=abk:
                               e.matmul(pk[_bk][:, 0:CAP],
                                        w1v(_sl, _h, _f), XGV(_h),
                                        start=(_h == 0), stop=(_h == 7)),
                               inc=(hb == 7))
                        mm1_te = cnt["s_te"]
                        wt("scalar", "s_te", mm1_te)
                        gse = se(lambda e, S, _f=fb, _bk=abk, _e=e_, _phi=phi:
                                 e.activation(
                            H1V(_f), pk[_bk][:, 0:CAP], AF.Gelu,
                            bias=b1s[:, _e * 32 + _phi * 16 + _f:
                                     _e * 32 + _phi * 16 + _f + 1]))
                        gelu_of[fb] = gse
                        bank_free[abk]["s_se"] = gse
                    mm1_end = cnt["s_te"]
                    wt("tensor", w2_sw[k][0], w2_sw[k][1])
                    for sb in (2, 0, 1):
                        bks = (2 + 2 * sb, 3 + 2 * sb)
                        if phi == 0:
                            bank_guard(bks[0])
                            bank_guard(bks[1])
                        wt("tensor", "s_se", gelu_of[13])
                        for fb in range(NFB):
                            if fb >= 14:
                                wt("tensor", "s_se", gelu_of[fb])
                            for dch in range(2):
                                if sb == 2:
                                    outap = (lambda _d=dch: pkb[_d][:]
                                             .bitcast(F32)[:, 0:512])
                                else:
                                    outap = (lambda _d=dch, _b=bks:
                                             pk[_b[_d]][:, 0:512])
                                nsb = min(CAP, (sb + 1) * 128) - sb * 128
                                te(lambda e, S, _sl=ar2, _f=fb, _sb=sb, _d=dch,
                                   _o=outap, _phi=phi, _n=nsb: e.matmul(
                                    _o()[0:_n, :],
                                    H1V(_f)[:, _sb * 128:_sb * 128 + _n],
                                    w2v(_sl, _f, _d),
                                    start=(_phi == 0 and _f == 0), stop=False),
                                   inc=(phi == 0 and fb == NFB - 1 and dch == 1))
                        if phi == 1:
                            for dch in range(2):
                                if sb == 2:
                                    outap = (lambda _d=dch: pkb[_d][:]
                                             .bitcast(F32)[:, 0:512])
                                else:
                                    outap = (lambda _d=dch, _b=bks:
                                             pk[_b[_d]][:, 0:512])
                                nsb = min(CAP, (sb + 1) * 128) - sb * 128
                                te(lambda e, S, _e=e_, _d=dch, _o=outap, _n=nsb:
                                   e.matmul(
                                    _o()[0:_n, :],
                                    sel9[:, _e * 128:_e * 128 + _n],
                                    b2bo[:, _d * 512:(_d + 1) * 512],
                                    start=False, stop=True), inc=(dch == 1))
                            mm2_sb_te = cnt["s_te"]
                            wt("scalar", "s_te", mm2_sb_te)
                            for dch in range(2):
                                if sb == 2:
                                    inap = (lambda _d=dch: pkb[_d][:]
                                            .bitcast(F32)[:, 0:512])
                                else:
                                    inap = (lambda _d=dch, _b=bks:
                                            pk[_b[_d]][:, 0:512])
                                nsb = min(CAP, (sb + 1) * 128) - sb * 128
                                se(lambda e, S, _sb=sb, _d=dch, _i=inap, _n=nsb:
                                   e.activation(
                                    O2V(_sb)[0:_n, _d * 512:(_d + 1) * 512],
                                    _i()[0:_n, :], AF.Copy))
                            bank_read(bks[0], "s_se")
                            bank_read(bks[1], "s_se")
                    area_prev[ar1] = mm1_end
                    area_prev[ar2] = cnt["s_te"]
                    if k + 1 < 16:
                        kk = k + 1
                        w2_sw[kk] = emit_piece(
                            area_of(2 * kk + 1),
                            w2_dmas(area_of(2 * kk + 1), kk // 2, kk % 2, p > 1),
                            w2_wb(area_of(2 * kk + 1), kk // 2, kk % 2)
                            if p == 1 else None)
                    if k + 2 < 16:
                        kk = k + 2
                        w1_sw[kk] = emit_piece(
                            area_of(2 * kk),
                            w1_dmas(area_of(2 * kk), kk // 2, kk % 2, p > 0),
                            w1_wb(area_of(2 * kk), kk // 2, kk % 2)
                            if p == 0 else None)
                    if k == 15 and p + 1 < NPASS:
                        att_w_next[0] = emit_attw(
                            (p + 1) % 3, cached=(p + 1 > 1), wbk=(p + 1 == 1))
                o2_se = cnt["s_se"]
                wt("tensor", "s_se", o2_se)
                for t in range(8):
                    sl = (3, 0, 1, 2, 0, 1, 2, 0)[t]
                    bks = (2 * sl, 2 * sl + 1)
                    bank_guard(bks[0])
                    bank_guard(bks[1])
                    if sl == 3:
                        slotap = (lambda _d: (lambda: pkb[_d][:]
                                              .bitcast(F32)[:, 0:512]))
                    else:
                        slotap = (lambda _d, _b=bks: (lambda: pk[_b[_d]][:, 0:512]))
                    for sc in range(3):
                        nsc = min(CAP, (sc + 1) * 128) - sc * 128
                        for dch in range(2):
                            te(lambda e, S, _t=t, _sc=sc, _d=dch, _o=slotap(dch),
                               _n=nsc: e.matmul(
                                _o(),
                                PTV(_sc)[0:_n, _t * 128:(_t + 1) * 128],
                                O2V(_sc)[0:_n, _d * 512:(_d + 1) * 512],
                                start=(_sc == 0), stop=(_sc == 2)),
                               inc=(sc == 2 and dch == 1))
                    st_te = cnt["s_te"]
                    wt("vector", "s_te", st_te)
                    wt("vector", "s_xa", acc_init_sx)
                    for dch in range(2):
                        ve(lambda e, S, _t=t, _d=dch, _e=e_, _i=slotap(dch):
                           e.scalar_tensor_tensor(
                            acc[:, _t * D + _d * 512: _t * D + (_d + 1) * 512],
                            _i(),
                            Gs[:, _t * E + _e: _t * E + _e + 1],
                            acc[:, _t * D + _d * 512: _t * D + (_d + 1) * 512],
                            ALU.mult, ALU.add))
                    bank_read(bks[0], "s_ve")
                    bank_read(bks[1], "s_ve")
            if p <= 1:
                flush_wb(0)
            pass_ve_end = cnt["s_ve"]

            wt("sync", "s_ve", pass_ve_end)
            op("sync", lambda e, S, _p=p: e.dma_start(
                out=out_t[_p * PT:(_p + 1) * PT, :]
                    .rearrange("(tb q) d -> q tb d", q=128),
                in_=acc[:].rearrange("q (tb d) -> q tb d", tb=8)), "s_out", 16)

        wt("sync", "s_out", cnt["s_out"])

        def run_body(lst):
            def body(e):
                for f in lst:
                    f(e, S)
            return body
        block.sync(run_body(ops["sync"]))
        block.gpsimd(run_body(ops["gpsimd"]))
        block.tensor(run_body(ops["tensor"]))
        block.scalar(run_body(ops["scalar"]))
        block.vector(run_body(ops["vector"]))

    return nc


def host_inputs(inputs):
    f32 = np.float32
    bf16 = ml_dtypes.bfloat16
    xf = np.asarray(inputs["L0_attn_x"], f32)
    shared = {
        "w1T": np.ascontiguousarray(np.asarray(inputs["L0_ffn_w1"], f32).transpose(0, 2, 1)),
        "w2T": np.ascontiguousarray(np.asarray(inputs["L0_ffn_w2"], f32).transpose(0, 2, 1)),
        "wqT": np.ascontiguousarray(np.asarray(inputs["L0_attn_wq"], f32).T),
        "wkT": np.ascontiguousarray(np.asarray(inputs["L0_attn_wk"], f32).T),
        "wvT": np.ascontiguousarray(np.asarray(inputs["L0_attn_wv"], f32).T),
        "woT": np.ascontiguousarray(np.asarray(inputs["L0_attn_wo"], f32).T),
        "rwT": np.ascontiguousarray(np.asarray(inputs["L0_ffn_router_w"], f32).T),
        "b1T": np.ascontiguousarray(
            np.asarray(inputs["L0_ffn_b1"], f32).reshape(E, 32, 128).transpose(0, 2, 1)),
        "b2i": np.ascontiguousarray(np.asarray(inputs["L0_ffn_b2"], f32)),
        "boi": np.ascontiguousarray(np.asarray(inputs["L0_attn_bo"], f32)[None, :]),
        "cw1t": np.ascontiguousarray(
            np.repeat(np.asarray(inputs["L0_attn_cw1"], f32)[:, :, 0], WIN, axis=0)),
        "cb1t": np.ascontiguousarray(
            np.repeat(np.asarray(inputs["L0_attn_cb1"], f32), WIN, axis=0)),
        "cw2f": np.ascontiguousarray(
            np.repeat(np.asarray(inputs["L0_attn_cw2"], f32)
                      .transpose(0, 2, 1).reshape(H, 1024), WIN, axis=0)),
        "cb2t": np.ascontiguousarray(
            np.repeat(np.asarray(inputs["L0_attn_cb2"], f32), WIN, axis=0)),
        "cw3s": np.ascontiguousarray(
            np.repeat(np.asarray(inputs["L0_attn_cw3"], f32)[:, 0, :], WIN, axis=0)),
        "cb3s": np.ascontiguousarray(
            np.repeat(np.asarray(inputs["L0_attn_cb3"], f32), WIN, axis=0)),
        "post": np.ascontiguousarray(
            np.tile(np.linspace(0.0, 1.0, WIN, dtype=f32), H)[:, None]),
        "identb": np.eye(128, dtype=bf16),
        "identf": np.eye(128, dtype=f32),
        "onescol_b": np.ones((128, 1), dtype=bf16),
        "onesrow_f": np.ones((1, 128), dtype=f32),
        "iotaM_i": np.ascontiguousarray(
            np.tile(np.arange(1, CAP + 1, dtype=f32)[None, :], (128, 1))),
        "ltri_i": np.triu(np.ones((128, 128))).astype(bf16),
        "ones128_i": np.ones((128, 128), dtype=bf16),
        "sel9_i": np.ascontiguousarray(np.kron(np.eye(9), np.ones((1, 128))).astype(bf16)),
    }
    r = np.arange(128)[:, None]
    j = np.arange(128)[None, :]
    D1m = np.stack([(r - j == w).astype(f32) for w in range(WIN)])
    r2 = np.arange(6)[:, None]
    D2m = np.stack([(128 + r2 - j == w).astype(f32) for w in range(WIN)])
    shared["D1"] = np.ascontiguousarray(D1m)
    shared["D2"] = np.ascontiguousarray(D2m)

    in_maps = []
    for c in range(8):
        b, s = c // 2, c % 2
        xc = np.zeros((XROWS, D), f32)
        lo = s * 4096 - 3
        hi = s * 4096 + 4096 + 3
        vlo, vhi = max(lo, 0), min(hi, 8192)
        xc[vlo - lo: vhi - lo] = xf[b, vlo:vhi]
        m = dict(shared)
        m["x"] = xc
        in_maps.append(m)
    return in_maps


def kernel(**inputs):
    in_maps = host_inputs(inputs)
    nc = build_program()
    res = run_bass_kernel_spmd(nc, in_maps, core_ids=list(range(8)))
    out = np.empty((4, 8192, D), np.float32)
    for c in range(8):
        b, s = c // 2, c % 2
        out[b, s * 4096:(s + 1) * 4096] = np.asarray(res.results[c]["out"])
    return out


# version 18
# speedup vs baseline: 1.1891x; 1.0058x over previous
import numpy as np
import ml_dtypes
import concourse.bass as bass
import concourse.mybir as mybir
from concourse.bass_utils import run_bass_kernel_spmd
from contextlib import ExitStack

F32 = mybir.dt.float32
BF16 = mybir.dt.bfloat16
AF = mybir.ActivationFunctionType
ALU = mybir.AluOpType

D = 1024
H = 16
DH = 64
WIN = 7
NPASS = 4
PT = 1024
PTH = PT + 6
CORE_T = NPASS * PT
ACH = 128
NACH = PT // ACH
E = 8
FF = 4096
FH = 2048
TH = 512
NFB = 16
XROWS = CORE_T + 6 + 122
CAP = 352

ENGS = ("sync", "gpsimd", "tensor", "scalar", "vector")


def build_program():
    nc = bass.Bass("TRN2", target_bir_lowering=False, debug=False)

    def din(name, shape, dt=F32):
        return nc.dram_tensor(name, shape, dt, kind="ExternalInput").ap()

    x = din("x", [XROWS, D])
    w1T = din("w1T", [E, D, FF])
    w2T = din("w2T", [E, FF, D])
    wqT = din("wqT", [D, D])
    wkT = din("wkT", [D, D])
    wvT = din("wvT", [D, D])
    woT = din("woT", [D, D])
    rwT = din("rwT", [D, E])
    b1T = din("b1T", [E, 128, 32])
    b2i = din("b2i", [E, D])
    boi = din("boi", [1, D])
    cw1t = din("cw1t", [112, 32])
    cb1t = din("cb1t", [112, 32])
    cw2f = din("cw2f", [112, 1024])
    cb2t = din("cb2t", [112, 32])
    cw3s = din("cw3s", [112, 32])
    cb3s = din("cb3s", [112, 1])
    post = din("post", [112, 1])
    D1 = din("D1", [WIN, 128, 128])
    D2 = din("D2", [WIN, 6, 128])
    identb = din("identb", [128, 128], BF16)
    identf = din("identf", [128, 128])
    onescol_b = din("onescol_b", [128, 1], BF16)
    onesrow_f = din("onesrow_f", [1, 128])
    iotaM_i = din("iotaM_i", [128, CAP])
    ltri_i = din("ltri_i", [128, 128], BF16)
    sel9_i = din("sel9_i", [9, 9 * 128], BF16)
    ones128_i = din("ones128_i", [128, 128], BF16)
    ones_b_row_src = onesrow_f

    out_t = nc.dram_tensor("out", [CORE_T, D], F32, kind="ExternalOutput").ap()
    w1d = nc.dram_tensor("w1d", [16, 128, 16384], BF16, kind="Internal").ap()
    w2d = nc.dram_tensor("w2d", [16, 128, 16384], BF16, kind="Internal").ap()
    attd = nc.dram_tensor("attd", [4, 128, 8192], BF16, kind="Internal").ap()

    ops = {e: [] for e in ENGS}
    cnt = {s: 0 for s in ("s_xc", "s_xs0", "s_xs1", "s_xs2", "s_xs3", "s_xa", "s_xb", "s_out",
                          "s_wc", "s_wa0", "s_wa1", "s_wa2", "s_wt", "s_wt2", "s_wv",
                          "s_wd0", "s_wd1", "s_wd2", "s_xd",
                          "s_te", "s_se", "s_ve")}
    hiw = {e: {s: 0 for s in cnt} for e in ENGS}

    def wt(eng, sem, val):
        if val is None or val <= 0 or val <= hiw[eng][sem]:
            return
        hiw[eng][sem] = val
        def _w(e, S, _s=sem, _v=val):
            e.wait_ge(S[_s], _v)
        ops[eng].append(_w)

    def op(eng, fn, inc_sem=None, inc_by=1):
        if inc_sem is not None:
            cnt[inc_sem] += inc_by
        def _f(e, S, _fn=fn, _i=inc_sem, _b=inc_by):
            ins = _fn(e, S)
            if _i is not None:
                ins.then_inc(S[_i], _b)
        ops[eng].append(_f)
        return cnt[inc_sem] if inc_sem else None

    def ve(fn):
        wt("vector", "s_ve", cnt["s_ve"])
        return op("vector", fn, "s_ve")

    def se(fn):
        wt("scalar", "s_se", cnt["s_se"])
        return op("scalar", fn, "s_se")

    def te(fn, inc=True):
        return op("tensor", fn, "s_te" if inc else None)

    bank_free = {b: {"s_se": 0, "s_ve": 0} for b in range(8)}

    def bank_guard(b):
        wt("tensor", "s_se", bank_free[b]["s_se"])
        wt("tensor", "s_ve", bank_free[b]["s_ve"])

    def bank_read(b, sem):
        bank_free[b][sem] = cnt[sem]

    with ExitStack() as ctx:
        En = ctx.enter_context
        acc = En(nc.sbuf_tensor("acc", [128, 8 * D], F32))
        slot0 = En(nc.sbuf_tensor("slot0", [128, 32768], BF16))
        slot1 = En(nc.sbuf_tensor("slot1", [128, 16384], BF16))
        scr = En(nc.sbuf_tensor("scr", [128, 16480], BF16))
        b1s = En(nc.sbuf_tensor("b1s", [128, E * 32], F32))
        b2bo = En(nc.sbuf_tensor("b2bo", [9, D], BF16))
        sel9 = En(nc.sbuf_tensor("sel9", [9, 9 * 128], BF16))
        rwb = En(nc.sbuf_tensor("rwb", [128, 8 * E], BF16))
        idb = En(nc.sbuf_tensor("idb", [128, 128], BF16))
        idf = En(nc.sbuf_tensor("idf", [112, 112], F32))
        ones_c = En(nc.sbuf_tensor("ones_c", [128, 1], BF16))
        cMall = En(nc.sbuf_tensor("cMall", [128, H * 256], BF16))
        Gs = En(nc.sbuf_tensor("Gs", [128, 8 * E], F32))
        m8 = En(nc.sbuf_tensor("m8", [128, 64], F32))
        poss = m8
        g2c = En(nc.sbuf_tensor("g2c", [128, 8], F32))
        g1c = En(nc.sbuf_tensor("g1c", [128, 8], F32))
        gtmp = En(nc.sbuf_tensor("gtmp", [128, 64], F32))
        bnst = En(nc.sbuf_tensor("bnst", [128, 36], F32))
        bnagg = En(nc.sbuf_tensor("bnagg", [128, 6], F32))
        rstd = En(nc.sbuf_tensor("rstd", [128, 3], F32))
        nmr = En(nc.sbuf_tensor("nmr", [128, 3], F32))
        lnt = En(nc.sbuf_tensor("lnt", [128, 3], F32))
        rden = En(nc.sbuf_tensor("rden", [128, 1], F32))
        xs_t = [En(nc.sbuf_tensor(f"xs{i}", [128, D], F32)) for i in range(3)]
        xnb_t = [En(nc.sbuf_tensor(f"xnb{i}", [128, D], BF16)) for i in range(8)]
        xnbH = En(nc.sbuf_tensor("xnbH", [6, D], BF16))
        iotaM = En(nc.sbuf_tensor("iotaM", [128, CAP], F32))
        ltri = En(nc.sbuf_tensor("ltri", [128, 128], BF16))
        ones128 = En(nc.sbuf_tensor("ones128", [128, 128], BF16))
        maskf = En(nc.sbuf_tensor("maskf", [128, 64], F32))
        gtm2 = maskf
        maskb = En(nc.sbuf_tensor("maskb", [128, 64], BF16))

        pk = [En(nc.psum_tensor(f"pk{i}", [128, 512], F32)) for i in range(6)]
        pkb = [En(nc.psum_tensor(f"pkb{i}", [128, 1024], BF16)) for i in range(2)]

        S = {s: En(nc.semaphore(s)) for s in cnt}
        block = En(nc.Block())

        def scr_f32(col, n):
            return scr[:, col:col + 2 * n].bitcast(F32)
        XS = [lambda i=i: xs_t[i][:] for i in range(3)]
        xnT = None
        def _cw(col, ncols_f32, np_=H):
            return lambda: scr[0:np_, col:col + 2 * ncols_f32].bitcast(F32)
        c_cw1 = _cw(0, 32, 112)
        c_cb1 = _cw(64, 32, 112)
        c_cw2 = _cw(128, 1024, 112)
        c_cb2 = _cw(2176, 32, 112)
        c_cw3 = _cw(2240, 32, 112)
        c_cb3 = _cw(2304, 8, 112)
        c_pos = _cw(2320, 8, 112)
        c_h1 = _cw(2336, 32, 112)
        c_h2 = _cw(2400, 32, 112)
        c_cwt = _cw(2464, 8, 112)
        c_bn = _cw(2480, 8, 112)
        c_row_t = scr[0:1, 4992:5216].bitcast(F32)
        XNB = [lambda i=i: xnb_t[i][:] for i in range(8)]
        D1V = lambda: scr_f32(5480, 896)
        D2V = lambda: scr[0:6, 0:1792].bitcast(F32)
        xnT = scr[:, 0:8240]
        A0 = 8240
        QTV = lambda: scr[:, A0:A0 + 1024]
        KTV = lambda: scr[:, A0 + 1024:A0 + 1024 + 8 * 134]
        V3 = [lambda k=k: scr[:, A0 + 2096 + 1024 * k:A0 + 2096 + 1024 * (k + 1)]
              for k in range(3)]
        E128 = scr[:, A0:A0 + 224].bitcast(F32)
        ones_rf = scr[0:1, A0 + 224:A0 + 480].bitcast(F32)
        AHB = lambda: scr[:, A0 + 5168:A0 + 6192]
        AHT = lambda: scr[:, A0 + 6192:A0 + 7216]
        EWF = [lambda k=k: scr[:, A0 + 7216 + 256 * k:A0 + 7472 + 256 * k] for k in range(4)]
        EW1 = [lambda k=k: scr[:, A0 + 7216 + 256 * k:A0 + 7344 + 256 * k] for k in range(4)]
        EW2 = [lambda k=k: scr[0:6, A0 + 7344 + 256 * k:A0 + 7472 + 256 * k] for k in range(4)]
        H1V = lambda fb: scr[:, fb * CAP:(fb + 1) * CAP]
        XGV = lambda hb: scr[:, 6144 + hb * CAP:6144 + (hb + 1) * CAP]
        PV = lambda t: scr[:, 9216 + t * CAP:9216 + (t + 1) * CAP]
        PTV = lambda sc: scr[:, 12288 + sc * PT:12288 + (sc + 1) * PT]
        O2V = lambda sb: scr[:, 6144 + sb * D:6144 + (sb + 1) * D]

        AREAS = [(slot0, 0), (slot0, 16384), (slot1, 0)]

        def w1v(ar, hb, fb):
            t, off = AREAS[ar]
            return t[:, off + hb * 2048 + fb * 128: off + hb * 2048 + (fb + 1) * 128]

        def w2v(ar, fb, dch):
            t, off = AREAS[ar]
            return t[:, off + fb * 1024 + dch * 512: off + fb * 1024 + (dch + 1) * 512]

        def attwv(wb, which, hb, c0, n):
            ar = (wb + 1) % 3 if which < 2 else (wb + 2) % 3
            t, off = AREAS[ar]
            base = off + (which % 2) * 8192 + hb * 1024
            return t[:, base + c0: base + c0 + n]

        cdmas = [
            (lambda: c_cw1(), cw1t), (lambda: c_cb1(), cb1t),
            (lambda: c_cw2(), cw2f), (lambda: c_cb2(), cb2t),
            (lambda: c_cw3(), cw3s), (lambda: c_cb3()[:, 0:1], cb3s),
            (lambda: c_pos()[:, 0:1], post),
            (lambda: idf[:], identf[0:112, 0:112]),
        ]
        for dstf, src in cdmas:
            op("sync", lambda e, S, _d=dstf, _s=src: e.dma_start(out=_d(), in_=_s),
               "s_xb", 16)
        n_conn_sx = cnt["s_xb"]
        cdmas2 = [
            (lambda: b1s[:].rearrange("p (e c) -> p e c", e=E), b1T.transpose([1, 0, 2])),
            (lambda: idb[:], identb),
            (lambda: ones_c[:], onescol_b),
            (lambda: ones_rf[:], onesrow_f),
            (lambda: D1V().rearrange("p (w j) -> p w j", w=WIN), D1.rearrange("w r j -> r w j")),
            (lambda: iotaM[:], iotaM_i),
            (lambda: ltri[:], ltri_i),
            (lambda: ones128[:], ones128_i),
        ]
        for dstf, src in cdmas2:
            op("sync", lambda e, S, _d=dstf, _s=src: e.dma_start(out=_d(), in_=_s),
               "s_xc", 16)
        n_const_sx = cnt["s_xc"]

        for dstf, src in ((lambda: rwb[:].rearrange("p (hb e) -> p hb e", hb=8), rwT.rearrange("(hb p) e -> p hb e", p=128)),
                          (lambda: b2bo[0:8, :], b2i), (lambda: b2bo[8:9, :], boi),
                          (lambda: sel9[:], sel9_i)):
            op("gpsimd", lambda e, S, _d=dstf, _s=src: e.dma_start(out=_d(), in_=_s),
               "s_wc", 16)
        n_const_sw = cnt["s_wc"]

        area_prev = [0, 0, 0]
        area_sem = ["s_wa0", "s_wa1", "s_wa2"]
        wd_sem = ["s_wd0", "s_wd1", "s_wd2"]
        area_wfree = [0, 0, 0]
        pend_wb = []

        def flush_wb(n_keep=0):
            while len(pend_wb) > n_keep:
                ar, lsem, lval, dsts = pend_wb.pop(0)
                wt("gpsimd", lsem, lval)
                for dst, view in dsts:
                    op("gpsimd", lambda e, S, _d=dst, _v=view: e.dma_start(
                        out=_d, in_=_v()), wd_sem[ar], 16)
                area_wfree[ar] = cnt[wd_sem[ar]]

        def emit_piece(ar, dmas, wb_dsts=None):
            flush_wb(0)
            wt("gpsimd", "s_te", area_prev[ar])
            wt("gpsimd", wd_sem[ar], area_wfree[ar])
            sem = area_sem[ar]
            for dstf, srcf in dmas:
                op("gpsimd", lambda e, S, _d=dstf, _s=srcf: e.dma_start(out=_d(), in_=_s),
                   sem, 16)
            if wb_dsts is not None:
                pend_wb.append((ar, sem, cnt[sem], wb_dsts))
            return (sem, cnt[sem])

        def w1_dmas(ar, e_, phi, cached):
            t, off = AREAS[ar]
            if cached:
                return [(lambda _t=t, _o=off: _t[:, _o:_o + 16384],
                         w1d[2 * e_ + phi])]
            return [(lambda _t=t, _o=off: _t[:, _o:_o + 16384]
                     .rearrange("p (hb f) -> p hb f", hb=8),
                     w1T[e_, :, phi * FH:(phi + 1) * FH]
                     .rearrange("(hb p) f -> p hb f", p=128))]

        def w2_dmas(ar, e_, phi, cached):
            t, off = AREAS[ar]
            if cached:
                return [(lambda _t=t, _o=off: _t[:, _o:_o + 16384],
                         w2d[2 * e_ + phi])]
            res = []
            for half in range(2):
                res.append((lambda _t=t, _o=off, _h=half: _t[:, _o + _h * 8192:
                                                             _o + (_h + 1) * 8192]
                            .rearrange("p (fb d) -> p fb d", fb=8),
                            w2T[e_, phi * FH + half * 1024: phi * FH + (half + 1) * 1024, :]
                            .rearrange("(fb p) d -> p fb d", p=128)))
            return res

        def w1_wb(ar, e_, phi):
            t, off = AREAS[ar]
            return [(w1d[2 * e_ + phi], lambda _t=t, _o=off: _t[:, _o:_o + 16384])]

        def w2_wb(ar, e_, phi):
            t, off = AREAS[ar]
            return [(w2d[2 * e_ + phi], lambda _t=t, _o=off: _t[:, _o:_o + 16384])]

        def emit_attw(wb, cached, wbk):
            flush_wb(0)
            ar1, ar2 = (wb + 1) % 3, (wb + 2) % 3
            wt("gpsimd", "s_te", area_prev[ar1])
            wt("gpsimd", wd_sem[ar1], area_wfree[ar1])
            for wi, wsrc in ((0, wqT), (1, wkT)):
                t, off = AREAS[ar1]
                if cached:
                    op("gpsimd", lambda e, S, _t=t, _o=off + wi * 8192, _w=wi:
                       e.dma_start(out=_t[:, _o:_o + 8192], in_=attd[_w]), "s_wt", 16)
                else:
                    op("gpsimd", lambda e, S, _t=t, _o=off + wi * 8192, _s=wsrc:
                       e.dma_start(
                        out=_t[:, _o:_o + 8192].rearrange("p (hb d) -> p hb d", hb=8),
                        in_=_s.rearrange("(hb p) d -> p hb d", p=128)), "s_wt", 16)
            if wbk:
                t, off = AREAS[ar1]
                pend_wb.append((ar1, "s_wt", cnt["s_wt"],
                                [(attd[0], lambda _t=t, _o=off: _t[:, _o:_o + 8192]),
                                 (attd[1], lambda _t=t, _o=off: _t[:, _o + 8192:_o + 16384])]))
            wt("gpsimd", "s_te", area_prev[ar2])
            wt("gpsimd", wd_sem[ar2], area_wfree[ar2])
            for wi, wsrc, sem in ((0, wvT, "s_wv"), (1, woT, "s_wt2")):
                t, off = AREAS[ar2]
                if cached:
                    op("gpsimd", lambda e, S, _t=t, _o=off + wi * 8192, _w=2 + wi:
                       e.dma_start(out=_t[:, _o:_o + 8192], in_=attd[_w]), sem, 16)
                else:
                    op("gpsimd", lambda e, S, _t=t, _o=off + wi * 8192, _s=wsrc:
                       e.dma_start(
                        out=_t[:, _o:_o + 8192].rearrange("p (hb d) -> p hb d", hb=8),
                        in_=_s.rearrange("(hb p) d -> p hb d", p=128)), sem, 16)
                if wbk:
                    pend_wb.append((ar2, sem, cnt[sem],
                                    [(attd[2 + wi], lambda _t=t, _o=off + wi * 8192:
                                      _t[:, _o:_o + 8192])]))
            return (cnt["s_wt"], cnt["s_wv"], cnt["s_wt2"])

        ve(lambda e, S: e.memset(cMall[:], 0.0))
        for _b in range(6):
            ve(lambda e, S, _bb=_b: e.memset(pk[_bb][:], 0.0))
        psum_init_ve = cnt["s_ve"]
        wt("vector", "s_xb", n_conn_sx)

        def r3(t, wdim):
            return t().rearrange("p (w j) -> p w j", w=wdim)

        ve(lambda e, S: e.scalar_tensor_tensor(
            c_h1(), c_cw1(), c_pos()[:, 0:1], c_cb1(), ALU.mult, ALU.add))
        wt("scalar", "s_ve", cnt["s_ve"])
        se(lambda e, S: e.activation(c_h1(), c_h1(), AF.Gelu))
        wt("vector", "s_se", cnt["s_se"])
        for i in range(32):
            blk = (lambda _i=i: c_cw2()[:, _i * 32:(_i + 1) * 32])
            col = (lambda _i=i: c_h1()[:, _i:_i + 1])
            if i == 0:
                ve(lambda e, S, _b=blk, _c=col: e.tensor_scalar(
                    c_h2(), _b(), _c(), None, ALU.mult))
            else:
                ve(lambda e, S, _b=blk, _c=col: e.scalar_tensor_tensor(
                    c_h2(), _b(), _c(), c_h2(), ALU.mult, ALU.add))
        ve(lambda e, S: e.tensor_tensor(c_h2(), c_h2(), c_cb2(), ALU.add))
        wt("scalar", "s_ve", cnt["s_ve"])
        se(lambda e, S: e.activation(c_h2(), c_h2(), AF.Gelu))
        wt("vector", "s_se", cnt["s_se"])
        ve(lambda e, S: e.tensor_tensor(c_h2(), c_h2(), c_cw3(), ALU.mult))
        ve(lambda e, S: e.bn_stats(c_bn()[:, 0:6], c_h2()))
        ve(lambda e, S: e.bn_aggr(c_bn()[:, 6:8], c_bn()[:, 0:6]))
        cwt_ve = ve(lambda e, S: e.scalar_tensor_tensor(
            c_cwt()[:, 0:1], c_bn()[:, 6:7], 32.0, c_cb3()[:, 0:1],
            ALU.mult, ALU.add))

        wt("sync", "s_ve", cwt_ve)
        op("sync", lambda e, S: e.dma_start(
            out=D2V().rearrange("p (w j) -> p w j", w=WIN),
            in_=D2.rearrange("w r j -> r w j")), "s_xd", 16)
        wt("tensor", "s_ve", psum_init_ve)
        wt("tensor", "s_ve", cwt_ve)
        wt("tensor", "s_xb", n_conn_sx)
        te(lambda e, S: e.matmul(pk[4][0:1, 0:112], c_cwt()[:, 0:1], idf[:],
                                 start=True, stop=True))
        wt("scalar", "s_te", cnt["s_te"])
        exp_se = se(lambda e, S: e.activation(c_row_t[:], pk[4][0:1, 0:112], AF.Exp))
        bank_read(4, "s_se")
        wt("tensor", "s_se", exp_se)
        wt("tensor", "s_xc", n_const_sx)
        bank_guard(5)
        te(lambda e, S: e.matmul(pk[5][:, 0:H * WIN], ones_rf[:], c_row_t[:],
                                 start=True, stop=True))
        wt("vector", "s_te", cnt["s_te"])
        ve(lambda e, S: e.tensor_copy(E128[:], pk[5][:, 0:H * WIN]))
        bank_read(5, "s_ve")
        wt("vector", "s_xd", cnt["s_xd"])
        for h in range(H):
            for w in range(WIN):
                d2w = (lambda _w=w: D2V()[:, _w * 128:(_w + 1) * 128])
                sc6 = (lambda _h=h, _w=w: E128[0:6, _h * WIN + _w:_h * WIN + _w + 1])
                o2 = (lambda _h=h: cMall[0:6, _h * 256 + 128:(_h + 1) * 256])
                if w == 0:
                    ve(lambda e, S, _d=d2w, _s=sc6, _o=o2: e.tensor_scalar(
                        _o(), _d(), _s(), None, ALU.mult))
                else:
                    ve(lambda e, S, _d=d2w, _s=sc6, _o=o2: e.scalar_tensor_tensor(
                        _o(), _d(), _s(), _o(), ALU.mult, ALU.add))
        cM2_ve = cnt["s_ve"]
        for h in range(H):
            for w in range(WIN):
                d1w = (lambda _w=w: D1V()[:, _w * 128:(_w + 1) * 128])
                sc = (lambda _h=h, _w=w: E128[:, _h * WIN + _w:_h * WIN + _w + 1])
                o1 = (lambda _h=h: cMall[:, _h * 256:_h * 256 + 128])
                if w == 0:
                    ve(lambda e, S, _d=d1w, _s=sc, _o=o1: e.tensor_scalar(
                        _o(), _d(), _s(), None, ALU.mult))
                else:
                    ve(lambda e, S, _d=d1w, _s=sc, _o=o1: e.scalar_tensor_tensor(
                        _o(), _d(), _s(), _o(), ALU.mult, ALU.add))
        connM_ve = cnt["s_ve"]

        xs_free = {}
        qkv_last_te = 0
        v3_free_te = {}
        ew_free_te = [0, 0, 0, 0]
        ah_t_te_prev = [0]
        xnb_pass_free = [0]
        aht_free_te = 0
        pass_ve_end = 0
        pre_x = {}

        att_w_next = [None]
        xdma_of = {}
        ln_affine = {}
        ln_evac = {}
        ln_tp_te = {}
        for p in range(NPASS):

            def emit_x(q, _p=p):
                g = _p * 9 + q
                buf = g % 3
                xsem = f"s_xs{buf}"
                if (_p, q) in pre_x:
                    return pre_x[(_p, q)]
                if g >= 3:
                    vefree, sefree = xs_free[g - 3]
                    wt("sync", "s_ve", vefree)
                    wt("sync", "s_se", sefree)
                if q == 0:
                    op("sync", lambda e, S, _pp=_p, _b=buf: e.dma_start(
                        out=XS[_b]()[0:3, :],
                        in_=x[_pp * PT: _pp * PT + 3, :]), xsem, 16)
                    op("sync", lambda e, S, _pp=_p, _b=buf: e.dma_start(
                        out=XS[_b]()[3:6, :],
                        in_=x[_pp * PT + 1027: _pp * PT + 1030, :]), xsem, 16)
                else:
                    c = q - 1
                    op("sync", lambda e, S, _pp=_p, _c=c, _b=buf: e.dma_start(
                        out=XS[_b](),
                        in_=x[_pp * PT + 3 + _c * 128: _pp * PT + 3 + (_c + 1) * 128, :]),
                        xsem, 16)
                return cnt[xsem]

            def emit_acc_init(_p=p):
                wt("sync", "s_out", 16 * _p)
                op("sync", lambda e, S, _pp=_p: e.dma_start(
                    out=acc[:].rearrange("q (tb d) -> q tb d", tb=8),
                    in_=x[3 + _pp * PT: 3 + (_pp + 1) * PT, :]
                        .rearrange("(tb q) d -> q tb d", q=128)), "s_xa", 16)
                return cnt["s_xa"]

            for c in range(3):
                if (p, c) not in xdma_of:
                    xdma_of[(p, c)] = emit_x(c)

            def emit_ln_front(q, _p=p):
                g = _p * 9 + q
                buf = g % 3
                np_ = 6 if q == 0 else 128
                xsem = f"s_xs{buf}"
                wt("vector", xsem, xdma_of[(_p, q)])
                B = buf
                ve(lambda e, S, _b=buf, _B=B, _n=np_: e.bn_stats(
                    bnst[0:_n, _B * 12:_B * 12 + 6], XS[_b]()[0:_n, 0:512]))
                ve(lambda e, S, _b=buf, _B=B, _n=np_: e.bn_stats(
                    bnst[0:_n, _B * 12 + 6:_B * 12 + 12], XS[_b]()[0:_n, 512:1024]))
                stats_ve = ve(lambda e, S, _B=B, _n=np_: e.bn_aggr(
                    bnagg[0:_n, _B * 2:_B * 2 + 2], bnst[0:_n, _B * 12:_B * 12 + 12]))
                veps = ve(lambda e, S, _B=B, _n=np_: e.tensor_scalar_add(
                    lnt[0:_n, _B:_B + 1], bnagg[0:_n, _B * 2 + 1:_B * 2 + 2], 1e-5))
                wt("scalar", "s_ve", veps)
                se(lambda e, S, _B=B, _n=np_: e.activation(
                    lnt[0:_n, _B:_B + 1], lnt[0:_n, _B:_B + 1], AF.Ln))
                rstd_se = se(lambda e, S, _B=B, _n=np_: e.activation(
                    rstd[0:_n, _B:_B + 1], lnt[0:_n, _B:_B + 1], AF.Exp, scale=-0.5))
                wt("vector", "s_se", rstd_se)
                nmr_ve = ve(lambda e, S, _B=B, _n=np_: e.scalar_tensor_tensor(
                    nmr[0:_n, _B:_B + 1], bnagg[0:_n, _B * 2:_B * 2 + 1], -1.0,
                    rstd[0:_n, _B:_B + 1], ALU.mult, ALU.mult))
                wt("scalar", "s_ve", nmr_ve)
                wt("scalar", "s_te", xnb_pass_free[0])
                xnbv = (lambda: xnbH[:]) if q == 0 else XNB[q - 1]
                aff_se = se(lambda e, S, _b=buf, _B=B, _n=np_, _xv=xnbv: e.activation(
                    _xv(), XS[_b]()[0:_n, :], AF.Identity,
                    bias=nmr[0:_n, _B:_B + 1], scale=rstd[0:_n, _B:_B + 1]))
                ln_affine[(_p, q)] = aff_se
                xs_free[g] = (stats_ve, aff_se)

            def emit_ln_back(q, _p=p):
                g = _p * 9 + q
                if q + 3 <= 8:
                    xdma_of[(_p, q + 3)] = emit_x(q + 3)
                pb = q % 2
                np_ = 6 if q == 0 else 128
                xnbv = (lambda: xnbH[:]) if q == 0 else XNB[q - 1]
                wt("tensor", "s_se", ln_affine[(_p, q)])
                wt("tensor", "s_se", bank_free[6 + pb]["s_se"])
                wt("tensor", "s_ve", bank_free[6 + pb]["s_ve"])
                for db in range(8):
                    te(lambda e, S, _db=db, _pb=pb, _n=np_, _xv=xnbv: e.transpose(
                        pkb[_pb][:, _db * 8 if _n == 6 else _db * 128:
                                 (_db * 8 + 6) if _n == 6 else (_db + 1) * 128],
                        _xv()[0:_n, _db * 128:(_db + 1) * 128], idb[0:_n, 0:_n]),
                        inc=(db == 7))
                ln_tp_te[(_p, q)] = cnt["s_te"]
                wt("vector", "s_te", cnt["s_te"])
                if q == 0:
                    ev_ve = ve(lambda e, S: e.tensor_copy(
                        xnT.rearrange("r (db t) -> r db t", db=8)[:, :, 0:3],
                        pkb[0][:, 0:64].rearrange("r (db t) -> r db t", t=8)[:, :, 0:3]))
                    ev_ve = ve(lambda e, S: e.tensor_copy(
                        xnT.rearrange("r (db t) -> r db t", db=8)[:, :, 1027:1030],
                        pkb[0][:, 0:64].rearrange("r (db t) -> r db t", t=8)[:, :, 3:6]))
                else:
                    c = q - 1
                    ev_ve = ve(lambda e, S, _c=c, _pb=pb: e.tensor_copy(
                        xnT.rearrange("r (db t) -> r db t", db=8)
                        [:, :, 3 + _c * 128:3 + _c * 128 + 128],
                        pkb[_pb][:].rearrange("r (db t) -> r db t", db=8)))
                ln_evac[(_p, q)] = ("s_ve", ev_ve)
                bank_free[6 + pb]["s_ve"] = ev_ve

            wb = p % 3
            area_of = lambda i, _wb=wb: (_wb + i) % 3
            if p >= 1:
                for a3 in range(3):
                    wt("gpsimd", wd_sem[a3], area_wfree[a3])
            if att_w_next[0] is not None:
                att_w_sw = att_w_next[0]
                att_w_next[0] = None
            else:
                att_w_sw = emit_attw(wb, cached=(p > 1), wbk=(p == 1))
            sl_att = wb
            w1_sw = {}
            w2_sw = {}
            w1_sw[0] = emit_piece(area_of(0), w1_dmas(area_of(0), 0, 0, p > 0),
                                  w1_wb(area_of(0), 0, 0) if p == 0 else None)

            if p == 0:
                for q0_ in (0, 1, 2):
                    emit_ln_front(q0_)
                emit_ln_back(0)
                emit_ln_back(1)
                emit_ln_front(3)
                emit_ln_back(2)
            else:
                for q0_ in (6, 7, 8):
                    emit_ln_front(q0_)
                for qb_ in (0, 1, 2, 3):
                    emit_ln_back(qb_)
            acc_init_sx = emit_acc_init()

            wt("tensor", "s_wt", att_w_sw[0])
            for ach in range(NACH):
                if ach + 4 <= 8 and (p, ach + 4) not in ln_affine:
                    emit_ln_front(ach + 4)
                nb_ = ach + (3 if p == 0 else 4)
                if nb_ <= 8 and (p, nb_) not in ln_evac:
                    emit_ln_back(nb_)
                qq = min(ach + 2, 8)
                wt("tensor", ln_evac[(p, qq)][0], ln_evac[(p, qq)][1])
                if p == 0 and ach == 0:
                    wt("scalar", "s_ve", connM_ve)
                for dqb in range(8):
                    bk = dqb % 4
                    bank_guard(bk)
                    for hb in range(8):
                        te(lambda e, S, _q=dqb, _h=hb, _a=ach, _sl=sl_att, _bk=bk:
                           e.matmul(pk[_bk][:, 0:ACH],
                                    attwv(_sl, 0, _h, _q * 128, 128),
                                    xnT[:, _h * PTH + 3 + _a * ACH:
                                        _h * PTH + 3 + (_a + 1) * ACH],
                                    start=(_h == 0), stop=(_h == 7)), inc=(hb == 7))
                    wt("scalar", "s_te", cnt["s_te"])
                    ev = se(lambda e, S, _q=dqb, _bk=bk: e.copy(
                        QTV()[:, _q * ACH:(_q + 1) * ACH], pk[_bk][:, 0:ACH]))
                    bank_free[bk]["s_se"] = ev
                for dqb in range(8):
                    bk = dqb % 4
                    bank_guard(bk)
                    for hb in range(8):
                        te(lambda e, S, _q=dqb, _h=hb, _a=ach, _sl=sl_att, _bk=bk:
                           e.matmul(pk[_bk][:, 0:134],
                                    attwv(_sl, 1, _h, _q * 128, 128),
                                    xnT[:, _h * PTH + _a * ACH:
                                        _h * PTH + _a * ACH + 134],
                                    start=(_h == 0), stop=(_h == 7)), inc=(hb == 7))
                    wt("scalar", "s_te", cnt["s_te"])
                    ev = se(lambda e, S, _q=dqb, _bk=bk: e.copy(
                        KTV()[:, _q * 134:(_q + 1) * 134], pk[_bk][:, 0:134]))
                    bank_free[bk]["s_se"] = ev
                newtiles = (0, 1) if ach == 0 else (ach + 1,)
                wt("tensor", "s_wv", att_w_sw[1])
                for gv in newtiles:
                    vb = gv % 3
                    nr = 6 if gv == 8 else 128
                    for dch in range(2):
                        bk = dch
                        bank_guard(bk)
                        for hb in range(8):
                            te(lambda e, S, _g=gv, _h=hb, _d=dch, _sl=sl_att,
                               _bk=bk, _nr=nr:
                               e.matmul(pk[_bk][0:_nr, 0:512],
                                        xnT[:, _h * PTH + _g * 128:
                                            _h * PTH + _g * 128 + _nr],
                                        attwv(_sl, 2, _h, _d * 512, 512),
                                        start=(_h == 0), stop=(_h == 7)),
                               inc=(hb == 7))
                        wt("scalar", "s_te", cnt["s_te"])
                        wt("scalar", "s_te", v3_free_te.get(vb, 0))
                        ev = se(lambda e, S, _vb=vb, _d=dch, _bk=bk, _nr=nr: e.copy(
                            V3[_vb]()[0:_nr, _d * 512:(_d + 1) * 512],
                            pk[_bk][0:_nr, 0:512]))
                        bank_free[bk]["s_se"] = ev

                vb0 = ach % 3
                vb1 = (ach + 1) % 3

                def emit_st(h):
                    sbk = h % 4
                    blk = h // 2
                    po = 64 * (h % 2)
                    bank_guard(sbk)
                    te(lambda e, S, _b=blk, _po=po, _bk=sbk: e.matmul(
                        pk[_bk][:, 0:128],
                        KTV()[_po:_po + 64, _b * 134: _b * 134 + 128],
                        QTV()[_po:_po + 64, _b * 128: _b * 128 + 128],
                        start=True, stop=True), inc=False)
                    return te(lambda e, S, _b=blk, _po=po, _bk=sbk: e.matmul(
                        pk[_bk][0:6, 128:256],
                        KTV()[_po:_po + 64, _b * 134 + 128: _b * 134 + 134],
                        QTV()[_po:_po + 64, _b * 128: _b * 128 + 128],
                        start=True, stop=True))

                st_done = {}
                od_of = {}
                ah_t_te = {}

                def emit_ah_t(db, _ach=ach):
                    _pb = _ach % 2
                    wt("tensor", "s_ve", cnt["s_ve"])
                    if db == 0:
                        wt("tensor", "s_se", bank_free[6 + _pb]["s_se"])
                        wt("tensor", "s_ve", bank_free[6 + _pb]["s_ve"])
                    te(lambda e, S, _db=db, _p2=_pb: e.transpose(
                        pkb[_p2][:, _db * 128:(_db + 1) * 128],
                        AHB()[:, _db * 128:(_db + 1) * 128], idb[:]))
                    ah_t_te[db] = cnt["s_te"]

                def emit_norm(h):
                    obk_ = 4 + h % 2
                    wt("vector", "s_te", od_of[h])
                    if h == 0:
                        wt("vector", "s_te", ah_t_te_prev[0])
                    ve(lambda e, S, _bk=obk_: e.reciprocal(
                        rden[:], pk[_bk][:, 64:65]))
                    ve(lambda e, S, _h=h, _bk=obk_: e.tensor_scalar(
                        AHB()[:, _h * 64:(_h + 1) * 64],
                        pk[_bk][:, 0:64], rden[:, 0:1], None, ALU.mult))
                    bank_read(obk_, "s_ve")

                for h in range(4):
                    st_done[h] = emit_st(h)
                qkv_last_te = st_done[3]
                for h in range(H):
                    sbk = h % 4
                    obk = 4 + h % 2
                    ewb = h % 4
                    wt("scalar", "s_te", st_done[h])
                    wt("scalar", "s_te", ew_free_te[ewb])
                    exp_se = se(lambda e, S, _e=ewb, _bk=sbk: e.activation(
                        EWF[_e](), pk[_bk][:, 0:256], AF.Exp, scale=0.125))
                    bank_free[sbk]["s_se"] = exp_se
                    wt("vector", "s_se", exp_se)
                    ewc_ve = ve(lambda e, S, _e=ewb, _h=h: e.tensor_tensor(
                        EWF[_e](), EWF[_e](), cMall[:, _h * 256:(_h + 1) * 256],
                        ALU.mult))
                    wt("tensor", "s_ve", ewc_ve)
                    bank_guard(obk)
                    te(lambda e, S, _e=ewb, _v=vb0, _h=h, _bk=obk: e.matmul(
                        pk[_bk][:, 0:64], EW1[_e](),
                        V3[_v]()[:, _h * 64:(_h + 1) * 64],
                        start=True, stop=False), inc=False)
                    te(lambda e, S, _e=ewb, _v=vb1, _h=h, _bk=obk: e.matmul(
                        pk[_bk][:, 0:64], EW2[_e](),
                        V3[_v]()[0:6, _h * 64:(_h + 1) * 64],
                        start=False, stop=True), inc=False)
                    te(lambda e, S, _e=ewb, _bk=obk: e.matmul(
                        pk[_bk][:, 64:65], EW1[_e](), ones_c[:],
                        start=True, stop=False), inc=False)
                    od_te = te(lambda e, S, _e=ewb, _bk=obk: e.matmul(
                        pk[_bk][:, 64:65], EW2[_e](), ones_c[0:6, :],
                        start=False, stop=True))
                    ew_free_te[ewb] = od_te
                    v3_free_te[vb0] = od_te
                    v3_free_te[vb1] = od_te
                    od_of[h] = od_te
                    if h + 4 < H:
                        st_done[h + 4] = emit_st(h + 4)
                        qkv_last_te = st_done[h + 4]
                    if h > 0:
                        emit_norm(h - 1)
                    if h >= 3 and h % 2 == 1:
                        emit_ah_t((h - 3) // 2)
                if True:
                    emit_norm(H - 1)
                    emit_ah_t(6)
                    emit_ah_t(7)
                ah_t_te_prev[0] = ah_t_te[7]
                tpb = ach % 2
                wt("scalar", "s_te", ah_t_te[3])
                wt("scalar", "s_te", aht_free_te)
                aht_se1 = se(lambda e, S, _pb=tpb: e.copy(
                    AHT()[:, 0:512], pkb[_pb][:, 0:512]))
                wt("scalar", "s_te", ah_t_te[7])
                aht_se2 = se(lambda e, S, _pb=tpb: e.copy(
                    AHT()[:, 512:1024], pkb[_pb][:, 512:1024]))
                wt("tensor", "s_wt2", att_w_sw[2])
                bank_free[6 + tpb]["s_se"] = aht_se2
                wt("tensor", "s_se", aht_se1)
                bank_guard(4)
                bank_guard(5)
                wt("tensor", "s_wc", n_const_sw)
                for dqb in range(8):
                    if dqb == 4:
                        wt("tensor", "s_se", aht_se2)
                    for dch in range(2):
                        te(lambda e, S, _q=dqb, _d=dch, _sl=sl_att: e.matmul(
                            pk[4 + _d][:, 0:512],
                            AHT()[:, _q * 128:(_q + 1) * 128],
                            attwv(_sl, 3, _q, _d * 512, 512),
                            start=(_q == 0), stop=False), inc=False)
                for dch in range(2):
                    te(lambda e, S, _d=dch: e.matmul(
                        pk[4 + _d][:, 0:512], sel9[:, 8 * 128:9 * 128],
                        b2bo[:, _d * 512:(_d + 1) * 512],
                        start=False, stop=True), inc=(dch == 1))
                aht_free_te = cnt["s_te"]
                wt("vector", "s_te", aht_free_te)
                wt("vector", "s_xa", acc_init_sx)
                for dch in range(2):
                    ve(lambda e, S, _tb=ach, _d=dch: e.tensor_add(
                        acc[:, _tb * D + _d * 512: _tb * D + (_d + 1) * 512],
                        acc[:, _tb * D + _d * 512: _tb * D + (_d + 1) * 512],
                        pk[4 + _d][:, 0:512]))
                bank_read(4, "s_ve")
                bank_read(5, "s_ve")
            att_te_end = cnt["s_te"]
            area_prev[(wb + 1) % 3] = att_te_end
            area_prev[(wb + 2) % 3] = att_te_end
            wt("tensor", ln_evac[(p, 8)][0], ln_evac[(p, 8)][1])
            bank_guard(5)
            for tb in range(8):
                for hb in range(8):
                    te(lambda e, S, _tb=tb, _hb=hb: e.matmul(
                        pk[5][:, _tb * E:(_tb + 1) * E],
                        xnT[:, _hb * PTH + 3 + _tb * 128: _hb * PTH + 3 + (_tb + 1) * 128],
                        rwb[:, _hb * E:(_hb + 1) * E],
                        start=(_hb == 0), stop=(_hb == 7)), inc=(hb == 7))
            lg_te = cnt["s_te"]
            wt("vector", "s_te", lg_te)
            for tb in range(8):
                ve(lambda e, S, _tb=tb: e.max(
                    out=m8[:, _tb * 8:(_tb + 1) * 8],
                    in_=pk[5][:, _tb * E:(_tb + 1) * E]))
            m3 = lambda c: (m8[:].rearrange("p (t c) -> p t c", t=8)
                            [:, :, c:c + 1].broadcast_to([128, 8, 8]))
            ve(lambda e, S: e.tensor_sub(
                g2c[:].rearrange("p t -> p t ()"),
                m8[:].rearrange("p (t c) -> p t c", t=8)[:, :, 1:2],
                m8[:].rearrange("p (t c) -> p t c", t=8)[:, :, 0:1]))
            sub_ve = cnt["s_ve"]
            wt("scalar", "s_ve", sub_ve)
            sig_se = se(lambda e, S: e.activation(g2c[:], g2c[:], AF.Sigmoid))
            wt("vector", "s_se", sig_se)
            ve(lambda e, S: e.tensor_scalar(
                g1c[:], g2c[:], -1.0, 1.0, ALU.mult, ALU.add))
            p5t = lambda: pk[5][:, 0:64].rearrange("p (t c) -> p t c", t=8)
            gv = lambda g: g[:].unsqueeze(2).broadcast_to([128, 8, 8])
            gt3 = lambda t: t[:].rearrange("p (t c) -> p t c", t=8)
            ve(lambda e, S: e.tensor_tensor(gt3(gtmp), p5t(), m3(0), ALU.is_equal))
            ve(lambda e, S: e.tensor_tensor(gt3(gtmp), gt3(gtmp), gv(g1c), ALU.mult))
            ve(lambda e, S: e.tensor_tensor(gt3(gtm2), p5t(), m3(1), ALU.is_equal))
            ve(lambda e, S: e.tensor_tensor(gt3(gtm2), gt3(gtm2), gv(g2c), ALU.mult))
            ve(lambda e, S: e.tensor_add(Gs[:], gtmp[:], gtm2[:]))
            ve(lambda e, S: e.tensor_scalar(
                maskf[:], Gs[:], 0.0, None, ALU.is_gt))
            mb_ve = ve(lambda e, S: e.tensor_copy(maskb[:], maskf[:]))
            wt("tensor", "s_ve", mb_ve)
            bank_guard(4)
            for kk in range(8):
                te(lambda e, S, _k=kk: e.matmul(
                    pk[4][:, _k * 8:(_k + 1) * 8], ltri[:],
                    maskb[:, _k * 8:(_k + 1) * 8],
                    start=True, stop=(_k == 0)), inc=False)
                for jj in range(kk):
                    te(lambda e, S, _k=kk, _j=jj: e.matmul(
                        pk[4][:, _k * 8:(_k + 1) * 8], ones128[:],
                        maskb[:, _j * 8:(_j + 1) * 8],
                        start=False, stop=(_j == _k - 1)),
                       inc=(kk == 7 and jj == 6))
            pos_te = cnt["s_te"]
            wt("vector", "s_te", pos_te)
            poss_ve = ve(lambda e, S: e.tensor_copy(poss[:], pk[4][:, 0:64]))
            bank_read(4, "s_ve")

            pbuild_ve = {}

            def emit_pbuild(e_, after_te):
                wt("vector", "s_te", after_te)
                for t in range(8):
                    ve(lambda e, S, _t=t, _e=e_: e.tensor_scalar(
                        PV(_t), iotaM[:],
                        poss[:, _t * 8 + _e:_t * 8 + _e + 1],
                        maskf[:, _t * 8 + _e:_t * 8 + _e + 1],
                        ALU.is_equal, ALU.mult))
                pbuild_ve[e_] = cnt["s_ve"]

            emit_pbuild(0, 0)

            if p + 1 < NPASS:
                for q2 in (0, 1, 2):
                    g2 = (p + 1) * 9 + q2
                    buf2 = g2 % 3
                    vefree, sefree = xs_free[g2 - 3]
                    wt("sync", "s_ve", vefree)
                    wt("sync", "s_se", sefree)
                    if q2 == 0:
                        op("sync", lambda e, S, _p=p + 1, _b=buf2: e.dma_start(
                            out=XS[_b]()[0:3, :],
                            in_=x[_p * PT: _p * PT + 3, :]), f"s_xs{buf2}", 16)
                        op("sync", lambda e, S, _p=p + 1, _b=buf2: e.dma_start(
                            out=XS[_b]()[3:6, :],
                            in_=x[_p * PT + 1027: _p * PT + 1030, :]),
                            f"s_xs{buf2}", 16)
                    else:
                        c2 = q2 - 1
                        op("sync", lambda e, S, _p=p + 1, _c=c2, _b=buf2: e.dma_start(
                            out=XS[_b](),
                            in_=x[_p * PT + 3 + _c * 128:
                                  _p * PT + 3 + (_c + 1) * 128, :]),
                            f"s_xs{buf2}", 16)
                    pre_x[(p + 1, q2)] = cnt[f"s_xs{buf2}"]
                    xdma_of[(p + 1, q2)] = pre_x[(p + 1, q2)]

            w2_sw[0] = emit_piece(area_of(1), w2_dmas(area_of(1), 0, 0, p > 1),
                                  w2_wb(area_of(1), 0, 0) if p == 1 else None)
            w1_sw[1] = emit_piece(area_of(2), w1_dmas(area_of(2), 0, 1, p > 0),
                                  w1_wb(area_of(2), 0, 1) if p == 0 else None)

            comp_te = {}
            for e_ in range(E):
                wt("tensor", "s_ve", pbuild_ve[e_])
                ptv_se = 0
                for sc in range(3):
                    pb = sc % 2
                    ns = min(CAP, (sc + 1) * 128) - sc * 128
                    wt("tensor", "s_se", bank_free[6 + pb]["s_se"])
                    wt("tensor", "s_ve", bank_free[6 + pb]["s_ve"])
                    for t in range(8):
                        te(lambda e, S, _t=t, _sc=sc, _pb=pb, _n=ns: e.transpose(
                            pkb[_pb][0:_n, _t * 128:(_t + 1) * 128],
                            PV(_t)[:, _sc * 128:_sc * 128 + _n], idb[:]),
                            inc=(t == 7))
                    tp_te = cnt["s_te"]
                    wt("scalar", "s_te", tp_te)
                    ptv_se = se(lambda e, S, _sc=sc, _pb=pb, _n=ns: e.activation(
                        PTV(_sc)[0:_n, :], pkb[_pb][0:_n, :], AF.Copy))
                    bank_free[6 + pb]["s_se"] = ptv_se
                for dc in range(8):
                    bk = dc % 2
                    bank_guard(bk)
                    for t in range(8):
                        te(lambda e, S, _t=t, _dc=dc, _bk=bk: e.matmul(
                            pk[_bk][:, 0:CAP],
                            XNB[_t]()[:, _dc * 128:(_dc + 1) * 128],
                            PV(_t), start=(_t == 0), stop=(_t == 7)),
                            inc=(t == 7))
                    cte = cnt["s_te"]
                    wt("scalar", "s_te", cte)
                    se(lambda e, S, _dc=dc, _bk=bk: e.activation(
                        XGV(_dc), pk[_bk][:, 0:CAP], AF.Copy))
                    bank_read(bk, "s_se")
                comp_te[e_] = cnt["s_te"]
                xg_se = cnt["s_se"]
                if e_ + 1 < E:
                    emit_pbuild(e_ + 1, comp_te[e_])
                else:
                    xnb_pass_free[0] = comp_te[e_]
                    if p + 1 < NPASS:
                        def pre_x_late(q3, _p=p + 1):
                            g3 = _p * 9 + q3
                            b3 = g3 % 3
                            vf3, sf3 = xs_free[g3 - 3]
                            wt("sync", "s_ve", vf3)
                            wt("sync", "s_se", sf3)
                            c3 = q3 - 1
                            op("sync", lambda e, S, _pp=_p, _c=c3, _b=b3:
                               e.dma_start(
                                out=XS[_b](),
                                in_=x[_pp * PT + 3 + _c * 128:
                                      _pp * PT + 3 + (_c + 1) * 128, :]),
                                f"s_xs{b3}", 16)
                            pre_x[(_p, q3)] = cnt[f"s_xs{b3}"]
                            xdma_of[(_p, q3)] = pre_x[(_p, q3)]
                        for qf in (0, 1, 2):
                            emit_ln_front(qf, _p=p + 1)
                        for q3 in (3, 4, 5):
                            pre_x_late(q3)
                        for qf in (3, 4, 5):
                            emit_ln_front(qf, _p=p + 1)
                        for q3 in (6, 7, 8):
                            pre_x_late(q3)
                for phi in range(2):
                    k = e_ * 2 + phi
                    ar1 = area_of(2 * k)
                    ar2 = area_of(2 * k + 1)
                    wt("tensor", w1_sw[k][0], w1_sw[k][1])
                    if phi == 0:
                        wt("tensor", "s_se", xg_se)
                    gelu_of = {}
                    for fb in range(NFB):
                        abk = fb % 2
                        bank_guard(abk)
                        for hb in range(8):
                            te(lambda e, S, _sl=ar1, _h=hb, _f=fb, _bk=abk:
                               e.matmul(pk[_bk][:, 0:CAP],
                                        w1v(_sl, _h, _f), XGV(_h),
                                        start=(_h == 0), stop=(_h == 7)),
                               inc=(hb == 7))
                        mm1_te = cnt["s_te"]
                        wt("scalar", "s_te", mm1_te)
                        gse = se(lambda e, S, _f=fb, _bk=abk, _e=e_, _phi=phi:
                                 e.activation(
                            H1V(_f), pk[_bk][:, 0:CAP], AF.Gelu,
                            bias=b1s[:, _e * 32 + _phi * 16 + _f:
                                     _e * 32 + _phi * 16 + _f + 1]))
                        gelu_of[fb] = gse
                        bank_free[abk]["s_se"] = gse
                    mm1_end = cnt["s_te"]
                    wt("tensor", w2_sw[k][0], w2_sw[k][1])
                    for sb in (2, 0, 1):
                        bks = (2 + 2 * sb, 3 + 2 * sb)
                        if phi == 0:
                            bank_guard(bks[0])
                            bank_guard(bks[1])
                        wt("tensor", "s_se", gelu_of[13])
                        for fb in range(NFB):
                            if fb >= 14:
                                wt("tensor", "s_se", gelu_of[fb])
                            for dch in range(2):
                                if sb == 2:
                                    outap = (lambda _d=dch: pkb[_d][:]
                                             .bitcast(F32)[:, 0:512])
                                else:
                                    outap = (lambda _d=dch, _b=bks:
                                             pk[_b[_d]][:, 0:512])
                                nsb = min(CAP, (sb + 1) * 128) - sb * 128
                                te(lambda e, S, _sl=ar2, _f=fb, _sb=sb, _d=dch,
                                   _o=outap, _phi=phi, _n=nsb: e.matmul(
                                    _o()[0:_n, :],
                                    H1V(_f)[:, _sb * 128:_sb * 128 + _n],
                                    w2v(_sl, _f, _d),
                                    start=(_phi == 0 and _f == 0), stop=False),
                                   inc=(phi == 0 and fb == NFB - 1 and dch == 1))
                        if phi == 1:
                            for dch in range(2):
                                if sb == 2:
                                    outap = (lambda _d=dch: pkb[_d][:]
                                             .bitcast(F32)[:, 0:512])
                                else:
                                    outap = (lambda _d=dch, _b=bks:
                                             pk[_b[_d]][:, 0:512])
                                nsb = min(CAP, (sb + 1) * 128) - sb * 128
                                te(lambda e, S, _e=e_, _d=dch, _o=outap, _n=nsb:
                                   e.matmul(
                                    _o()[0:_n, :],
                                    sel9[:, _e * 128:_e * 128 + _n],
                                    b2bo[:, _d * 512:(_d + 1) * 512],
                                    start=False, stop=True), inc=(dch == 1))
                            mm2_sb_te = cnt["s_te"]
                            wt("scalar", "s_te", mm2_sb_te)
                            for dch in range(2):
                                if sb == 2:
                                    inap = (lambda _d=dch: pkb[_d][:]
                                            .bitcast(F32)[:, 0:512])
                                else:
                                    inap = (lambda _d=dch, _b=bks:
                                            pk[_b[_d]][:, 0:512])
                                nsb = min(CAP, (sb + 1) * 128) - sb * 128
                                se(lambda e, S, _sb=sb, _d=dch, _i=inap, _n=nsb:
                                   e.activation(
                                    O2V(_sb)[0:_n, _d * 512:(_d + 1) * 512],
                                    _i()[0:_n, :], AF.Copy))
                            bank_read(bks[0], "s_se")
                            bank_read(bks[1], "s_se")
                    area_prev[ar1] = mm1_end
                    area_prev[ar2] = cnt["s_te"]
                    if k + 1 < 16:
                        kk = k + 1
                        w2_sw[kk] = emit_piece(
                            area_of(2 * kk + 1),
                            w2_dmas(area_of(2 * kk + 1), kk // 2, kk % 2, p > 1),
                            w2_wb(area_of(2 * kk + 1), kk // 2, kk % 2)
                            if p == 1 else None)
                    if k + 2 < 16:
                        kk = k + 2
                        w1_sw[kk] = emit_piece(
                            area_of(2 * kk),
                            w1_dmas(area_of(2 * kk), kk // 2, kk % 2, p > 0),
                            w1_wb(area_of(2 * kk), kk // 2, kk % 2)
                            if p == 0 else None)
                    if k == 15 and p + 1 < NPASS:
                        att_w_next[0] = emit_attw(
                            (p + 1) % 3, cached=(p + 1 > 1), wbk=(p + 1 == 1))
                o2_se = cnt["s_se"]
                wt("tensor", "s_se", o2_se)
                for t in range(8):
                    sl = (3, 0, 1, 2, 0, 1, 2, 0)[t]
                    bks = (2 * sl, 2 * sl + 1)
                    bank_guard(bks[0])
                    bank_guard(bks[1])
                    if sl == 3:
                        slotap = (lambda _d: (lambda: pkb[_d][:]
                                              .bitcast(F32)[:, 0:512]))
                    else:
                        slotap = (lambda _d, _b=bks: (lambda: pk[_b[_d]][:, 0:512]))
                    for sc in range(3):
                        nsc = min(CAP, (sc + 1) * 128) - sc * 128
                        for dch in range(2):
                            te(lambda e, S, _t=t, _sc=sc, _d=dch, _o=slotap(dch),
                               _n=nsc: e.matmul(
                                _o(),
                                PTV(_sc)[0:_n, _t * 128:(_t + 1) * 128],
                                O2V(_sc)[0:_n, _d * 512:(_d + 1) * 512],
                                start=(_sc == 0), stop=(_sc == 2)),
                               inc=(sc == 2 and dch == 1))
                    st_te = cnt["s_te"]
                    wt("vector", "s_te", st_te)
                    wt("vector", "s_xa", acc_init_sx)
                    for dch in range(2):
                        ve(lambda e, S, _t=t, _d=dch, _e=e_, _i=slotap(dch):
                           e.scalar_tensor_tensor(
                            acc[:, _t * D + _d * 512: _t * D + (_d + 1) * 512],
                            _i(),
                            Gs[:, _t * E + _e: _t * E + _e + 1],
                            acc[:, _t * D + _d * 512: _t * D + (_d + 1) * 512],
                            ALU.mult, ALU.add))
                    bank_read(bks[0], "s_ve")
                    bank_read(bks[1], "s_ve")
            if p <= 1:
                flush_wb(0)
            pass_ve_end = cnt["s_ve"]

            wt("sync", "s_ve", pass_ve_end)
            op("sync", lambda e, S, _p=p: e.dma_start(
                out=out_t[_p * PT:(_p + 1) * PT, :]
                    .rearrange("(tb q) d -> q tb d", q=128),
                in_=acc[:].rearrange("q (tb d) -> q tb d", tb=8)), "s_out", 16)

        wt("sync", "s_out", cnt["s_out"])

        def run_body(lst):
            def body(e):
                for f in lst:
                    f(e, S)
            return body
        block.sync(run_body(ops["sync"]))
        block.gpsimd(run_body(ops["gpsimd"]))
        block.tensor(run_body(ops["tensor"]))
        block.scalar(run_body(ops["scalar"]))
        block.vector(run_body(ops["vector"]))

    return nc


def host_inputs(inputs):
    f32 = np.float32
    bf16 = ml_dtypes.bfloat16
    xf = np.asarray(inputs["L0_attn_x"], f32)
    shared = {
        "w1T": np.ascontiguousarray(np.asarray(inputs["L0_ffn_w1"], f32).transpose(0, 2, 1)),
        "w2T": np.ascontiguousarray(np.asarray(inputs["L0_ffn_w2"], f32).transpose(0, 2, 1)),
        "wqT": np.ascontiguousarray(np.asarray(inputs["L0_attn_wq"], f32).T),
        "wkT": np.ascontiguousarray(np.asarray(inputs["L0_attn_wk"], f32).T),
        "wvT": np.ascontiguousarray(np.asarray(inputs["L0_attn_wv"], f32).T),
        "woT": np.ascontiguousarray(np.asarray(inputs["L0_attn_wo"], f32).T),
        "rwT": np.ascontiguousarray(np.asarray(inputs["L0_ffn_router_w"], f32).T),
        "b1T": np.ascontiguousarray(
            np.asarray(inputs["L0_ffn_b1"], f32).reshape(E, 32, 128).transpose(0, 2, 1)),
        "b2i": np.ascontiguousarray(np.asarray(inputs["L0_ffn_b2"], f32)),
        "boi": np.ascontiguousarray(np.asarray(inputs["L0_attn_bo"], f32)[None, :]),
        "cw1t": np.ascontiguousarray(
            np.repeat(np.asarray(inputs["L0_attn_cw1"], f32)[:, :, 0], WIN, axis=0)),
        "cb1t": np.ascontiguousarray(
            np.repeat(np.asarray(inputs["L0_attn_cb1"], f32), WIN, axis=0)),
        "cw2f": np.ascontiguousarray(
            np.repeat(np.asarray(inputs["L0_attn_cw2"], f32)
                      .transpose(0, 2, 1).reshape(H, 1024), WIN, axis=0)),
        "cb2t": np.ascontiguousarray(
            np.repeat(np.asarray(inputs["L0_attn_cb2"], f32), WIN, axis=0)),
        "cw3s": np.ascontiguousarray(
            np.repeat(np.asarray(inputs["L0_attn_cw3"], f32)[:, 0, :], WIN, axis=0)),
        "cb3s": np.ascontiguousarray(
            np.repeat(np.asarray(inputs["L0_attn_cb3"], f32), WIN, axis=0)),
        "post": np.ascontiguousarray(
            np.tile(np.linspace(0.0, 1.0, WIN, dtype=f32), H)[:, None]),
        "identb": np.eye(128, dtype=bf16),
        "identf": np.eye(128, dtype=f32),
        "onescol_b": np.ones((128, 1), dtype=bf16),
        "onesrow_f": np.ones((1, 128), dtype=f32),
        "iotaM_i": np.ascontiguousarray(
            np.tile(np.arange(1, CAP + 1, dtype=f32)[None, :], (128, 1))),
        "ltri_i": np.triu(np.ones((128, 128))).astype(bf16),
        "ones128_i": np.ones((128, 128), dtype=bf16),
        "sel9_i": np.ascontiguousarray(np.kron(np.eye(9), np.ones((1, 128))).astype(bf16)),
    }
    r = np.arange(128)[:, None]
    j = np.arange(128)[None, :]
    D1m = np.stack([(r - j == w).astype(f32) for w in range(WIN)])
    r2 = np.arange(6)[:, None]
    D2m = np.stack([(128 + r2 - j == w).astype(f32) for w in range(WIN)])
    shared["D1"] = np.ascontiguousarray(D1m)
    shared["D2"] = np.ascontiguousarray(D2m)

    in_maps = []
    for c in range(8):
        b, s = c // 2, c % 2
        xc = np.zeros((XROWS, D), f32)
        lo = s * 4096 - 3
        hi = s * 4096 + 4096 + 3
        vlo, vhi = max(lo, 0), min(hi, 8192)
        xc[vlo - lo: vhi - lo] = xf[b, vlo:vhi]
        m = dict(shared)
        m["x"] = xc
        in_maps.append(m)
    return in_maps


def kernel(**inputs):
    in_maps = host_inputs(inputs)
    nc = build_program()
    res = run_bass_kernel_spmd(nc, in_maps, core_ids=list(range(8)))
    out = np.empty((4, 8192, D), np.float32)
    for c in range(8):
        b, s = c // 2, c % 2
        out[b, s * 4096:(s + 1) * 4096] = np.asarray(res.results[c]["out"])
    return out


# version 19 (final)
# speedup vs baseline: 1.2019x; 1.0107x over previous
import numpy as np
import ml_dtypes
import concourse.bass as bass
import concourse.mybir as mybir
from concourse.bass_utils import run_bass_kernel_spmd
from contextlib import ExitStack

F32 = mybir.dt.float32
BF16 = mybir.dt.bfloat16
AF = mybir.ActivationFunctionType
ALU = mybir.AluOpType

D = 1024
H = 16
DH = 64
WIN = 7
NPASS = 4
PT = 1024
PTH = PT + 6
CORE_T = NPASS * PT
ACH = 128
NACH = PT // ACH
E = 8
FF = 4096
FH = 2048
TH = 512
NFB = 16
XROWS = CORE_T + 6 + 122
CAP = 352

ENGS = ("sync", "gpsimd", "tensor", "scalar", "vector")


def build_program():
    nc = bass.Bass("TRN2", target_bir_lowering=False, debug=False)

    def din(name, shape, dt=F32):
        return nc.dram_tensor(name, shape, dt, kind="ExternalInput").ap()

    x = din("x", [XROWS, D])
    w1T = din("w1T", [E, D, FF])
    w2T = din("w2T", [E, FF, D])
    wqT = din("wqT", [D, D])
    wkT = din("wkT", [D, D])
    wvT = din("wvT", [D, D])
    woT = din("woT", [D, D])
    rwT = din("rwT", [D, E])
    b1T = din("b1T", [E, 128, 32])
    b2i = din("b2i", [E, D])
    boi = din("boi", [1, D])
    cw1t = din("cw1t", [112, 32])
    cb1t = din("cb1t", [112, 32])
    cw2f = din("cw2f", [112, 1024])
    cb2t = din("cb2t", [112, 32])
    cw3s = din("cw3s", [112, 32])
    cb3s = din("cb3s", [112, 1])
    post = din("post", [112, 1])
    D1 = din("D1", [WIN, 128, 128])
    D2 = din("D2", [WIN, 6, 128])
    identb = din("identb", [128, 128], BF16)
    identf = din("identf", [128, 128])
    onescol_b = din("onescol_b", [128, 1], BF16)
    onesrow_f = din("onesrow_f", [1, 128])
    iotaM_i = din("iotaM_i", [128, CAP])
    ltri_i = din("ltri_i", [128, 128], BF16)
    sel9_i = din("sel9_i", [9, 9 * 128], BF16)
    ones128_i = din("ones128_i", [128, 128], BF16)
    ones_b_row_src = onesrow_f

    out_t = nc.dram_tensor("out", [CORE_T, D], F32, kind="ExternalOutput").ap()
    w1d = nc.dram_tensor("w1d", [16, 128, 16384], BF16, kind="Internal").ap()
    w2d = nc.dram_tensor("w2d", [16, 128, 16384], BF16, kind="Internal").ap()
    attd = nc.dram_tensor("attd", [4, 128, 8192], BF16, kind="Internal").ap()

    ops = {e: [] for e in ENGS}
    cnt = {s: 0 for s in ("s_xc", "s_xs0", "s_xs1", "s_xs2", "s_xs3", "s_xa", "s_xb", "s_out",
                          "s_wc", "s_wa0", "s_wa1", "s_wa2", "s_wt", "s_wt2", "s_wv",
                          "s_wd0", "s_wd1", "s_wd2", "s_xd",
                          "s_te", "s_se", "s_ve")}
    hiw = {e: {s: 0 for s in cnt} for e in ENGS}

    def wt(eng, sem, val):
        if val is None or val <= 0 or val <= hiw[eng][sem]:
            return
        hiw[eng][sem] = val
        def _w(e, S, _s=sem, _v=val):
            e.wait_ge(S[_s], _v)
        ops[eng].append(_w)

    def op(eng, fn, inc_sem=None, inc_by=1):
        if inc_sem is not None:
            cnt[inc_sem] += inc_by
        def _f(e, S, _fn=fn, _i=inc_sem, _b=inc_by):
            ins = _fn(e, S)
            if _i is not None:
                ins.then_inc(S[_i], _b)
        ops[eng].append(_f)
        return cnt[inc_sem] if inc_sem else None

    def ve(fn):
        wt("vector", "s_ve", cnt["s_ve"])
        return op("vector", fn, "s_ve")

    def se(fn):
        wt("scalar", "s_se", cnt["s_se"])
        return op("scalar", fn, "s_se")

    def te(fn, inc=True):
        return op("tensor", fn, "s_te" if inc else None)

    bank_free = {b: {"s_se": 0, "s_ve": 0} for b in range(8)}

    def bank_guard(b):
        wt("tensor", "s_se", bank_free[b]["s_se"])
        wt("tensor", "s_ve", bank_free[b]["s_ve"])

    def bank_read(b, sem):
        bank_free[b][sem] = cnt[sem]

    with ExitStack() as ctx:
        En = ctx.enter_context
        acc = En(nc.sbuf_tensor("acc", [128, 8 * D], F32))
        slot0 = En(nc.sbuf_tensor("slot0", [128, 32768], BF16))
        slot1 = En(nc.sbuf_tensor("slot1", [128, 16384], BF16))
        scr = En(nc.sbuf_tensor("scr", [128, 16480], BF16))
        b1s = En(nc.sbuf_tensor("b1s", [128, E * 32], F32))
        b2bo = En(nc.sbuf_tensor("b2bo", [9, D], BF16))
        sel9 = En(nc.sbuf_tensor("sel9", [9, 9 * 128], BF16))
        rwb = En(nc.sbuf_tensor("rwb", [128, 8 * E], BF16))
        idb = En(nc.sbuf_tensor("idb", [128, 128], BF16))
        idf = En(nc.sbuf_tensor("idf", [112, 112], F32))
        ones_c = En(nc.sbuf_tensor("ones_c", [128, 1], BF16))
        cMall = En(nc.sbuf_tensor("cMall", [128, H * 256], BF16))
        Gs = En(nc.sbuf_tensor("Gs", [128, 8 * E], F32))
        m8 = En(nc.sbuf_tensor("m8", [128, 64], F32))
        poss = m8
        g2c = En(nc.sbuf_tensor("g2c", [128, 8], F32))
        g1c = En(nc.sbuf_tensor("g1c", [128, 8], F32))
        gtmp = En(nc.sbuf_tensor("gtmp", [128, 64], F32))
        bnst = En(nc.sbuf_tensor("bnst", [128, 36], F32))
        bnagg = En(nc.sbuf_tensor("bnagg", [128, 6], F32))
        rstd = En(nc.sbuf_tensor("rstd", [128, 3], F32))
        nmr = En(nc.sbuf_tensor("nmr", [128, 3], F32))
        lnt = En(nc.sbuf_tensor("lnt", [128, 3], F32))
        rden = En(nc.sbuf_tensor("rden", [128, 1], F32))
        xs_t = [En(nc.sbuf_tensor(f"xs{i}", [128, D], F32)) for i in range(3)]
        xnb_t = [En(nc.sbuf_tensor(f"xnb{i}", [128, D], BF16)) for i in range(8)]
        xnbH = En(nc.sbuf_tensor("xnbH", [6, D], BF16))
        iotaM = En(nc.sbuf_tensor("iotaM", [128, CAP], F32))
        ltri = En(nc.sbuf_tensor("ltri", [128, 128], BF16))
        ones128 = En(nc.sbuf_tensor("ones128", [128, 128], BF16))
        maskf = En(nc.sbuf_tensor("maskf", [128, 64], F32))
        gtm2 = maskf
        maskb = En(nc.sbuf_tensor("maskb", [128, 64], BF16))

        pk = [En(nc.psum_tensor(f"pk{i}", [128, 512], F32)) for i in range(6)]
        pkb = [En(nc.psum_tensor(f"pkb{i}", [128, 1024], BF16)) for i in range(2)]

        S = {s: En(nc.semaphore(s)) for s in cnt}
        block = En(nc.Block())

        def scr_f32(col, n):
            return scr[:, col:col + 2 * n].bitcast(F32)
        XS = [lambda i=i: xs_t[i][:] for i in range(3)]
        xnT = None
        def _cw(col, ncols_f32, np_=H):
            return lambda: scr[0:np_, col:col + 2 * ncols_f32].bitcast(F32)
        c_cw1 = _cw(0, 32, 112)
        c_cb1 = _cw(64, 32, 112)
        c_cw2 = _cw(128, 1024, 112)
        c_cb2 = _cw(2176, 32, 112)
        c_cw3 = _cw(2240, 32, 112)
        c_cb3 = _cw(2304, 8, 112)
        c_pos = _cw(2320, 8, 112)
        c_h1 = _cw(2336, 32, 112)
        c_h2 = _cw(2400, 32, 112)
        c_cwt = _cw(2464, 8, 112)
        c_bn = _cw(2480, 8, 112)
        c_row_t = scr[0:1, 4992:5216].bitcast(F32)
        XNB = [lambda i=i: xnb_t[i][:] for i in range(8)]
        D1V = lambda: scr_f32(5480, 896)
        D2V = lambda: scr[0:6, 0:1792].bitcast(F32)
        xnT = scr[:, 0:8240]
        A0 = 8240
        QTV = lambda: scr[:, A0:A0 + 1024]
        KTV = lambda: scr[:, A0 + 1024:A0 + 1024 + 8 * 134]
        V3 = [lambda k=k: scr[:, A0 + 2096 + 1024 * k:A0 + 2096 + 1024 * (k + 1)]
              for k in range(3)]
        E128 = scr[:, A0:A0 + 224].bitcast(F32)
        ones_rf = scr[0:1, A0 + 224:A0 + 480].bitcast(F32)
        AHB = lambda: scr[:, A0 + 5168:A0 + 6192]
        AHT = lambda: scr[:, A0 + 6192:A0 + 7216]
        EWF = [lambda k=k: scr[:, A0 + 7216 + 256 * k:A0 + 7472 + 256 * k] for k in range(4)]
        EW1 = [lambda k=k: scr[:, A0 + 7216 + 256 * k:A0 + 7344 + 256 * k] for k in range(4)]
        EW2 = [lambda k=k: scr[0:6, A0 + 7344 + 256 * k:A0 + 7472 + 256 * k] for k in range(4)]
        H1V = lambda fb: scr[:, fb * CAP:(fb + 1) * CAP]
        XGV = lambda hb: scr[:, 6144 + hb * CAP:6144 + (hb + 1) * CAP]
        PV = lambda t: scr[:, 9216 + t * CAP:9216 + (t + 1) * CAP]
        PTV = lambda sc: scr[:, 12288 + sc * PT:12288 + (sc + 1) * PT]
        O2V = lambda sb: scr[:, 6144 + sb * D:6144 + (sb + 1) * D]

        AREAS = [(slot0, 0), (slot0, 16384), (slot1, 0)]

        def w1v(ar, hb, fb):
            t, off = AREAS[ar]
            return t[:, off + hb * 2048 + fb * 128: off + hb * 2048 + (fb + 1) * 128]

        def w2v(ar, fb, dch):
            t, off = AREAS[ar]
            return t[:, off + fb * 1024 + dch * 512: off + fb * 1024 + (dch + 1) * 512]

        def attwv(wb, which, hb, c0, n):
            ar = (wb + 1) % 3 if which < 2 else (wb + 2) % 3
            t, off = AREAS[ar]
            base = off + (which % 2) * 8192 + hb * 1024
            return t[:, base + c0: base + c0 + n]

        cdmas = [
            (lambda: c_cw1(), cw1t), (lambda: c_cb1(), cb1t),
            (lambda: c_cw2(), cw2f), (lambda: c_cb2(), cb2t),
            (lambda: c_cw3(), cw3s), (lambda: c_cb3()[:, 0:1], cb3s),
            (lambda: c_pos()[:, 0:1], post),
            (lambda: idf[:], identf[0:112, 0:112]),
            (lambda: ones_rf[:], onesrow_f),
            (lambda: D1V().rearrange("p (w j) -> p w j", w=WIN), D1.rearrange("w r j -> r w j")),
        ]
        for dstf, src in cdmas:
            op("sync", lambda e, S, _d=dstf, _s=src: e.dma_start(out=_d(), in_=_s),
               "s_xb", 16)
        n_conn_sx = cnt["s_xb"]
        cdmas2 = [
            (lambda: b1s[:].rearrange("p (e c) -> p e c", e=E), b1T.transpose([1, 0, 2])),
            (lambda: idb[:], identb),
            (lambda: ones_c[:], onescol_b),
            (lambda: iotaM[:], iotaM_i),
            (lambda: ltri[:], ltri_i),
            (lambda: ones128[:], ones128_i),
        ]
        for dstf, src in cdmas2:
            op("sync", lambda e, S, _d=dstf, _s=src: e.dma_start(out=_d(), in_=_s),
               "s_xc", 16)
        n_const_sx = cnt["s_xc"]

        for dstf, src in ((lambda: rwb[:].rearrange("p (hb e) -> p hb e", hb=8), rwT.rearrange("(hb p) e -> p hb e", p=128)),
                          (lambda: b2bo[0:8, :], b2i), (lambda: b2bo[8:9, :], boi),
                          (lambda: sel9[:], sel9_i)):
            op("gpsimd", lambda e, S, _d=dstf, _s=src: e.dma_start(out=_d(), in_=_s),
               "s_wc", 16)
        n_const_sw = cnt["s_wc"]

        area_prev = [0, 0, 0]
        area_sem = ["s_wa0", "s_wa1", "s_wa2"]
        wd_sem = ["s_wd0", "s_wd1", "s_wd2"]
        area_wfree = [0, 0, 0]
        pend_wb = []

        def flush_wb(n_keep=0):
            while len(pend_wb) > n_keep:
                ar, lsem, lval, dsts = pend_wb.pop(0)
                wt("gpsimd", lsem, lval)
                for dst, view in dsts:
                    op("gpsimd", lambda e, S, _d=dst, _v=view: e.dma_start(
                        out=_d, in_=_v()), wd_sem[ar], 16)
                area_wfree[ar] = cnt[wd_sem[ar]]

        def emit_piece(ar, dmas, wb_dsts=None):
            flush_wb(0)
            wt("gpsimd", "s_te", area_prev[ar])
            wt("gpsimd", wd_sem[ar], area_wfree[ar])
            sem = area_sem[ar]
            for dstf, srcf in dmas:
                op("gpsimd", lambda e, S, _d=dstf, _s=srcf: e.dma_start(out=_d(), in_=_s),
                   sem, 16)
            if wb_dsts is not None:
                pend_wb.append((ar, sem, cnt[sem], wb_dsts))
            return (sem, cnt[sem])

        def w1_dmas(ar, e_, phi, cached):
            t, off = AREAS[ar]
            if cached:
                return [(lambda _t=t, _o=off: _t[:, _o:_o + 16384],
                         w1d[2 * e_ + phi])]
            return [(lambda _t=t, _o=off: _t[:, _o:_o + 16384]
                     .rearrange("p (hb f) -> p hb f", hb=8),
                     w1T[e_, :, phi * FH:(phi + 1) * FH]
                     .rearrange("(hb p) f -> p hb f", p=128))]

        def w2_dmas(ar, e_, phi, cached):
            t, off = AREAS[ar]
            if cached:
                return [(lambda _t=t, _o=off: _t[:, _o:_o + 16384],
                         w2d[2 * e_ + phi])]
            res = []
            for half in range(2):
                res.append((lambda _t=t, _o=off, _h=half: _t[:, _o + _h * 8192:
                                                             _o + (_h + 1) * 8192]
                            .rearrange("p (fb d) -> p fb d", fb=8),
                            w2T[e_, phi * FH + half * 1024: phi * FH + (half + 1) * 1024, :]
                            .rearrange("(fb p) d -> p fb d", p=128)))
            return res

        def w1_wb(ar, e_, phi):
            t, off = AREAS[ar]
            return [(w1d[2 * e_ + phi], lambda _t=t, _o=off: _t[:, _o:_o + 16384])]

        def w2_wb(ar, e_, phi):
            t, off = AREAS[ar]
            return [(w2d[2 * e_ + phi], lambda _t=t, _o=off: _t[:, _o:_o + 16384])]

        def emit_attw(wb, cached, wbk):
            flush_wb(0)
            ar1, ar2 = (wb + 1) % 3, (wb + 2) % 3
            wt("gpsimd", "s_te", area_prev[ar1])
            wt("gpsimd", wd_sem[ar1], area_wfree[ar1])
            for wi, wsrc in ((0, wqT), (1, wkT)):
                t, off = AREAS[ar1]
                if cached:
                    op("gpsimd", lambda e, S, _t=t, _o=off + wi * 8192, _w=wi:
                       e.dma_start(out=_t[:, _o:_o + 8192], in_=attd[_w]), "s_wt", 16)
                else:
                    op("gpsimd", lambda e, S, _t=t, _o=off + wi * 8192, _s=wsrc:
                       e.dma_start(
                        out=_t[:, _o:_o + 8192].rearrange("p (hb d) -> p hb d", hb=8),
                        in_=_s.rearrange("(hb p) d -> p hb d", p=128)), "s_wt", 16)
            if wbk:
                t, off = AREAS[ar1]
                pend_wb.append((ar1, "s_wt", cnt["s_wt"],
                                [(attd[0], lambda _t=t, _o=off: _t[:, _o:_o + 8192]),
                                 (attd[1], lambda _t=t, _o=off: _t[:, _o + 8192:_o + 16384])]))
            wt("gpsimd", "s_te", area_prev[ar2])
            wt("gpsimd", wd_sem[ar2], area_wfree[ar2])
            for wi, wsrc, sem in ((0, wvT, "s_wv"), (1, woT, "s_wt2")):
                t, off = AREAS[ar2]
                if cached:
                    op("gpsimd", lambda e, S, _t=t, _o=off + wi * 8192, _w=2 + wi:
                       e.dma_start(out=_t[:, _o:_o + 8192], in_=attd[_w]), sem, 16)
                else:
                    op("gpsimd", lambda e, S, _t=t, _o=off + wi * 8192, _s=wsrc:
                       e.dma_start(
                        out=_t[:, _o:_o + 8192].rearrange("p (hb d) -> p hb d", hb=8),
                        in_=_s.rearrange("(hb p) d -> p hb d", p=128)), sem, 16)
                if wbk:
                    pend_wb.append((ar2, sem, cnt[sem],
                                    [(attd[2 + wi], lambda _t=t, _o=off + wi * 8192:
                                      _t[:, _o:_o + 8192])]))
            return (cnt["s_wt"], cnt["s_wv"], cnt["s_wt2"])

        ve(lambda e, S: e.memset(cMall[:], 0.0))
        for _b in range(6):
            ve(lambda e, S, _bb=_b: e.memset(pk[_bb][:], 0.0))
        psum_init_ve = cnt["s_ve"]
        wt("vector", "s_xb", n_conn_sx)

        def r3(t, wdim):
            return t().rearrange("p (w j) -> p w j", w=wdim)

        ve(lambda e, S: e.scalar_tensor_tensor(
            c_h1(), c_cw1(), c_pos()[:, 0:1], c_cb1(), ALU.mult, ALU.add))
        wt("scalar", "s_ve", cnt["s_ve"])
        se(lambda e, S: e.activation(c_h1(), c_h1(), AF.Gelu))
        wt("vector", "s_se", cnt["s_se"])
        for i in range(32):
            blk = (lambda _i=i: c_cw2()[:, _i * 32:(_i + 1) * 32])
            col = (lambda _i=i: c_h1()[:, _i:_i + 1])
            if i == 0:
                ve(lambda e, S, _b=blk, _c=col: e.tensor_scalar(
                    c_h2(), _b(), _c(), None, ALU.mult))
            else:
                ve(lambda e, S, _b=blk, _c=col: e.scalar_tensor_tensor(
                    c_h2(), _b(), _c(), c_h2(), ALU.mult, ALU.add))
        ve(lambda e, S: e.tensor_tensor(c_h2(), c_h2(), c_cb2(), ALU.add))
        wt("scalar", "s_ve", cnt["s_ve"])
        se(lambda e, S: e.activation(c_h2(), c_h2(), AF.Gelu))
        wt("vector", "s_se", cnt["s_se"])
        ve(lambda e, S: e.tensor_tensor(c_h2(), c_h2(), c_cw3(), ALU.mult))
        ve(lambda e, S: e.bn_stats(c_bn()[:, 0:6], c_h2()))
        ve(lambda e, S: e.bn_aggr(c_bn()[:, 6:8], c_bn()[:, 0:6]))
        cwt_ve = ve(lambda e, S: e.scalar_tensor_tensor(
            c_cwt()[:, 0:1], c_bn()[:, 6:7], 32.0, c_cb3()[:, 0:1],
            ALU.mult, ALU.add))

        wt("sync", "s_ve", cwt_ve)
        op("sync", lambda e, S: e.dma_start(
            out=D2V().rearrange("p (w j) -> p w j", w=WIN),
            in_=D2.rearrange("w r j -> r w j")), "s_xd", 16)
        wt("tensor", "s_ve", psum_init_ve)
        wt("tensor", "s_ve", cwt_ve)
        wt("tensor", "s_xb", n_conn_sx)
        te(lambda e, S: e.matmul(pk[4][0:1, 0:112], c_cwt()[:, 0:1], idf[:],
                                 start=True, stop=True))
        wt("scalar", "s_te", cnt["s_te"])
        exp_se = se(lambda e, S: e.activation(c_row_t[:], pk[4][0:1, 0:112], AF.Exp))
        bank_read(4, "s_se")
        wt("tensor", "s_se", exp_se)
        bank_guard(5)
        te(lambda e, S: e.matmul(pk[5][:, 0:H * WIN], ones_rf[:], c_row_t[:],
                                 start=True, stop=True))
        wt("vector", "s_te", cnt["s_te"])
        ve(lambda e, S: e.tensor_copy(E128[:], pk[5][:, 0:H * WIN]))
        bank_read(5, "s_ve")
        wt("vector", "s_xd", cnt["s_xd"])
        cmtmp = scr[:, 10240:14336].bitcast(F32)
        e3 = lambda w, n: (E128[0:n].rearrange("p (h w) -> p h w", h=H)
                           [:, :, w:w + 1].broadcast_to([n, H, 128]))
        cm3 = lambda off, n: (cMall[0:n].rearrange("p (h c) -> p h c", h=H)
                              [:, :, off:off + 128])
        tm3 = lambda n: cmtmp[0:n, 0:2048].rearrange("p (h c) -> p h c", h=H)
        for w in range(WIN):
            d2b = (lambda _w=w: D2V()[:, _w * 128:(_w + 1) * 128]
                   .unsqueeze(1).broadcast_to([6, H, 128]))
            if w == 0:
                ve(lambda e, S, _d=d2b, _w=w: e.tensor_tensor(
                    cm3(128, 6), _d(), e3(_w, 6), ALU.mult))
            else:
                ve(lambda e, S, _d=d2b, _w=w: e.tensor_tensor(
                    tm3(6), _d(), e3(_w, 6), ALU.mult))
                ve(lambda e, S: e.tensor_tensor(
                    cm3(128, 6), cm3(128, 6), tm3(6), ALU.add))
        cM2_ve = cnt["s_ve"]
        for w in range(WIN):
            d1b = (lambda _w=w: D1V()[:, _w * 128:(_w + 1) * 128]
                   .unsqueeze(1).broadcast_to([128, H, 128]))
            if w == 0:
                ve(lambda e, S, _d=d1b, _w=w: e.tensor_tensor(
                    cm3(0, 128), _d(), e3(_w, 128), ALU.mult))
            else:
                ve(lambda e, S, _d=d1b, _w=w: e.tensor_tensor(
                    tm3(128), _d(), e3(_w, 128), ALU.mult))
                ve(lambda e, S: e.tensor_tensor(
                    cm3(0, 128), cm3(0, 128), tm3(128), ALU.add))
                connM_ve = cnt["s_ve"]

        wt("tensor", "s_xc", n_const_sx)
        wt("scalar", "s_xc", n_const_sx)
        wt("vector", "s_xc", n_const_sx)

        xs_free = {}
        qkv_last_te = 0
        v3_free_te = {}
        ew_free_te = [0, 0, 0, 0]
        ah_t_te_prev = [0]
        xnb_pass_free = [0]
        aht_free_te = 0
        pass_ve_end = 0
        pre_x = {}

        att_w_next = [None]
        xdma_of = {}
        ln_affine = {}
        ln_evac = {}
        ln_tp_te = {}
        for p in range(NPASS):

            def emit_x(q, _p=p):
                g = _p * 9 + q
                buf = g % 3
                xsem = f"s_xs{buf}"
                if (_p, q) in pre_x:
                    return pre_x[(_p, q)]
                if g >= 3:
                    vefree, sefree = xs_free[g - 3]
                    wt("sync", "s_ve", vefree)
                    wt("sync", "s_se", sefree)
                if q == 0:
                    op("sync", lambda e, S, _pp=_p, _b=buf: e.dma_start(
                        out=XS[_b]()[0:3, :],
                        in_=x[_pp * PT: _pp * PT + 3, :]), xsem, 16)
                    op("sync", lambda e, S, _pp=_p, _b=buf: e.dma_start(
                        out=XS[_b]()[3:6, :],
                        in_=x[_pp * PT + 1027: _pp * PT + 1030, :]), xsem, 16)
                else:
                    c = q - 1
                    op("sync", lambda e, S, _pp=_p, _c=c, _b=buf: e.dma_start(
                        out=XS[_b](),
                        in_=x[_pp * PT + 3 + _c * 128: _pp * PT + 3 + (_c + 1) * 128, :]),
                        xsem, 16)
                return cnt[xsem]

            def emit_acc_init(_p=p):
                wt("sync", "s_out", 16 * _p)
                op("sync", lambda e, S, _pp=_p: e.dma_start(
                    out=acc[:].rearrange("q (tb d) -> q tb d", tb=8),
                    in_=x[3 + _pp * PT: 3 + (_pp + 1) * PT, :]
                        .rearrange("(tb q) d -> q tb d", q=128)), "s_xa", 16)
                return cnt["s_xa"]

            for c in range(3):
                if (p, c) not in xdma_of:
                    xdma_of[(p, c)] = emit_x(c)

            def emit_ln_front(q, _p=p):
                g = _p * 9 + q
                buf = g % 3
                np_ = 6 if q == 0 else 128
                xsem = f"s_xs{buf}"
                wt("vector", xsem, xdma_of[(_p, q)])
                B = buf
                ve(lambda e, S, _b=buf, _B=B, _n=np_: e.bn_stats(
                    bnst[0:_n, _B * 12:_B * 12 + 6], XS[_b]()[0:_n, 0:512]))
                ve(lambda e, S, _b=buf, _B=B, _n=np_: e.bn_stats(
                    bnst[0:_n, _B * 12 + 6:_B * 12 + 12], XS[_b]()[0:_n, 512:1024]))
                stats_ve = ve(lambda e, S, _B=B, _n=np_: e.bn_aggr(
                    bnagg[0:_n, _B * 2:_B * 2 + 2], bnst[0:_n, _B * 12:_B * 12 + 12]))
                veps = ve(lambda e, S, _B=B, _n=np_: e.tensor_scalar_add(
                    lnt[0:_n, _B:_B + 1], bnagg[0:_n, _B * 2 + 1:_B * 2 + 2], 1e-5))
                wt("scalar", "s_ve", veps)
                se(lambda e, S, _B=B, _n=np_: e.activation(
                    lnt[0:_n, _B:_B + 1], lnt[0:_n, _B:_B + 1], AF.Ln))
                rstd_se = se(lambda e, S, _B=B, _n=np_: e.activation(
                    rstd[0:_n, _B:_B + 1], lnt[0:_n, _B:_B + 1], AF.Exp, scale=-0.5))
                wt("vector", "s_se", rstd_se)
                nmr_ve = ve(lambda e, S, _B=B, _n=np_: e.scalar_tensor_tensor(
                    nmr[0:_n, _B:_B + 1], bnagg[0:_n, _B * 2:_B * 2 + 1], -1.0,
                    rstd[0:_n, _B:_B + 1], ALU.mult, ALU.mult))
                wt("scalar", "s_ve", nmr_ve)
                wt("scalar", "s_te", xnb_pass_free[0])
                xnbv = (lambda: xnbH[:]) if q == 0 else XNB[q - 1]
                aff_se = se(lambda e, S, _b=buf, _B=B, _n=np_, _xv=xnbv: e.activation(
                    _xv(), XS[_b]()[0:_n, :], AF.Identity,
                    bias=nmr[0:_n, _B:_B + 1], scale=rstd[0:_n, _B:_B + 1]))
                ln_affine[(_p, q)] = aff_se
                xs_free[g] = (stats_ve, aff_se)

            def emit_ln_back(q, _p=p):
                g = _p * 9 + q
                if q + 3 <= 8:
                    xdma_of[(_p, q + 3)] = emit_x(q + 3)
                pb = q % 2
                np_ = 6 if q == 0 else 128
                xnbv = (lambda: xnbH[:]) if q == 0 else XNB[q - 1]
                wt("tensor", "s_se", ln_affine[(_p, q)])
                wt("tensor", "s_se", bank_free[6 + pb]["s_se"])
                wt("tensor", "s_ve", bank_free[6 + pb]["s_ve"])
                for db in range(8):
                    te(lambda e, S, _db=db, _pb=pb, _n=np_, _xv=xnbv: e.transpose(
                        pkb[_pb][:, _db * 8 if _n == 6 else _db * 128:
                                 (_db * 8 + 6) if _n == 6 else (_db + 1) * 128],
                        _xv()[0:_n, _db * 128:(_db + 1) * 128], idb[0:_n, 0:_n]),
                        inc=(db == 7))
                ln_tp_te[(_p, q)] = cnt["s_te"]
                wt("vector", "s_te", cnt["s_te"])
                if q == 0:
                    ev_ve = ve(lambda e, S: e.tensor_copy(
                        xnT.rearrange("r (db t) -> r db t", db=8)[:, :, 0:3],
                        pkb[0][:, 0:64].rearrange("r (db t) -> r db t", t=8)[:, :, 0:3]))
                    ev_ve = ve(lambda e, S: e.tensor_copy(
                        xnT.rearrange("r (db t) -> r db t", db=8)[:, :, 1027:1030],
                        pkb[0][:, 0:64].rearrange("r (db t) -> r db t", t=8)[:, :, 3:6]))
                else:
                    c = q - 1
                    ev_ve = ve(lambda e, S, _c=c, _pb=pb: e.tensor_copy(
                        xnT.rearrange("r (db t) -> r db t", db=8)
                        [:, :, 3 + _c * 128:3 + _c * 128 + 128],
                        pkb[_pb][:].rearrange("r (db t) -> r db t", db=8)))
                ln_evac[(_p, q)] = ("s_ve", ev_ve)
                bank_free[6 + pb]["s_ve"] = ev_ve

            wb = p % 3
            area_of = lambda i, _wb=wb: (_wb + i) % 3
            if p >= 1:
                for a3 in range(3):
                    wt("gpsimd", wd_sem[a3], area_wfree[a3])
            if att_w_next[0] is not None:
                att_w_sw = att_w_next[0]
                att_w_next[0] = None
            else:
                att_w_sw = emit_attw(wb, cached=(p > 1), wbk=(p == 1))
            sl_att = wb
            w1_sw = {}
            w2_sw = {}
            w1_sw[0] = emit_piece(area_of(0), w1_dmas(area_of(0), 0, 0, p > 0),
                                  w1_wb(area_of(0), 0, 0) if p == 0 else None)

            if p == 0:
                for q0_ in (0, 1, 2):
                    emit_ln_front(q0_)
                emit_ln_back(0)
                emit_ln_back(1)
                emit_ln_front(3)
                emit_ln_back(2)
            else:
                for q0_ in (6, 7, 8):
                    emit_ln_front(q0_)
                for qb_ in (0, 1, 2, 3):
                    emit_ln_back(qb_)
            acc_init_sx = emit_acc_init()

            wt("tensor", "s_wt", att_w_sw[0])
            for ach in range(NACH):
                if ach + 4 <= 8 and (p, ach + 4) not in ln_affine:
                    emit_ln_front(ach + 4)
                nb_ = ach + (3 if p == 0 else 4)
                if nb_ <= 8 and (p, nb_) not in ln_evac:
                    emit_ln_back(nb_)
                qq = min(ach + 2, 8)
                wt("tensor", ln_evac[(p, qq)][0], ln_evac[(p, qq)][1])
                if p == 0 and ach == 0:
                    wt("scalar", "s_ve", connM_ve)
                for dqb in range(8):
                    bk = dqb % 4
                    bank_guard(bk)
                    for hb in range(8):
                        te(lambda e, S, _q=dqb, _h=hb, _a=ach, _sl=sl_att, _bk=bk:
                           e.matmul(pk[_bk][:, 0:ACH],
                                    attwv(_sl, 0, _h, _q * 128, 128),
                                    xnT[:, _h * PTH + 3 + _a * ACH:
                                        _h * PTH + 3 + (_a + 1) * ACH],
                                    start=(_h == 0), stop=(_h == 7)), inc=(hb == 7))
                    wt("scalar", "s_te", cnt["s_te"])
                    ev = se(lambda e, S, _q=dqb, _bk=bk: e.copy(
                        QTV()[:, _q * ACH:(_q + 1) * ACH], pk[_bk][:, 0:ACH]))
                    bank_free[bk]["s_se"] = ev
                for dqb in range(8):
                    bk = dqb % 4
                    bank_guard(bk)
                    for hb in range(8):
                        te(lambda e, S, _q=dqb, _h=hb, _a=ach, _sl=sl_att, _bk=bk:
                           e.matmul(pk[_bk][:, 0:134],
                                    attwv(_sl, 1, _h, _q * 128, 128),
                                    xnT[:, _h * PTH + _a * ACH:
                                        _h * PTH + _a * ACH + 134],
                                    start=(_h == 0), stop=(_h == 7)), inc=(hb == 7))
                    wt("scalar", "s_te", cnt["s_te"])
                    ev = se(lambda e, S, _q=dqb, _bk=bk: e.copy(
                        KTV()[:, _q * 134:(_q + 1) * 134], pk[_bk][:, 0:134]))
                    bank_free[bk]["s_se"] = ev
                newtiles = (0, 1) if ach == 0 else (ach + 1,)
                wt("tensor", "s_wv", att_w_sw[1])
                for gv in newtiles:
                    vb = gv % 3
                    nr = 6 if gv == 8 else 128
                    for dch in range(2):
                        bk = dch
                        bank_guard(bk)
                        for hb in range(8):
                            te(lambda e, S, _g=gv, _h=hb, _d=dch, _sl=sl_att,
                               _bk=bk, _nr=nr:
                               e.matmul(pk[_bk][0:_nr, 0:512],
                                        xnT[:, _h * PTH + _g * 128:
                                            _h * PTH + _g * 128 + _nr],
                                        attwv(_sl, 2, _h, _d * 512, 512),
                                        start=(_h == 0), stop=(_h == 7)),
                               inc=(hb == 7))
                        wt("scalar", "s_te", cnt["s_te"])
                        wt("scalar", "s_te", v3_free_te.get(vb, 0))
                        ev = se(lambda e, S, _vb=vb, _d=dch, _bk=bk, _nr=nr: e.copy(
                            V3[_vb]()[0:_nr, _d * 512:(_d + 1) * 512],
                            pk[_bk][0:_nr, 0:512]))
                        bank_free[bk]["s_se"] = ev

                vb0 = ach % 3
                vb1 = (ach + 1) % 3

                def emit_st(h):
                    sbk = h % 4
                    blk = h // 2
                    po = 64 * (h % 2)
                    bank_guard(sbk)
                    te(lambda e, S, _b=blk, _po=po, _bk=sbk: e.matmul(
                        pk[_bk][:, 0:128],
                        KTV()[_po:_po + 64, _b * 134: _b * 134 + 128],
                        QTV()[_po:_po + 64, _b * 128: _b * 128 + 128],
                        start=True, stop=True), inc=False)
                    return te(lambda e, S, _b=blk, _po=po, _bk=sbk: e.matmul(
                        pk[_bk][0:6, 128:256],
                        KTV()[_po:_po + 64, _b * 134 + 128: _b * 134 + 134],
                        QTV()[_po:_po + 64, _b * 128: _b * 128 + 128],
                        start=True, stop=True))

                st_done = {}
                od_of = {}
                ah_t_te = {}

                def emit_ah_t(db, _ach=ach):
                    _pb = _ach % 2
                    wt("tensor", "s_ve", cnt["s_ve"])
                    if db == 0:
                        wt("tensor", "s_se", bank_free[6 + _pb]["s_se"])
                        wt("tensor", "s_ve", bank_free[6 + _pb]["s_ve"])
                    te(lambda e, S, _db=db, _p2=_pb: e.transpose(
                        pkb[_p2][:, _db * 128:(_db + 1) * 128],
                        AHB()[:, _db * 128:(_db + 1) * 128], idb[:]))
                    ah_t_te[db] = cnt["s_te"]

                def emit_norm(h):
                    obk_ = 4 + h % 2
                    wt("vector", "s_te", od_of[h])
                    if h == 0:
                        wt("vector", "s_te", ah_t_te_prev[0])
                    ve(lambda e, S, _bk=obk_: e.reciprocal(
                        rden[:], pk[_bk][:, 64:65]))
                    ve(lambda e, S, _h=h, _bk=obk_: e.tensor_scalar(
                        AHB()[:, _h * 64:(_h + 1) * 64],
                        pk[_bk][:, 0:64], rden[:, 0:1], None, ALU.mult))
                    bank_read(obk_, "s_ve")

                for h in range(4):
                    st_done[h] = emit_st(h)
                qkv_last_te = st_done[3]
                for h in range(H):
                    sbk = h % 4
                    obk = 4 + h % 2
                    ewb = h % 4
                    wt("scalar", "s_te", st_done[h])
                    wt("scalar", "s_te", ew_free_te[ewb])
                    exp_se = se(lambda e, S, _e=ewb, _bk=sbk: e.activation(
                        EWF[_e](), pk[_bk][:, 0:256], AF.Exp, scale=0.125))
                    bank_free[sbk]["s_se"] = exp_se
                    wt("vector", "s_se", exp_se)
                    ewc_ve = ve(lambda e, S, _e=ewb, _h=h: e.tensor_tensor(
                        EWF[_e](), EWF[_e](), cMall[:, _h * 256:(_h + 1) * 256],
                        ALU.mult))
                    wt("tensor", "s_ve", ewc_ve)
                    bank_guard(obk)
                    te(lambda e, S, _e=ewb, _v=vb0, _h=h, _bk=obk: e.matmul(
                        pk[_bk][:, 0:64], EW1[_e](),
                        V3[_v]()[:, _h * 64:(_h + 1) * 64],
                        start=True, stop=False), inc=False)
                    te(lambda e, S, _e=ewb, _v=vb1, _h=h, _bk=obk: e.matmul(
                        pk[_bk][:, 0:64], EW2[_e](),
                        V3[_v]()[0:6, _h * 64:(_h + 1) * 64],
                        start=False, stop=True), inc=False)
                    te(lambda e, S, _e=ewb, _bk=obk: e.matmul(
                        pk[_bk][:, 64:65], EW1[_e](), ones_c[:],
                        start=True, stop=False), inc=False)
                    od_te = te(lambda e, S, _e=ewb, _bk=obk: e.matmul(
                        pk[_bk][:, 64:65], EW2[_e](), ones_c[0:6, :],
                        start=False, stop=True))
                    ew_free_te[ewb] = od_te
                    v3_free_te[vb0] = od_te
                    v3_free_te[vb1] = od_te
                    od_of[h] = od_te
                    if h + 4 < H:
                        st_done[h + 4] = emit_st(h + 4)
                        qkv_last_te = st_done[h + 4]
                    if h > 0:
                        emit_norm(h - 1)
                    if h >= 3 and h % 2 == 1:
                        emit_ah_t((h - 3) // 2)
                if True:
                    emit_norm(H - 1)
                    emit_ah_t(6)
                    emit_ah_t(7)
                ah_t_te_prev[0] = ah_t_te[7]
                tpb = ach % 2
                wt("scalar", "s_te", ah_t_te[3])
                wt("scalar", "s_te", aht_free_te)
                aht_se1 = se(lambda e, S, _pb=tpb: e.copy(
                    AHT()[:, 0:512], pkb[_pb][:, 0:512]))
                wt("scalar", "s_te", ah_t_te[7])
                aht_se2 = se(lambda e, S, _pb=tpb: e.copy(
                    AHT()[:, 512:1024], pkb[_pb][:, 512:1024]))
                wt("tensor", "s_wt2", att_w_sw[2])
                bank_free[6 + tpb]["s_se"] = aht_se2
                wt("tensor", "s_se", aht_se1)
                bank_guard(4)
                bank_guard(5)
                wt("tensor", "s_wc", n_const_sw)
                for dqb in range(8):
                    if dqb == 4:
                        wt("tensor", "s_se", aht_se2)
                    for dch in range(2):
                        te(lambda e, S, _q=dqb, _d=dch, _sl=sl_att: e.matmul(
                            pk[4 + _d][:, 0:512],
                            AHT()[:, _q * 128:(_q + 1) * 128],
                            attwv(_sl, 3, _q, _d * 512, 512),
                            start=(_q == 0), stop=False), inc=False)
                for dch in range(2):
                    te(lambda e, S, _d=dch: e.matmul(
                        pk[4 + _d][:, 0:512], sel9[:, 8 * 128:9 * 128],
                        b2bo[:, _d * 512:(_d + 1) * 512],
                        start=False, stop=True), inc=(dch == 1))
                aht_free_te = cnt["s_te"]
                wt("vector", "s_te", aht_free_te)
                wt("vector", "s_xa", acc_init_sx)
                for dch in range(2):
                    ve(lambda e, S, _tb=ach, _d=dch: e.tensor_add(
                        acc[:, _tb * D + _d * 512: _tb * D + (_d + 1) * 512],
                        acc[:, _tb * D + _d * 512: _tb * D + (_d + 1) * 512],
                        pk[4 + _d][:, 0:512]))
                bank_read(4, "s_ve")
                bank_read(5, "s_ve")
            att_te_end = cnt["s_te"]
            area_prev[(wb + 1) % 3] = att_te_end
            area_prev[(wb + 2) % 3] = att_te_end
            wt("tensor", ln_evac[(p, 8)][0], ln_evac[(p, 8)][1])
            bank_guard(5)
            for tb in range(8):
                for hb in range(8):
                    te(lambda e, S, _tb=tb, _hb=hb: e.matmul(
                        pk[5][:, _tb * E:(_tb + 1) * E],
                        xnT[:, _hb * PTH + 3 + _tb * 128: _hb * PTH + 3 + (_tb + 1) * 128],
                        rwb[:, _hb * E:(_hb + 1) * E],
                        start=(_hb == 0), stop=(_hb == 7)), inc=(hb == 7))
            lg_te = cnt["s_te"]
            wt("vector", "s_te", lg_te)
            for tb in range(8):
                ve(lambda e, S, _tb=tb: e.max(
                    out=m8[:, _tb * 8:(_tb + 1) * 8],
                    in_=pk[5][:, _tb * E:(_tb + 1) * E]))
            m3 = lambda c: (m8[:].rearrange("p (t c) -> p t c", t=8)
                            [:, :, c:c + 1].broadcast_to([128, 8, 8]))
            ve(lambda e, S: e.tensor_sub(
                g2c[:].rearrange("p t -> p t ()"),
                m8[:].rearrange("p (t c) -> p t c", t=8)[:, :, 1:2],
                m8[:].rearrange("p (t c) -> p t c", t=8)[:, :, 0:1]))
            sub_ve = cnt["s_ve"]
            wt("scalar", "s_ve", sub_ve)
            sig_se = se(lambda e, S: e.activation(g2c[:], g2c[:], AF.Sigmoid))
            wt("vector", "s_se", sig_se)
            ve(lambda e, S: e.tensor_scalar(
                g1c[:], g2c[:], -1.0, 1.0, ALU.mult, ALU.add))
            p5t = lambda: pk[5][:, 0:64].rearrange("p (t c) -> p t c", t=8)
            gv = lambda g: g[:].unsqueeze(2).broadcast_to([128, 8, 8])
            gt3 = lambda t: t[:].rearrange("p (t c) -> p t c", t=8)
            ve(lambda e, S: e.tensor_tensor(gt3(gtmp), p5t(), m3(0), ALU.is_equal))
            ve(lambda e, S: e.tensor_tensor(gt3(gtmp), gt3(gtmp), gv(g1c), ALU.mult))
            ve(lambda e, S: e.tensor_tensor(gt3(gtm2), p5t(), m3(1), ALU.is_equal))
            ve(lambda e, S: e.tensor_tensor(gt3(gtm2), gt3(gtm2), gv(g2c), ALU.mult))
            ve(lambda e, S: e.tensor_add(Gs[:], gtmp[:], gtm2[:]))
            ve(lambda e, S: e.tensor_scalar(
                maskf[:], Gs[:], 0.0, None, ALU.is_gt))
            mb_ve = ve(lambda e, S: e.tensor_copy(maskb[:], maskf[:]))
            wt("tensor", "s_ve", mb_ve)
            bank_guard(4)
            for kk in range(8):
                te(lambda e, S, _k=kk: e.matmul(
                    pk[4][:, _k * 8:(_k + 1) * 8], ltri[:],
                    maskb[:, _k * 8:(_k + 1) * 8],
                    start=True, stop=(_k == 0)), inc=False)
                for jj in range(kk):
                    te(lambda e, S, _k=kk, _j=jj: e.matmul(
                        pk[4][:, _k * 8:(_k + 1) * 8], ones128[:],
                        maskb[:, _j * 8:(_j + 1) * 8],
                        start=False, stop=(_j == _k - 1)),
                       inc=(kk == 7 and jj == 6))
            pos_te = cnt["s_te"]
            wt("vector", "s_te", pos_te)
            poss_ve = ve(lambda e, S: e.tensor_copy(poss[:], pk[4][:, 0:64]))
            bank_read(4, "s_ve")

            pbuild_ve = {}

            def emit_pbuild(e_, after_te):
                wt("vector", "s_te", after_te)
                for t in range(8):
                    ve(lambda e, S, _t=t, _e=e_: e.tensor_scalar(
                        PV(_t), iotaM[:],
                        poss[:, _t * 8 + _e:_t * 8 + _e + 1],
                        maskf[:, _t * 8 + _e:_t * 8 + _e + 1],
                        ALU.is_equal, ALU.mult))
                pbuild_ve[e_] = cnt["s_ve"]

            emit_pbuild(0, 0)

            if p + 1 < NPASS:
                for q2 in (0, 1, 2):
                    g2 = (p + 1) * 9 + q2
                    buf2 = g2 % 3
                    vefree, sefree = xs_free[g2 - 3]
                    wt("sync", "s_ve", vefree)
                    wt("sync", "s_se", sefree)
                    if q2 == 0:
                        op("sync", lambda e, S, _p=p + 1, _b=buf2: e.dma_start(
                            out=XS[_b]()[0:3, :],
                            in_=x[_p * PT: _p * PT + 3, :]), f"s_xs{buf2}", 16)
                        op("sync", lambda e, S, _p=p + 1, _b=buf2: e.dma_start(
                            out=XS[_b]()[3:6, :],
                            in_=x[_p * PT + 1027: _p * PT + 1030, :]),
                            f"s_xs{buf2}", 16)
                    else:
                        c2 = q2 - 1
                        op("sync", lambda e, S, _p=p + 1, _c=c2, _b=buf2: e.dma_start(
                            out=XS[_b](),
                            in_=x[_p * PT + 3 + _c * 128:
                                  _p * PT + 3 + (_c + 1) * 128, :]),
                            f"s_xs{buf2}", 16)
                    pre_x[(p + 1, q2)] = cnt[f"s_xs{buf2}"]
                    xdma_of[(p + 1, q2)] = pre_x[(p + 1, q2)]

            w2_sw[0] = emit_piece(area_of(1), w2_dmas(area_of(1), 0, 0, p > 1),
                                  w2_wb(area_of(1), 0, 0) if p == 1 else None)
            w1_sw[1] = emit_piece(area_of(2), w1_dmas(area_of(2), 0, 1, p > 0),
                                  w1_wb(area_of(2), 0, 1) if p == 0 else None)

            comp_te = {}
            for e_ in range(E):
                wt("tensor", "s_ve", pbuild_ve[e_])
                ptv_se = 0
                for sc in range(3):
                    pb = sc % 2
                    ns = min(CAP, (sc + 1) * 128) - sc * 128
                    wt("tensor", "s_se", bank_free[6 + pb]["s_se"])
                    wt("tensor", "s_ve", bank_free[6 + pb]["s_ve"])
                    for t in range(8):
                        te(lambda e, S, _t=t, _sc=sc, _pb=pb, _n=ns: e.transpose(
                            pkb[_pb][0:_n, _t * 128:(_t + 1) * 128],
                            PV(_t)[:, _sc * 128:_sc * 128 + _n], idb[:]),
                            inc=(t == 7))
                    tp_te = cnt["s_te"]
                    wt("scalar", "s_te", tp_te)
                    ptv_se = se(lambda e, S, _sc=sc, _pb=pb, _n=ns: e.activation(
                        PTV(_sc)[0:_n, :], pkb[_pb][0:_n, :], AF.Copy))
                    bank_free[6 + pb]["s_se"] = ptv_se
                for dc in range(8):
                    bk = dc % 2
                    bank_guard(bk)
                    for t in range(8):
                        te(lambda e, S, _t=t, _dc=dc, _bk=bk: e.matmul(
                            pk[_bk][:, 0:CAP],
                            XNB[_t]()[:, _dc * 128:(_dc + 1) * 128],
                            PV(_t), start=(_t == 0), stop=(_t == 7)),
                            inc=(t == 7))
                    cte = cnt["s_te"]
                    wt("scalar", "s_te", cte)
                    se(lambda e, S, _dc=dc, _bk=bk: e.activation(
                        XGV(_dc), pk[_bk][:, 0:CAP], AF.Copy))
                    bank_read(bk, "s_se")
                comp_te[e_] = cnt["s_te"]
                xg_se = cnt["s_se"]
                if e_ + 1 < E:
                    emit_pbuild(e_ + 1, comp_te[e_])
                else:
                    xnb_pass_free[0] = comp_te[e_]
                    if p + 1 < NPASS:
                        def pre_x_late(q3, _p=p + 1):
                            g3 = _p * 9 + q3
                            b3 = g3 % 3
                            vf3, sf3 = xs_free[g3 - 3]
                            wt("sync", "s_ve", vf3)
                            wt("sync", "s_se", sf3)
                            c3 = q3 - 1
                            op("sync", lambda e, S, _pp=_p, _c=c3, _b=b3:
                               e.dma_start(
                                out=XS[_b](),
                                in_=x[_pp * PT + 3 + _c * 128:
                                      _pp * PT + 3 + (_c + 1) * 128, :]),
                                f"s_xs{b3}", 16)
                            pre_x[(_p, q3)] = cnt[f"s_xs{b3}"]
                            xdma_of[(_p, q3)] = pre_x[(_p, q3)]
                        for qf in (0, 1, 2):
                            emit_ln_front(qf, _p=p + 1)
                        for q3 in (3, 4, 5):
                            pre_x_late(q3)
                        for qf in (3, 4, 5):
                            emit_ln_front(qf, _p=p + 1)
                        for q3 in (6, 7, 8):
                            pre_x_late(q3)
                for phi in range(2):
                    k = e_ * 2 + phi
                    ar1 = area_of(2 * k)
                    ar2 = area_of(2 * k + 1)
                    wt("tensor", w1_sw[k][0], w1_sw[k][1])
                    if phi == 0:
                        wt("tensor", "s_se", xg_se)
                    gelu_of = {}
                    for fb in range(NFB):
                        abk = fb % 2
                        bank_guard(abk)
                        for hb in range(8):
                            te(lambda e, S, _sl=ar1, _h=hb, _f=fb, _bk=abk:
                               e.matmul(pk[_bk][:, 0:CAP],
                                        w1v(_sl, _h, _f), XGV(_h),
                                        start=(_h == 0), stop=(_h == 7)),
                               inc=(hb == 7))
                        mm1_te = cnt["s_te"]
                        wt("scalar", "s_te", mm1_te)
                        gse = se(lambda e, S, _f=fb, _bk=abk, _e=e_, _phi=phi:
                                 e.activation(
                            H1V(_f), pk[_bk][:, 0:CAP], AF.Gelu,
                            bias=b1s[:, _e * 32 + _phi * 16 + _f:
                                     _e * 32 + _phi * 16 + _f + 1]))
                        gelu_of[fb] = gse
                        bank_free[abk]["s_se"] = gse
                    mm1_end = cnt["s_te"]
                    wt("tensor", w2_sw[k][0], w2_sw[k][1])
                    for sb in (2, 0, 1):
                        bks = (2 + 2 * sb, 3 + 2 * sb)
                        if phi == 0:
                            bank_guard(bks[0])
                            bank_guard(bks[1])
                        wt("tensor", "s_se", gelu_of[13])
                        for fb in range(NFB):
                            if fb >= 14:
                                wt("tensor", "s_se", gelu_of[fb])
                            for dch in range(2):
                                if sb == 2:
                                    outap = (lambda _d=dch: pkb[_d][:]
                                             .bitcast(F32)[:, 0:512])
                                else:
                                    outap = (lambda _d=dch, _b=bks:
                                             pk[_b[_d]][:, 0:512])
                                nsb = min(CAP, (sb + 1) * 128) - sb * 128
                                te(lambda e, S, _sl=ar2, _f=fb, _sb=sb, _d=dch,
                                   _o=outap, _phi=phi, _n=nsb: e.matmul(
                                    _o()[0:_n, :],
                                    H1V(_f)[:, _sb * 128:_sb * 128 + _n],
                                    w2v(_sl, _f, _d),
                                    start=(_phi == 0 and _f == 0), stop=False),
                                   inc=(phi == 0 and fb == NFB - 1 and dch == 1))
                        if phi == 1:
                            for dch in range(2):
                                if sb == 2:
                                    outap = (lambda _d=dch: pkb[_d][:]
                                             .bitcast(F32)[:, 0:512])
                                else:
                                    outap = (lambda _d=dch, _b=bks:
                                             pk[_b[_d]][:, 0:512])
                                nsb = min(CAP, (sb + 1) * 128) - sb * 128
                                te(lambda e, S, _e=e_, _d=dch, _o=outap, _n=nsb:
                                   e.matmul(
                                    _o()[0:_n, :],
                                    sel9[:, _e * 128:_e * 128 + _n],
                                    b2bo[:, _d * 512:(_d + 1) * 512],
                                    start=False, stop=True), inc=(dch == 1))
                            mm2_sb_te = cnt["s_te"]
                            wt("scalar", "s_te", mm2_sb_te)
                            for dch in range(2):
                                if sb == 2:
                                    inap = (lambda _d=dch: pkb[_d][:]
                                            .bitcast(F32)[:, 0:512])
                                else:
                                    inap = (lambda _d=dch, _b=bks:
                                            pk[_b[_d]][:, 0:512])
                                nsb = min(CAP, (sb + 1) * 128) - sb * 128
                                se(lambda e, S, _sb=sb, _d=dch, _i=inap, _n=nsb:
                                   e.activation(
                                    O2V(_sb)[0:_n, _d * 512:(_d + 1) * 512],
                                    _i()[0:_n, :], AF.Copy))
                            bank_read(bks[0], "s_se")
                            bank_read(bks[1], "s_se")
                    area_prev[ar1] = mm1_end
                    area_prev[ar2] = cnt["s_te"]
                    if k + 1 < 16:
                        kk = k + 1
                        w2_sw[kk] = emit_piece(
                            area_of(2 * kk + 1),
                            w2_dmas(area_of(2 * kk + 1), kk // 2, kk % 2, p > 1),
                            w2_wb(area_of(2 * kk + 1), kk // 2, kk % 2)
                            if p == 1 else None)
                    if k + 2 < 16:
                        kk = k + 2
                        w1_sw[kk] = emit_piece(
                            area_of(2 * kk),
                            w1_dmas(area_of(2 * kk), kk // 2, kk % 2, p > 0),
                            w1_wb(area_of(2 * kk), kk // 2, kk % 2)
                            if p == 0 else None)
                    if k == 15 and p + 1 < NPASS:
                        att_w_next[0] = emit_attw(
                            (p + 1) % 3, cached=(p + 1 > 1), wbk=(p + 1 == 1))
                o2_se = cnt["s_se"]
                wt("tensor", "s_se", o2_se)
                for t in range(8):
                    sl = (3, 0, 1, 2, 0, 1, 2, 0)[t]
                    bks = (2 * sl, 2 * sl + 1)
                    bank_guard(bks[0])
                    bank_guard(bks[1])
                    if sl == 3:
                        slotap = (lambda _d: (lambda: pkb[_d][:]
                                              .bitcast(F32)[:, 0:512]))
                    else:
                        slotap = (lambda _d, _b=bks: (lambda: pk[_b[_d]][:, 0:512]))
                    for sc in range(3):
                        nsc = min(CAP, (sc + 1) * 128) - sc * 128
                        for dch in range(2):
                            te(lambda e, S, _t=t, _sc=sc, _d=dch, _o=slotap(dch),
                               _n=nsc: e.matmul(
                                _o(),
                                PTV(_sc)[0:_n, _t * 128:(_t + 1) * 128],
                                O2V(_sc)[0:_n, _d * 512:(_d + 1) * 512],
                                start=(_sc == 0), stop=(_sc == 2)),
                               inc=(sc == 2 and dch == 1))
                    st_te = cnt["s_te"]
                    wt("vector", "s_te", st_te)
                    wt("vector", "s_xa", acc_init_sx)
                    for dch in range(2):
                        ve(lambda e, S, _t=t, _d=dch, _e=e_, _i=slotap(dch):
                           e.scalar_tensor_tensor(
                            acc[:, _t * D + _d * 512: _t * D + (_d + 1) * 512],
                            _i(),
                            Gs[:, _t * E + _e: _t * E + _e + 1],
                            acc[:, _t * D + _d * 512: _t * D + (_d + 1) * 512],
                            ALU.mult, ALU.add))
                    bank_read(bks[0], "s_ve")
                    bank_read(bks[1], "s_ve")
            if p <= 1:
                flush_wb(0)
            pass_ve_end = cnt["s_ve"]

            wt("sync", "s_ve", pass_ve_end)
            op("sync", lambda e, S, _p=p: e.dma_start(
                out=out_t[_p * PT:(_p + 1) * PT, :]
                    .rearrange("(tb q) d -> q tb d", q=128),
                in_=acc[:].rearrange("q (tb d) -> q tb d", tb=8)), "s_out", 16)

        wt("sync", "s_out", cnt["s_out"])

        def run_body(lst):
            def body(e):
                for f in lst:
                    f(e, S)
            return body
        block.sync(run_body(ops["sync"]))
        block.gpsimd(run_body(ops["gpsimd"]))
        block.tensor(run_body(ops["tensor"]))
        block.scalar(run_body(ops["scalar"]))
        block.vector(run_body(ops["vector"]))

    return nc


def host_inputs(inputs):
    f32 = np.float32
    bf16 = ml_dtypes.bfloat16
    xf = np.asarray(inputs["L0_attn_x"], f32)
    shared = {
        "w1T": np.ascontiguousarray(np.asarray(inputs["L0_ffn_w1"], f32).transpose(0, 2, 1)),
        "w2T": np.ascontiguousarray(np.asarray(inputs["L0_ffn_w2"], f32).transpose(0, 2, 1)),
        "wqT": np.ascontiguousarray(np.asarray(inputs["L0_attn_wq"], f32).T),
        "wkT": np.ascontiguousarray(np.asarray(inputs["L0_attn_wk"], f32).T),
        "wvT": np.ascontiguousarray(np.asarray(inputs["L0_attn_wv"], f32).T),
        "woT": np.ascontiguousarray(np.asarray(inputs["L0_attn_wo"], f32).T),
        "rwT": np.ascontiguousarray(np.asarray(inputs["L0_ffn_router_w"], f32).T),
        "b1T": np.ascontiguousarray(
            np.asarray(inputs["L0_ffn_b1"], f32).reshape(E, 32, 128).transpose(0, 2, 1)),
        "b2i": np.ascontiguousarray(np.asarray(inputs["L0_ffn_b2"], f32)),
        "boi": np.ascontiguousarray(np.asarray(inputs["L0_attn_bo"], f32)[None, :]),
        "cw1t": np.ascontiguousarray(
            np.repeat(np.asarray(inputs["L0_attn_cw1"], f32)[:, :, 0], WIN, axis=0)),
        "cb1t": np.ascontiguousarray(
            np.repeat(np.asarray(inputs["L0_attn_cb1"], f32), WIN, axis=0)),
        "cw2f": np.ascontiguousarray(
            np.repeat(np.asarray(inputs["L0_attn_cw2"], f32)
                      .transpose(0, 2, 1).reshape(H, 1024), WIN, axis=0)),
        "cb2t": np.ascontiguousarray(
            np.repeat(np.asarray(inputs["L0_attn_cb2"], f32), WIN, axis=0)),
        "cw3s": np.ascontiguousarray(
            np.repeat(np.asarray(inputs["L0_attn_cw3"], f32)[:, 0, :], WIN, axis=0)),
        "cb3s": np.ascontiguousarray(
            np.repeat(np.asarray(inputs["L0_attn_cb3"], f32), WIN, axis=0)),
        "post": np.ascontiguousarray(
            np.tile(np.linspace(0.0, 1.0, WIN, dtype=f32), H)[:, None]),
        "identb": np.eye(128, dtype=bf16),
        "identf": np.eye(128, dtype=f32),
        "onescol_b": np.ones((128, 1), dtype=bf16),
        "onesrow_f": np.ones((1, 128), dtype=f32),
        "iotaM_i": np.ascontiguousarray(
            np.tile(np.arange(1, CAP + 1, dtype=f32)[None, :], (128, 1))),
        "ltri_i": np.triu(np.ones((128, 128))).astype(bf16),
        "ones128_i": np.ones((128, 128), dtype=bf16),
        "sel9_i": np.ascontiguousarray(np.kron(np.eye(9), np.ones((1, 128))).astype(bf16)),
    }
    r = np.arange(128)[:, None]
    j = np.arange(128)[None, :]
    D1m = np.stack([(r - j == w).astype(f32) for w in range(WIN)])
    r2 = np.arange(6)[:, None]
    D2m = np.stack([(128 + r2 - j == w).astype(f32) for w in range(WIN)])
    shared["D1"] = np.ascontiguousarray(D1m)
    shared["D2"] = np.ascontiguousarray(D2m)

    in_maps = []
    for c in range(8):
        b, s = c // 2, c % 2
        xc = np.zeros((XROWS, D), f32)
        lo = s * 4096 - 3
        hi = s * 4096 + 4096 + 3
        vlo, vhi = max(lo, 0), min(hi, 8192)
        xc[vlo - lo: vhi - lo] = xf[b, vlo:vhi]
        m = dict(shared)
        m["x"] = xc
        in_maps.append(m)
    return in_maps


def kernel(**inputs):
    in_maps = host_inputs(inputs)
    nc = build_program()
    res = run_bass_kernel_spmd(nc, in_maps, core_ids=list(range(8)))
    out = np.empty((4, 8192, D), np.float32)
    for c in range(8):
        b, s = c // 2, c % 2
        out[b, s * 4096:(s + 1) * 4096] = np.asarray(res.results[c]["out"])
    return out
